# version 15 (final)
# baseline (speedup 1.0000x reference)
import math
import numpy as np
import ml_dtypes
import concourse.bass as bass
import concourse.mybir as mybir
from concourse.bass_utils import run_bass_kernel_spmd

F32 = mybir.dt.float32
BF16 = mybir.dt.bfloat16
AF = mybir.ActivationFunctionType
ALU = mybir.AluOpType
AX = mybir.AxisListType

NCORES = 8
DEBUG_TAPS = False
T = 2048
D = 1024
NT = T // 128
ND = D // 128
EPS = 1e-5
LN8 = math.log(8.0)

ENGS = ("t", "v", "s", "p", "q")


class Sem:
    __slots__ = ("name", "n", "h")

    def __init__(self, name):
        self.name = name
        self.n = 0
        self.h = None


class Em:

    def __init__(self, nc):
        self.nc = nc
        self.ops = {e: [] for e in ENGS}
        self.sems = {}

    def sem(self, name):
        if name not in self.sems:
            self.sems[name] = Sem(name)
        return self.sems[name]

    def emit(self, eng, fn, waits=(), inc=None, n=1):
        w = [(s, v) for (s, v) in waits if v > 0]
        self.ops[eng].append((w, fn, inc, n))
        if inc is not None:
            inc.n += n
            return inc.n
        return None

    def token(self, sem):
        return sem.n

    def dma(self, eng, out, in_, waits=(), inc=None, **kw):
        return self.emit(eng, lambda e: e.dma_start(out=out, in_=in_, **kw),
                         waits=waits, inc=inc, n=16)

    def replay(self, prewaits=()):
        nc = self.nc
        for s in self.sems.values():
            s.h = nc.alloc_semaphore(name=s.name)
        with nc.Block() as block:
            def body(eng_ops, pre):
                def _(eng):
                    for (s, v) in pre:
                        if v > 0:
                            eng.wait_ge(s.h, v)
                    for (waits, fn, inc, n) in eng_ops:
                        for (s, v) in waits:
                            eng.wait_ge(s.h, v)
                        ins = fn(eng)
                        if inc is not None:
                            assert ins is not None
                            ins.then_inc(inc.h, n)
                return _
            if self.ops["q"]:
                block.sync(body(self.ops["q"], prewaits))
            if self.ops["p"]:
                block.gpsimd(body(self.ops["p"], prewaits))
            if self.ops["t"]:
                block.tensor(body(self.ops["t"], prewaits))
            if self.ops["v"]:
                block.vector(body(self.ops["v"], ()))
            if self.ops["s"]:
                block.scalar(body(self.ops["s"], prewaits))



def host_consts():
    c = {}
    bf = ml_dtypes.bfloat16
    c["c_ident"] = np.eye(128, dtype=np.float32).astype(bf)
    u = np.triu(np.ones((64, 64), np.float32)) / 16.0
    c["c_u16"] = u.astype(bf)
    c["c_triu"] = np.triu(np.ones((64, 64), np.float32)).astype(bf)
    tr2 = np.concatenate([np.triu(np.ones((64, 64), np.float32))] * 2, axis=1)
    c["c_triu2"] = tr2.astype(bf)
    masks = []
    kk = np.arange(128)[:, None]
    qq = np.arange(512)[None, :]
    for j in range(4):
        m = np.where(qq >= 128 * j + kk, 0.0, -1e30).astype(np.float32)
        masks.append(m.astype(bf))
    c["c_cmask"] = np.concatenate(masks, axis=1)
    mmasks = [np.where(qq >= 128 * j + kk, 1.0, 0.0).astype(np.float32).astype(bf)
              for j in range(4)]
    c["c_mmask"] = np.concatenate(mmasks, axis=1)
    pm = np.zeros((128, 128), np.float32)
    for r in range(128):
        blk = (r // 64) * 64
        pm[r, blk + (r % 64 + 32) % 64] = 1.0
    c["c_rot128"] = pm.T.astype(bf)
    o65 = np.zeros((65, 128), np.float32)
    o65[64, :] = 1.0
    c["c_ones65"] = o65
    c["c_ones_row32"] = np.ones((1, 128), np.float32)
    e = (np.arange(512)[None, :] // 16 == np.arange(32)[:, None]).astype(np.float32)
    c["c_e32"] = e.astype(bf)
    c["c_e32h"] = (0.5 * e).astype(bf)
    hd = 64
    inv = 1.0 / (10000.0 ** (np.arange(0, hd, 2, dtype=np.float64) / hd))
    tpos = np.arange(T, dtype=np.float64)
    fr = np.outer(tpos, inv)
    emb = np.concatenate([fr, fr], axis=1)
    cos = np.cos(emb).T.astype(np.float32)
    sin = np.sin(emb).T.astype(np.float32)
    c["c_cos"] = np.concatenate([cos, cos], axis=0).astype(bf)
    c["c_sin"] = np.concatenate([sin, sin], axis=0).astype(bf)
    sr = np.concatenate([-sin[0:32], sin[32:64]], axis=0)
    c["c_cos2"] = c["c_cos"]
    c["c_sinrot"] = np.concatenate([sr, sr], axis=0).astype(bf)
    c["c_ones_row"] = np.ones((1, 128), np.float32).astype(bf)
    c["c_ones_col"] = np.ones((128, 1), np.float32).astype(bf)
    return c


class Ctx:

    def __init__(self, nc):
        self.nc = nc
        self.em = Em(nc)
        self.sb = {}
        self.ps = {}
        self.sv = self.em.sem("sv")
        self.ss = self.em.sem("ss")
        self.st_ = self.em.sem("stn")
        self.sq = self.em.sem("sq")
        self.sp = self.em.sem("sp")
        self.sqc = self.em.sem("sqc")
        self.sqx = self.em.sem("sqx")
        self.sl0w = [self.em.sem("sl0w0"), self.em.sem("sl0w1")]
        self.sprin = self.em.sem("sprin")
        self.sglaw = self.em.sem("sglaw")
        self.swvg = self.em.sem("swvg")
        self.sout = self.em.sem("sout")
        self.spout = self.em.sem("spout")

    def sbuf(self, name, shape, dtype, at=None):
        if name not in self.sb:
            if getattr(self, "frozen", False):
                raise RuntimeError("unplanned sbuf alloc: " + name)
            if at is None:
                self.sb[name] = self.nc.alloc_sbuf_tensor(name, shape, dtype)
            else:
                self.sb[name] = self.nc.alloc_sbuf_tensor_at(
                    name, shape, dtype, offset=at)
        return self.sb[name]

    def psum(self, name, shape, dtype=F32):
        if name not in self.ps:
            self.ps[name] = self.nc.alloc_psum_tensor(name, shape, dtype)
        return self.ps[name]

    def load_consts(self, consts, skip=()):
        for k, arr in consts.items():
            if k in skip:
                continue
            dt = BF16 if arr.dtype == ml_dtypes.bfloat16 else F32
            h = self.nc.dram_tensor(k, list(arr.shape), dt, kind="ExternalInput")
            t = self.sbuf(k + "_sb", list(arr.shape), dt)
            self.em.dma("q", t[:, :], h.ap(), inc=self.sqc)
        self.tok_consts = self.em.token(self.sqc)

    def cst(self, name):
        return self.sb[name + "_sb"]

    FCONST_VALUES = (EPS, 1.0 / D, -0.5, -LN8, -1.0, 1.0 / 128, 1.0, 0.5,
                     0.125, 2.0)

    def prelude_fconsts(self):
        vals = list(self.FCONST_VALUES)
        pool = self.sbuf("fconst_pool", [128, len(vals)], F32)
        tok = None
        for i, val in enumerate(vals):
            ap = pool[:, i:i + 1]
            last = (i == len(vals) - 1)
            tok = self.em.emit("v", lambda v, ap=ap, val=val: v.memset(ap, val),
                               inc=self.sv if last else None)
        for i, val in enumerate(vals):
            self.nc.const_aps.register(F32, float(val), pool[:, i:i + 1])
        self.tok_fc = tok
        return [(self.sv, tok)]



def emit_ln_tile(cx, x_ap, out_ap, scratch, waits=(), width=D, out2_ap=None):
    em = cx.em
    s1, mu, ss2, rstd, xc = (scratch[k] for k in
                             ("s1", "mu", "ss", "rstd", "xc"))
    tv = em.emit("v", lambda v: v.reduce_sum(s1[:, 0:1], x_ap, axis=AX.X),
                 waits=waits, inc=cx.sv)
    ts = em.emit("s", lambda s: s.mul(mu[:, 0:1], s1[:, 0:1], 1.0 / width),
                 waits=[(cx.sv, tv)], inc=cx.ss)
    tv = em.emit("v", lambda v: v.tensor_scalar(xc[:, 0:width], x_ap, mu[:, 0:1],
                                                None, op0=ALU.subtract),
                 waits=[(cx.ss, ts)], inc=cx.sv)
    ts = em.emit("s", lambda s: s.activation(out_ap, xc[:, 0:width],
                                             AF.Square, accum_out=ss2[:, 0:1]),
                 waits=[(cx.sv, tv)], inc=cx.ss)
    ts = em.emit("s", lambda s: s.activation(rstd[:, 0:1], ss2[:, 0:1], AF.Ln,
                                             bias=EPS, scale=1.0 / width),
                 waits=[(cx.ss, ts)], inc=cx.ss)
    ts = em.emit("s", lambda s: s.activation(rstd[:, 0:1], rstd[:, 0:1], AF.Exp,
                                             scale=-0.5),
                 waits=[(cx.ss, ts)], inc=cx.ss)
    tv = em.emit("v", lambda v: v.tensor_scalar(out_ap, xc[:, 0:width],
                                                rstd[:, 0:1], None, op0=ALU.mult),
                 waits=[(cx.ss, ts)], inc=cx.sv)
    if out2_ap is not None:
        tv = em.emit("v", lambda v: v.tensor_scalar(out2_ap, xc[:, 0:width],
                                                    rstd[:, 0:1], None,
                                                    op0=ALU.mult), inc=cx.sv)
    return tv, ts


def ln_scratch(cx, tag=""):
    return {
        "s1": cx.sbuf("lns1" + tag, [128, 1], F32),
        "mu": cx.sbuf("lnmu" + tag, [128, 1], F32),
        "ss": cx.sbuf("lnss" + tag, [128, 1], F32),
        "rstd": cx.sbuf("lnrstd" + tag, [128, 1], F32),
        "xc": cx.sbuf("lnxc" + tag, [128, D], F32),
    }



class TransposeHelper:

    def __init__(self, cx, tag="tr"):
        self.cx = cx
        self.p0 = cx.psum(tag + "_p0", [128, 1024], BF16)
        self.p1 = cx.psum(tag + "_p1", [128, 1024], BF16)
        self.last_evac = [0, 0]
        self.last_eng = ["v", "v"]
        self.slot = 0

    def transpose(self, in_ap, out_ap, waits=(), evac_eng="s"):
        cx, em = self.cx, self.cx.em
        sl = self.slot
        self.slot ^= 1
        ps = (self.p0, self.p1)[sl]
        pin, fin = in_ap.shape
        ident = cx.cst("c_ident")
        w = list(waits) + [(cx.sqc, cx.tok_consts)]
        if self.last_evac[sl]:
            sem = cx.sv if self.last_eng[sl] == "v" else cx.ss
            w.append((sem, self.last_evac[sl]))
        tt = em.emit("t", lambda t: t.transpose(ps[0:fin, 0:pin], in_ap,
                                                ident[0:pin, 0:pin]),
                     waits=w, inc=cx.st_)
        if evac_eng == "s":
            te = em.emit("s", lambda s: s.copy(out_ap, ps[0:fin, 0:pin]),
                         waits=[(cx.st_, tt)], inc=cx.ss)
        else:
            te = em.emit("v", lambda v: v.tensor_copy(out_ap, ps[0:fin, 0:pin]),
                         waits=[(cx.st_, tt)], inc=cx.sv)
        self.last_evac[sl] = te
        self.last_eng[sl] = evac_eng
        return te



class Fp4Helper:
    def __init__(self, cx, tag="fp4", **_):
        self.cx = cx
        self.last_tre = 0
        self.amax = cx.sbuf(tag + "_amax", [128, 64], F32)
        self.scf = cx.sbuf(tag + "_scf", [128, 64], F32)
        self.rs = cx.sbuf(tag + "_rs", [128, 64], F32)
        self.xn = cx.sbuf(tag + "_xn", [128, 1024], BF16)
        self.u = cx.sbuf(tag + "_u", [128, 1024], BF16)
        self.sg = cx.sbuf(tag + "_sg", [128, 1024], BF16)
        self.a = cx.sbuf(tag + "_a", [128, 1024], BF16)
        self.c = cx.sbuf(tag + "_c", [128, 1024], BF16)

    def quant(self, x_ap, out_ap, W, waits=(), pe_gate=0):
        cx, em = self.cx, self.cx.em
        B = W // 16
        x3 = x_ap.rearrange("p (b k) -> p b k", k=16)
        o3 = out_ap.rearrange("p (b k) -> p b k", k=16)
        xn3 = self.xn[:, 0:W].rearrange("p (b k) -> p b k", k=16)
        a3 = self.a[:, 0:W].rearrange("p (b k) -> p b k", k=16)
        sc_b = self.scf[:, 0:B].unsqueeze(2).broadcast_to((128, B, 16))
        tv = em.emit("v", lambda v: v.reduce_max(self.amax[:, 0:B].unsqueeze(2),
                                                 x3, axis=AX.X,
                                                 apply_absolute_value=True),
                     waits=waits, inc=cx.sv)
        tv0 = em.emit("v", lambda v: v.tensor_scalar(self.scf[:, 0:B],
                                                     self.amax[:, 0:B],
                                                     1.0 / 6.0, 1e-12,
                                                     op0=ALU.mult, op1=ALU.max),
                      waits=[(cx.sv, tv)], inc=cx.sv)
        trc = em.emit("v", lambda v: v.reciprocal(self.rs[:, 0:B],
                                                  self.scf[:, 0:B]),
                      waits=[(cx.sv, tv0)], inc=cx.sv)
        rs_b = self.rs[:, 0:B].unsqueeze(2).broadcast_to((128, B, 16))
        tvx = em.emit("v", lambda v: v.tensor_tensor(xn3, x3, rs_b,
                                                     op=ALU.mult),
                      waits=[(cx.sv, trc)], inc=cx.sv)
        tu = em.emit("s", lambda s: s.activation(self.u[:, 0:W],
                                                 self.xn[:, 0:W], AF.Abs),
                     waits=[(cx.sv, tvx)], inc=cx.ss)
        tsg = em.emit("s", lambda s: s.sign(self.sg[:, 0:W], self.xn[:, 0:W]),
                      inc=cx.ss)
        tl = em.emit("v", lambda v: v.tensor_scalar(self.a[:, 0:W],
                                                    self.u[:, 0:W],
                                                    0.25, 1.0, op0=ALU.is_gt,
                                                    op1=ALU.mult),
                     waits=[(cx.ss, tu)], inc=cx.sv)
        for thr in (0.75, 1.25, 1.75):
            tl = em.emit("v", lambda v, th=thr: v.scalar_tensor_tensor(
                self.a[:, 0:W], self.u[:, 0:W], th, self.a[:, 0:W],
                op0=ALU.is_gt, op1=ALU.add), waits=[(cx.sv, tl)], inc=cx.sv)
        tc = em.emit("v", lambda v: v.tensor_scalar(self.c[:, 0:W],
                                                    self.u[:, 0:W],
                                                    5.0, 2.0, op0=ALU.is_gt,
                                                    op1=ALU.mult), inc=cx.sv)
        for thr in (2.5, 3.5):
            tc = em.emit("v", lambda v, th=thr: v.scalar_tensor_tensor(
                self.c[:, 0:W], self.u[:, 0:W], th, self.c[:, 0:W],
                op0=ALU.is_gt, op1=ALU.add), waits=[(cx.sv, tc)], inc=cx.sv)
        tl = em.emit("v", lambda v: v.scalar_tensor_tensor(
            self.a[:, 0:W], self.a[:, 0:W], 0.5, self.c[:, 0:W],
            op0=ALU.mult, op1=ALU.add),
            waits=[(cx.sv, max(tl, tc))], inc=cx.sv)
        tl = em.emit("v", lambda v: v.tensor_tensor(self.a[:, 0:W],
                                                    self.a[:, 0:W],
                                                    self.sg[:, 0:W],
                                                    op=ALU.mult),
                     waits=[(cx.ss, tsg), (cx.sv, tl)], inc=cx.sv)
        tv = em.emit("v", lambda v: v.tensor_tensor(o3, a3, sc_b,
                                                    op=ALU.mult),
                     waits=[(cx.sv, tl)], inc=cx.sv)
        return tv



def chunks_of(total, step=512):
    out = []
    c0 = 0
    while c0 < total:
        cw = min(step, total - c0)
        out.append((c0, cw))
        c0 += cw
    return out


def alloc_l0f_bufs(cx):
    return {
        "GBUF": [cx.sbuf(f"l0f_g{i}", [128, 8 * 512], BF16) for i in range(2)],
        "UBUF": [cx.sbuf(f"l0f_u{i}", [128, 8 * 512], BF16) for i in range(2)],
        "DBUF": [cx.sbuf(f"l0f_d{i}", [128, 4 * 1024], BF16) for i in range(2)],
        "HQT": cx.sbuf("l0f_hqt", [128, 4 * 2048], BF16),
    }


class QSched:

    def __init__(self, fp4s):
        self.fp4s = fp4s
        self.i = 0

    def quant(self, x_ap, out_ap, W, waits=()):
        h = self.fp4s[self.i % len(self.fp4s)]
        self.i += 1
        return h.quant(x_ap, out_ap, W, waits=waits)


def prep_units(cx, qs, th, srcs, dsts):
    em = cx.em
    sld = [cx.sprin, em.sem("sprin_b")]
    sst = [cx.sout, em.sem("sprep_ob")]
    wt = [cx.sbuf("prp_w0", [128, 1024], BF16),
          cx.sbuf("prp_w1", [128, 1024], BF16)]
    state = {"tq": [0, 0], "ttr": [0, 0], "tst": [0, 0], "ui": 0}
    chs = chunks_of(II)
    chunk_tok = [{"g": {}, "u": {}, "d": {}} for _ in chs]

    def part_b(par, cw, tq, dst_slice, nch, kind):
        wtile = stg = wt[par]
        te = 0
        for kt in range(cw // 128):
            te = th.transpose(wtile[:, kt * 128:(kt + 1) * 128],
                              stg[:, kt * 128:(kt + 1) * 128],
                              waits=[(cx.sv, tq),
                                     (sst[par], state["tst"][par])])
        state["ttr"][par] = te
        tso = em.dma("q", dst_slice.rearrange("(a p) c -> p a c", p=128),
                     stg[:, 0:cw].rearrange("p (a c) -> p a c", c=128),
                     waits=[(cx.ss, te)], inc=sst[par])
        state["tst"][par] = tso
        chunk_tok[nch][kind][sst[par].name] = (sst[par], tso)

    pending = []

    def unit(src, rows0, c0, cw, dst_slice, nch, kind):
        def run():
            if pending:
                pending.pop(0)()
            par = state["ui"] % 2
            state["ui"] += 1
            wtile = wt[par]
            w = []
            if state["tq"][par]:
                w.append((cx.sv, state["tq"][par]))
            if state["ttr"][par]:
                w.append((cx.ss, state["ttr"][par]))
            if state["tst"][par]:
                w.append((sst[par], state["tst"][par]))
            tl = em.dma("p", wtile[:, 0:cw],
                        src[rows0:rows0 + 128, c0:c0 + cw],
                        waits=w, inc=sld[par])
            tq = qs.quant(wtile[:, 0:cw], wtile[:, 0:cw], cw,
                          waits=[(sld[par], tl)])
            state["tq"][par] = tq
            pending.append(
                lambda par=par: part_b(par, cw, tq, dst_slice, nch, kind))
        return run

    def drain():
        while pending:
            pending.pop(0)()

    wg, wu, wd = srcs
    wgT, wuT, wdT = dsts
    units = []
    for nch, (c0, cw) in enumerate(chs):
        for r in range(c0 // 128, (c0 + cw) // 128):
            units.append(unit(wg, r * 128, 0, D,
                              wgT.ap()[:, r * 128:(r + 1) * 128], nch, "g"))
            units.append(unit(wu, r * 128, 0, D,
                              wuT.ap()[:, r * 128:(r + 1) * 128], nch, "u"))
        for rr in range(D // 128):
            units.append(unit(wd, rr * 128, c0, cw,
                              wdT.ap()[c0:c0 + cw, rr * 128:(rr + 1) * 128],
                              nch, "d"))
    units.append(lambda: drain())
    return units, chunk_tok



def stage_l0ffn(cx, qs, th, XRES, XQT, wgT, wuT, wdT, bufs, tag,
                start_waits=(), xres_waits=(), units=None, chunk_tok=None):
    em = cx.em
    Htmp = cx.sbuf("l0f_h", [128, D], BF16)
    XQ = cx.sbuf("l0f_xq", [128, D], BF16)
    XQ2 = cx.sbuf("l0f_xq2", [128, D], BF16)
    XQ3 = cx.sbuf("l0f_xq3", [128, D], BF16)
    scr = ln_scratch(cx, "z")
    feed = list(units) if units else []

    def tick():
        if feed:
            feed.pop(0)()

    def qcall(x_ap, out_ap, W, waits):
        return qs.quant(x_ap, out_ap, W, waits=waits)
    def p0_flush(pend):
        st_p, xqb, tq_p = pend
        for kd in range(ND):
            th.transpose(xqb[:, kd * 128:(kd + 1) * 128],
                         XQT[:, kd * T + st_p * 128:kd * T + (st_p + 1) * 128],
                         waits=[(cx.sv, tq_p)])

    p0_pend = None
    for st in range(NT):
        tick()
        tv, _ = emit_ln_tile(cx, XRES[:, st * D:(st + 1) * D], Htmp[:, 0:D], scr,
                             waits=xres_waits)
        xqb = (XQ, XQ2, XQ3)[st % 3]
        tv2 = qcall(Htmp[:, 0:D], xqb[:, 0:D], D,
                    [(cx.sv, tv), (cx.st_, cx.em.token(cx.st_))])
        if p0_pend is not None:
            p0_flush(p0_pend)
        p0_pend = (st, xqb, tv2)
    p0_flush(p0_pend)
    GBUF = bufs["GBUF"]
    UBUF = bufs["UBUF"]
    DBUF = bufs["DBUF"]
    HQT = bufs["HQT"]
    TG = cx.sbuf("l0f_tg", [128, 512], BF16)
    TU = cx.sbuf("l0f_tu", [128, 512], BF16)
    PS_G = cx.psum("PS_G", [128, 512], F32)
    PS_U = cx.psum("PS_U", [128, 512], F32)
    PS_D = [cx.psum("l0f_psd0", [128, 512], F32),
            cx.psum("l0f_psd1", [128, 512], F32)]
    last_d_evac = [0, 0]
    last_use = [0, 0]
    tok_load = [0, 0]
    tvh_prev = 0
    ts2_prev = 0
    chs = chunks_of(2816)
    for nch, (c0, cw) in enumerate(chs):
        par = nch % 2
        nkt = (cw + 127) // 128
        def do_loads(par=par, c0=c0, cw=cw, nkt=nkt, nch=nch):
            w = ([(cx.st_, last_use[par])] if last_use[par]
                 else list(start_waits))
            wu_w, wd_w = [], []
            if chunk_tok is not None:
                for kind, dst in (("g", w), ("u", wu_w), ("d", wd_w)):
                    toks = list(chunk_tok[nch][kind].values())
                    assert toks, (tag, nch, kind)
                    dst.extend(toks)
            em.dma("q", GBUF[par][:, 0:8 * cw].rearrange("p (a c) -> p a c", a=ND),
                   wgT.ap()[:, c0:c0 + cw].rearrange("(a p) c -> p a c", p=128),
                   waits=w, inc=cx.sl0w[par])
            em.dma("q", UBUF[par][:, 0:8 * cw].rearrange("p (a c) -> p a c", a=ND),
                   wuT.ap()[:, c0:c0 + cw].rearrange("(a p) c -> p a c", p=128),
                   waits=wu_w, inc=cx.sl0w[par])
            tl = em.dma("q", DBUF[par][:, 0:nkt * 1024].rearrange("p (a c) -> p a c", a=nkt),
                        wdT.ap()[c0:c0 + cw, :].rearrange("(a p) c -> p a c", p=128),
                        waits=wd_w, inc=cx.sl0w[par])
            tok_load[par] = tl
        tmm = 0
        pendq = []

        def flush(pend, par=par, cw=cw, nkt=nkt):
            st_p, tq_p, xqb = pend
            te = 0
            for kt in range(nkt):
                te = th.transpose(xqb[:, kt * 128:(kt + 1) * 128],
                                  HQT[:, kt * T + st_p * 128:kt * T + (st_p + 1) * 128],
                                  waits=[(cx.sv, tq_p)])
            for oc in range(2):
                dp = oc % 2
                wt2 = [(cx.ss, te), (cx.sv, last_d_evac[dp])]
                tdm = 0
                for kt in range(nkt):
                    tdm = em.emit("t", lambda t, kt=kt, oc=oc, par=par, dp=dp, st=st_p, nkt=nkt: t.matmul(
                        PS_D[dp][:, 0:512],
                        HQT[:, kt * T + st * 128:kt * T + (st + 1) * 128],
                        DBUF[par][:, kt * 1024 + oc * 512:kt * 1024 + (oc + 1) * 512],
                        start=(kt == 0), stop=(kt == nkt - 1)),
                        waits=wt2 if kt == 0 else (),
                        inc=cx.st_ if kt == nkt - 1 else None)
                last_d_evac[dp] = em.emit(
                    "v", lambda v, st=st_p, oc=oc, dp=dp: v.scalar_tensor_tensor(
                        XRES[:, st * D + oc * 512:st * D + (oc + 1) * 512],
                        PS_D[dp][:, 0:512], 0.5,
                        XRES[:, st * D + oc * 512:st * D + (oc + 1) * 512],
                        op0=ALU.mult, op1=ALU.add),
                    waits=[(cx.st_, tdm)], inc=cx.sv)

        for st in range(NT):
            tick()
            if st == 0:
                do_loads()
            wt = [(cx.sl0w[par], tok_load[par]), (cx.ss, ts2_prev)]
            if st == 0 and nch == 0:
                wt.append((cx.ss, cx.em.token(cx.ss)))
            for kd in range(ND):
                tmm = em.emit("t", lambda t, kd=kd, par=par, cw=cw, st=st: t.matmul(
                    PS_G[:, 0:cw],
                    XQT[:, kd * T + st * 128:kd * T + (st + 1) * 128],
                    GBUF[par][:, kd * cw:(kd + 1) * cw],
                    start=(kd == 0), stop=(kd == ND - 1)),
                    waits=wt if kd == 0 else (), inc=cx.st_ if kd == ND - 1 else None)
            for kd in range(ND):
                tmm = em.emit("t", lambda t, kd=kd, par=par, cw=cw, st=st: t.matmul(
                    PS_U[:, 0:cw],
                    XQT[:, kd * T + st * 128:kd * T + (st + 1) * 128],
                    UBUF[par][:, kd * cw:(kd + 1) * cw],
                    start=(kd == 0), stop=(kd == ND - 1)),
                    inc=cx.st_ if kd == ND - 1 else None)
            ts1 = em.emit("s", lambda s, cw=cw: s.activation(TG[:, 0:cw],
                                                             PS_G[:, 0:cw],
                                                             AF.Silu),
                          waits=[(cx.st_, tmm), (cx.sv, tvh_prev)], inc=cx.ss)
            ts2 = em.emit("s", lambda s, cw=cw: s.copy(TU[:, 0:cw], PS_U[:, 0:cw]),
                          inc=cx.ss)
            ts2_prev = ts2
            tvh = em.emit("v", lambda v, cw=cw: v.tensor_tensor(
                Htmp[:, 0:cw], TG[:, 0:cw], TU[:, 0:cw], op=ALU.mult),
                waits=[(cx.ss, ts2)], inc=cx.sv)
            tvh_prev = tvh
            xqb = (XQ, XQ2, XQ3)[st % 3]
            tq = qcall(Htmp[:, 0:cw], xqb[:, 0:cw], cw,
                       [(cx.sv, tvh), (cx.st_, cx.em.token(cx.st_))])
            pendq.append((st, tq, xqb))
            if len(pendq) > 2:
                flush(pendq.pop(0))
        while pendq:
            flush(pendq.pop(0))
        last_use[par] = cx.em.token(cx.st_)
    while feed:
        feed.pop(0)()
    return max(last_d_evac)



def stage_ln_transpose(cx, th, XRES, HT, tag, xres_waits=()):
    Htmp = cx.sbuf("lnt_h" + tag, [128, D], BF16)
    scr = ln_scratch(cx, "z")
    te = 0
    for st in range(NT):
        tv, _ = emit_ln_tile(cx, XRES[:, st * D:(st + 1) * D], Htmp[:, 0:D], scr,
                             waits=xres_waits)
        for kd in range(ND):
            te = th.transpose(Htmp[:, kd * 128:(kd + 1) * 128],
                              HT[:, kd * T + st * 128:kd * T + (st + 1) * 128],
                              waits=[(cx.sv, tv)])
    return te



GLA_H = 8
GLA_DK = 64
GLA_DV = 128
CHUNK = 64
SC_TOK = 256
NSC = T // SC_TOK


def stage_gla_weights(cx, w, bufs):
    em = cx.em
    tl = 0
    for name, ap in (("WQ", w["Wq"]), ("WK", w["Wk"]),
                     ("WGK1", w["Wgk1"])):
        dst = bufs[name]
        tl = em.dma("p", dst[:, :].rearrange("p (a c) -> p a c", a=ND),
                    ap.rearrange("(a p) c -> p a c", p=128), inc=cx.sglaw)
    tl = em.dma("p", bufs["WGK2"][:, :], w["Wgk2"], inc=cx.sglaw)
    tl = em.dma("p", bufs["BGK2"][:, :], w["bgk2"].unsqueeze(0), inc=cx.sglaw)
    tl = em.dma("p", bufs["NW"][:, :], w["norm_w"].unsqueeze(1), inc=cx.sglaw)
    return {"sp": tl, "sv": 0}


def stage_gla(cx, th, XRES, HT, bufs, wtok, w, tag, ht_tok=0, phase=3):
    em = cx.em
    cs = cx.cst
    QT = cx.sbuf("gla_qt", [128, 4 * SC_TOK], BF16)
    KT = cx.sbuf("gla_kt", [128, 4 * SC_TOK], BF16)
    KTOK = cx.sbuf("gla_ktok", [64, 4 * 512], BF16)
    VTOK = cx.sbuf("gla_vtok", [64, 4 * 1024], BF16)
    GKC = cx.sbuf("gla_gk", [64, 4 * 512], BF16)
    T1T = cx.sbuf("gla_t1t", [16, SC_TOK], BF16)
    SGT = cx.sbuf("gla_sg", [64, 512], BF16)
    EB = cx.sbuf("gla_eb", [128, 4 * 256], BF16)
    EMB = cx.sbuf("gla_emb", [128, 4 * 256], BF16)
    QET = cx.sbuf("gla_qet", [128, 4 * 256], BF16)
    KET = cx.sbuf("gla_ket", [128, 4 * 256], BF16)
    EMBT = cx.sbuf("gla_embt", [64, 512], BF16)
    KEMB = cx.sbuf("gla_kemb", [64, 4 * 512], BF16)
    ELAST = cx.sbuf("gla_elast", [128, 4 * 4], F32)
    ATS = cx.sbuf("gla_ats", [64, 128], BF16)
    SHP = [cx.sbuf(f"gla_s{hp}", [128, GLA_DV], F32) for hp in range(4)]
    SHPB = [cx.sbuf(f"gla_sb{hp}", [128, GLA_DV], BF16) for hp in range(4)]
    OSC = cx.sbuf("gla_osc", [64, 4 * 1024], BF16)
    OTMP = cx.sbuf("gla_otmp", [64, 256], BF16)
    GTOK = cx.sbuf("gla_gtok", [64, 4 * 1024], BF16)
    OGT = cx.sbuf("gla_ogt", [128, ND * SC_TOK], BF16)
    RSTD = cx.sbuf("gla_rstd", [64, 4 * 8], F32)
    SSQ = cx.sbuf("gla_ssq", [64, 4 * 8], F32)
    SQS = cx.sbuf("gla_sqs", [64, 128], F32)
    LNT = cx.sbuf("gla_lnt", [64, 4 * 8], F32)
    PSP = cx.ps["PS_G"]
    PSB = cx.ps["PS_U"]
    PSD0 = cx.ps["l0f_psd0"]
    PSD1 = cx.ps["l0f_psd1"]

    tv0 = 0
    for hp in range(4):
        em.emit("v", lambda v, hp=hp: v.memset(SHP[hp][:, :], 0.0))
        tv0 = em.emit("v", lambda v, hp=hp: v.memset(SHPB[hp][:, :], 0.0),
                      inc=cx.sv)

    tok_sc_done = 0
    tok_out_evac = 0
    tok_proj_evac = ht_tok
    tok_wo_mm = 0
    import os as _os
    nsc_lim = int(_os.environ.get("GLA_NSC", str(NSC)))
    for sc in range(nsc_lim):
        st0 = 2 * sc
        pe_tok = 0
        evs = []
        for (dst, wname) in ((QT, "WQ"), (KT, "WK")):
            for hp in range(4):
                wt = [(cx.sglaw, wtok["sp"]), (cx.ss, tok_proj_evac),
                      (cx.st_, tok_wo_mm), (cx.sv, tok_sc_done)]
                for kd in range(ND):
                    pe_tok = em.emit("t", lambda t, kd=kd, hp=hp, sc=sc, wname=wname: t.matmul(
                        PSP[:, 0:SC_TOK],
                        bufs[wname][:, kd * 512 + hp * 128:kd * 512 + (hp + 1) * 128],
                        HT[:, kd * T + sc * SC_TOK:kd * T + (sc + 1) * SC_TOK],
                        start=(kd == 0), stop=(kd == ND - 1)),
                        waits=wt if kd == 0 else (),
                        inc=cx.st_ if kd == ND - 1 else None)
                ev = em.emit("s", lambda s, dst=dst, hp=hp: s.copy(
                    dst[:, hp * SC_TOK:(hp + 1) * SC_TOK], PSP[:, 0:SC_TOK]),
                    waits=[(cx.st_, pe_tok)], inc=cx.ss)
                tok_proj_evac = ev
        for cc in range(4):
            c = 4 * sc + cc
            stc = c // 2
            rb = 64 * (c % 2)
            for kd in range(ND):
                pe_tok = em.emit("t", lambda t, kd=kd, stc=stc, rb=rb, c=c: t.matmul(
                    PSP[0:64, 0:512],
                    HT[:, kd * T + c * 64:kd * T + (c + 1) * 64],
                    bufs["WK"][:, kd * 512:(kd + 1) * 512],
                    start=(kd == 0), stop=(kd == ND - 1)),
                    waits=[(cx.ss, tok_proj_evac)] if kd == 0 else (),
                    inc=cx.st_ if kd == ND - 1 else None)
            tok_proj_evac = em.emit("s", lambda s, cc=cc: s.copy(
                KTOK[0:64, cc * 512:(cc + 1) * 512], PSP[0:64, 0:512]),
                waits=[(cx.st_, pe_tok)], inc=cx.ss)
        WVG = bufs["WVG"]
        for (dst, wap) in ((VTOK, w["Wv"]), (GTOK, w["Wg"])):
            tok_wvg = em.dma("p", WVG[:, :].rearrange("p (a c) -> p a c", a=ND),
                             wap.rearrange("(a p) c -> p a c", p=128),
                             waits=[(cx.st_, em.token(cx.st_))], inc=cx.swvg)
            for cc in range(4):
                c = 4 * sc + cc
                for oc in range(2):
                    for kd in range(ND):
                        pe_tok = em.emit("t", lambda t, kd=kd, oc=oc, c=c: t.matmul(
                            PSP[0:64, 0:512],
                            HT[:, kd * T + c * 64:kd * T + (c + 1) * 64],
                            WVG[:, kd * 1024 + oc * 512:kd * 1024 + (oc + 1) * 512],
                            start=(kd == 0), stop=(kd == ND - 1)),
                            waits=([(cx.ss, tok_proj_evac), (cx.swvg, tok_wvg)]
                                   if kd == 0 else ()),
                            inc=cx.st_ if kd == ND - 1 else None)
                    tok_proj_evac = em.emit("s", lambda s, dst=dst, cc=cc, oc=oc: s.copy(
                        dst[0:64, cc * 1024 + oc * 512:cc * 1024 + (oc + 1) * 512],
                        PSP[0:64, 0:512]),
                        waits=[(cx.st_, pe_tok)], inc=cx.ss)
        for kd in range(ND):
            pe_tok = em.emit("t", lambda t, kd=kd, sc=sc: t.matmul(
                T1T_ps(cx)[0:16, 0:SC_TOK],
                bufs["WGK1"][:, kd * 16:(kd + 1) * 16],
                HT[:, kd * T + sc * SC_TOK:kd * T + (sc + 1) * SC_TOK],
                start=(kd == 0), stop=(kd == ND - 1)),
                waits=[(cx.ss, tok_proj_evac)] if kd == 0 else (),
                inc=cx.st_ if kd == ND - 1 else None)
        tok_proj_evac = em.emit("s", lambda s: s.copy(T1T[0:16, 0:SC_TOK],
                                                      T1T_ps(cx)[0:16, 0:SC_TOK]),
                                waits=[(cx.st_, pe_tok)], inc=cx.ss)
        for cc in range(4):
            pe_tok = em.emit("t", lambda t, cc=cc: t.matmul(
                PSP[0:64, 0:512], T1T[0:16, cc * 64:(cc + 1) * 64],
                bufs["WGK2"][0:16, 0:512], start=True, stop=False),
                waits=[(cx.ss, tok_proj_evac)], inc=cx.st_)
            pe_tok = em.emit("t", lambda t, cc=cc: t.matmul(
                PSP[0:64, 0:512], cs("c_ones_row")[0:1, 0:64],
                bufs["BGK2"][0:1, 0:512], start=False, stop=True),
                inc=cx.st_)
            ts = em.emit("s", lambda s, cc=cc: s.activation(
                SGT[0:64, 0:512], PSP[0:64, 0:512], AF.Sigmoid),
                waits=[(cx.st_, pe_tok)], inc=cx.ss)
            tok_proj_evac = em.emit("s", lambda s, cc=cc: s.activation(
                GKC[0:64, cc * 512:(cc + 1) * 512], SGT[0:64, 0:512], AF.Ln),
                waits=[(cx.ss, ts)], inc=cx.ss)

        if phase < 2:
            continue
        tok_exp = {}
        tok_psb_read = tok_proj_evac
        tkm_prev = 0
        for hp in range(4):
            wb = [(cx.ss, tok_psb_read), (cx.sv, tok_sc_done)]
            pe2 = 0
            for cc in range(4):
                pe2 = em.emit("t", lambda t, cc=cc, hp=hp: t.matmul(
                    PSB[:, cc * 64:(cc + 1) * 64],
                    GKC[0:64, cc * 512 + hp * 128:cc * 512 + (hp + 1) * 128],
                    cs("c_u16")[0:64, 0:64], start=True, stop=True),
                    waits=wb if cc == 0 else (), inc=cx.st_)
                pe2 = em.emit("t", lambda t, cc=cc, hp=hp: t.matmul(
                    PSB[:, 256 + cc:256 + cc + 1],
                    GKC[0:64, cc * 512 + hp * 128:cc * 512 + (hp + 1) * 128],
                    cs("c_u16")[0:64, 63:64], start=True, stop=True),
                    inc=cx.st_)
            t_eb = em.emit("s", lambda s, hp=hp: s.activation(
                EB[:, hp * 256:(hp + 1) * 256], PSB[:, 0:256], AF.Exp,
                bias=-LN8), waits=[(cx.st_, pe2)], inc=cx.ss)
            t_emb = em.emit("s", lambda s, hp=hp: s.activation(
                EMB[:, hp * 256:(hp + 1) * 256], PSB[:, 0:256], AF.Exp,
                scale=-1.0), inc=cx.ss)
            t_el = em.emit("s", lambda s, hp=hp: s.activation(
                ELAST[:, hp * 4:(hp + 1) * 4], PSB[:, 256:260], AF.Exp),
                inc=cx.ss)
            tok_psb_read = t_el
            pe3 = 0
            for cc in range(4):
                pe3 = em.emit("t", lambda t, cc=cc, hp=hp: t.matmul(
                    PSD0[0:64, cc * 128:(cc + 1) * 128],
                    cs("c_u16")[0:64, 0:64],
                    GKC[0:64, cc * 512 + hp * 128:cc * 512 + (hp + 1) * 128],
                    start=True, stop=True),
                    waits=[(cx.ss, t_el)] if cc == 0 else (), inc=cx.st_)
            t_embt = em.emit("s", lambda s, hp=hp: s.activation(
                EMBT[0:64, 0:512], PSD0[0:64, 0:512],
                AF.Exp, scale=-1.0),
                waits=[(cx.st_, pe3), (cx.sv, tkm_prev)], inc=cx.ss)
            tqe = em.emit("v", lambda v, hp=hp: v.tensor_tensor(
                QET[:, hp * 256:(hp + 1) * 256], QT[:, hp * 256:(hp + 1) * 256],
                EB[:, hp * 256:(hp + 1) * 256], op=ALU.mult),
                waits=[(cx.ss, t_embt)], inc=cx.sv)
            tke = em.emit("v", lambda v, hp=hp: v.tensor_tensor(
                KET[:, hp * 256:(hp + 1) * 256], KT[:, hp * 256:(hp + 1) * 256],
                EMB[:, hp * 256:(hp + 1) * 256], op=ALU.mult), inc=cx.sv)
            tkm = 0
            for cc in range(4):
                tkm = em.emit("v", lambda v, cc=cc, hp=hp: v.tensor_tensor(
                    KEMB[0:64, hp * 512 + cc * 128:hp * 512 + (cc + 1) * 128],
                    KTOK[0:64, cc * 512 + hp * 128:cc * 512 + (hp + 1) * 128],
                    EMBT[0:64, cc * 128:(cc + 1) * 128],
                    op=ALU.mult), waits=[(cx.ss, t_embt)] if cc == 0 else (),
                    inc=cx.sv)
            tok_exp[hp] = (tqe, tke, tkm)
            tkm_prev = tkm

        if phase < 2.5:
            continue
        t_at_evac = 0
        t_oevac = em.token(cx.sv)
        t_o2cp = em.token(cx.ss)
        t_psd0_s = em.token(cx.ss)
        t_supd = [em.token(cx.sv)] * 4
        for cc in range(4):
            c = 4 * sc + cc
            for hp in range(4):
                tqe, tke, tkm = tok_exp[hp]
                tat = 0
                for e in range(2):
                    dstp = PSD1 if e == 0 else PSP
                    tat = em.emit("t", lambda t, cc=cc, hp=hp, e=e, dstp=dstp: t.matmul(
                        dstp[0:64, 0:64],
                        KET[64 * e:64 * (e + 1), hp * 256 + cc * 64:hp * 256 + (cc + 1) * 64],
                        QET[64 * e:64 * (e + 1), hp * 256 + cc * 64:hp * 256 + (cc + 1) * 64],
                        start=True, stop=True),
                        waits=([(cx.sv, em.token(cx.sv)), (cx.ss, t_o2cp)]
                               if e == 0 else ()),
                        inc=cx.st_)
                em.emit("v", lambda v: v.tensor_tensor(
                    ATS[0:64, 0:64], PSD1[0:64, 0:64], cs("c_triu")[0:64, 0:64],
                    op=ALU.mult), waits=[(cx.st_, tat)])
                t_at_evac = em.emit("v", lambda v: v.tensor_tensor(
                    ATS[0:64, 64:128], PSP[0:64, 0:64], cs("c_triu")[0:64, 0:64],
                    op=ALU.mult), inc=cx.sv)
                to = 0
                to2 = 0
                for e in range(2):
                    h = 2 * hp + e
                    to = em.emit("t", lambda t, cc=cc, hp=hp, e=e, h=h: t.matmul(
                        PSD0[0:64, 128 * e:128 * (e + 1)],
                        ATS[0:64, 64 * e:64 * (e + 1)],
                        VTOK[0:64, cc * 1024 + h * 128:cc * 1024 + (h + 1) * 128],
                        start=True, stop=True),
                        waits=([(cx.sv, t_at_evac), (cx.sv, t_oevac),
                                (cx.ss, t_psd0_s)] if e == 0 else ()),
                        inc=cx.st_)
                    dst2 = PSB if e == 0 else PSP
                    c2 = 0 if e == 0 else 128
                    to2 = em.emit("t", lambda t, cc=cc, hp=hp, e=e, dst2=dst2, c2=c2: t.matmul(
                        dst2[0:64, c2:c2 + 128],
                        QET[64 * e:64 * (e + 1), hp * 256 + cc * 64:hp * 256 + (cc + 1) * 64],
                        SHPB[hp][64 * e:64 * (e + 1), :],
                        start=True, stop=True),
                        waits=([(cx.sv, t_supd[hp]), (cx.ss, t_o2cp)]
                               if e == 0 else ()),
                        inc=cx.st_)
                em.emit("s", lambda s: s.copy(OTMP[0:64, 0:128],
                                              PSB[0:64, 0:128]),
                        waits=[(cx.st_, to2)])
                t_o2cp = em.emit("s", lambda s: s.copy(OTMP[0:64, 128:256],
                                                       PSP[0:64, 128:256]),
                                 inc=cx.ss)
                t_oevac = em.emit("v", lambda v, cc=cc, hp=hp: v.tensor_tensor(
                    OSC[0:64, cc * 1024 + hp * 256:cc * 1024 + (hp + 1) * 256],
                    PSD0[0:64, 0:256], OTMP[0:64, 0:256], op=ALU.add),
                    waits=[(cx.st_, to), (cx.ss, t_o2cp)], inc=cx.sv)
                tsm = 0
                for e in range(2):
                    h = 2 * hp + e
                    tsm = em.emit("t", lambda t, cc=cc, hp=hp, e=e, h=h: t.matmul(
                        PSD1[64 * e:64 * (e + 1), 0:GLA_DV],
                        KEMB[0:64, hp * 512 + cc * 128 + 64 * e:hp * 512 + cc * 128 + 64 * (e + 1)],
                        VTOK[0:64, cc * 1024 + h * 128:cc * 1024 + (h + 1) * 128],
                        start=True, stop=True, tile_position=(0, 64 * e)),
                        waits=[(cx.sv, t_at_evac), (cx.sv, t_supd[hp])] if e == 0 else (),
                        inc=cx.st_)
                tu1 = em.emit("v", lambda v, hp=hp: v.tensor_tensor(
                    SHP[hp][:, :], SHP[hp][:, :], PSD1[:, 0:GLA_DV], op=ALU.add),
                    waits=[(cx.st_, tsm)], inc=cx.sv)
                tu2 = em.emit("v", lambda v, hp=hp, cc=cc: v.tensor_scalar(
                    SHP[hp][:, :], SHP[hp][:, :],
                    ELAST[:, hp * 4 + cc:hp * 4 + cc + 1], None, op0=ALU.mult),
                    waits=[(cx.sv, tu1)], inc=cx.sv)
                t_supd[hp] = em.emit("v", lambda v, hp=hp: v.tensor_copy(
                    SHPB[hp][:, :], SHP[hp][:, :]), waits=[(cx.sv, tu2)],
                    inc=cx.sv)

        if phase < 3:
            continue
        t_sq = 0
        for cc in range(4):
            for h in range(GLA_H):
                wsq = [(cx.ss, t_sq)]
                if cc == 0 and h == 0:
                    wsq.append((cx.sv, t_oevac))
                t_sq = em.emit("s", lambda s, cc=cc, h=h: s.activation(
                    SQS[0:64, 0:128], OSC[0:64, cc * 1024 + h * 128:cc * 1024 + (h + 1) * 128],
                    AF.Square, accum_out=SSQ[0:64, cc * 8 + h:cc * 8 + h + 1]),
                    waits=wsq, inc=cx.ss)
        tln = em.emit("s", lambda s: s.activation(LNT[0:64, 0:32],
                                                  SSQ[0:64, 0:32],
                                                  AF.Ln, bias=EPS,
                                                  scale=1.0 / 128),
                      waits=[(cx.ss, t_sq)], inc=cx.ss)
        t_rs = em.emit("s", lambda s: s.activation(RSTD[0:64, 0:32],
                                                   LNT[0:64, 0:32], AF.Exp,
                                                   scale=-0.5),
                       waits=[(cx.ss, tln)], inc=cx.ss)
        t_sil = {}
        for cc in range(4):
            t_sil[cc] = em.emit("s", lambda s, cc=cc: s.activation(
                GTOK[0:64, cc * 1024:(cc + 1) * 1024],
                GTOK[0:64, cc * 1024:(cc + 1) * 1024], AF.Silu), inc=cx.ss)
        tvg = 0
        for cc in range(4):
            for h in range(GLA_H):
                tvg = em.emit("v", lambda v, cc=cc, h=h: v.tensor_scalar(
                    OSC[0:64, cc * 1024 + h * 128:cc * 1024 + (h + 1) * 128],
                    OSC[0:64, cc * 1024 + h * 128:cc * 1024 + (h + 1) * 128],
                    RSTD[0:64, cc * 8 + h:cc * 8 + h + 1], None, op0=ALU.mult),
                    waits=[(cx.ss, t_rs)] if (cc == 0 and h == 0) else (),
                    inc=cx.sv)
            tvg = em.emit("v", lambda v, cc=cc: v.tensor_tensor(
                OSC[0:64, cc * 1024:(cc + 1) * 1024],
                OSC[0:64, cc * 1024:(cc + 1) * 1024],
                GTOK[0:64, cc * 1024:(cc + 1) * 1024], op=ALU.mult),
                waits=[(cx.ss, t_sil[cc]), (cx.sv, tvg)], inc=cx.sv)
        tok_wo = em.dma("p", WVG[:, :].rearrange("p (a c) -> p a c", a=ND),
                        w["Wo"].rearrange("(a p) c -> p a c", p=128),
                        waits=[(cx.st_, em.token(cx.st_))], inc=cx.swvg)
        t_tr = 0
        for cc in range(4):
            for kd in range(ND):
                t_tr = th.transpose(
                    OSC[0:64, cc * 1024 + kd * 128:cc * 1024 + (kd + 1) * 128],
                    OGT[:, kd * SC_TOK + cc * 64:kd * SC_TOK + (cc + 1) * 64],
                    waits=[(cx.sv, tvg)])
        t_nw = 0
        for kd in range(ND):
            t_nw = em.emit("v", lambda v, kd=kd: v.tensor_scalar(
                OGT[:, kd * SC_TOK:(kd + 1) * SC_TOK],
                OGT[:, kd * SC_TOK:(kd + 1) * SC_TOK],
                bufs["NW"][:, 0:1], None, op0=ALU.mult),
                waits=[(cx.ss, t_tr)] if kd == 0 else (), inc=cx.sv)
        for tt in range(2):
            stt = st0 + tt
            for oc in range(2):
                pw = 0
                for kd in range(ND):
                    pw = em.emit("t", lambda t, kd=kd, tt=tt, oc=oc: t.matmul(
                        PSP[:, 0:512],
                        OGT[:, kd * SC_TOK + tt * 128:kd * SC_TOK + (tt + 1) * 128],
                        WVG[:, kd * 1024 + oc * 512:kd * 1024 + (oc + 1) * 512],
                        start=(kd == 0), stop=(kd == ND - 1)),
                        waits=([(cx.sv, t_nw), (cx.swvg, tok_wo),
                                (cx.sv, tok_sc_done)] if kd == 0 else ()),
                        inc=cx.st_ if kd == ND - 1 else None)
                tok_wo_mm = pw
                tok_sc_done = em.emit("v", lambda v, stt=stt, oc=oc: v.tensor_tensor(
                    XRES[:, stt * D + oc * 512:stt * D + (oc + 1) * 512],
                    XRES[:, stt * D + oc * 512:stt * D + (oc + 1) * 512],
                    PSP[:, 0:512], op=ALU.add),
                    waits=[(cx.st_, pw)], inc=cx.sv)
        tok_out_evac = tok_sc_done
    return tok_sc_done


def T1T_ps(cx):
    return cx.ps["l0f_psd1"]



NH = 16
HD = 64
TP = T + 2


def emit_ln_affine_tile(cx, x_ap, out_ap, scr, wrep, brep, waits=()):
    em = cx.em
    tv, ts = emit_ln_tile(cx, x_ap, out_ap, scr, waits=waits)
    tv = em.emit("v", lambda v: v.tensor_tensor(out_ap, out_ap, wrep[:, :],
                                                op=ALU.mult),
                 waits=[(cx.sv, tv)], inc=cx.sv)
    tv = em.emit("v", lambda v: v.tensor_tensor(out_ap, out_ap, brep[:, :],
                                                op=ALU.add),
                 waits=[(cx.sv, tv)], inc=cx.sv)
    return tv


def stage_attn(cx, th, xres_hbm, bufs, w, mode, tag, hbm_xinner=None,
               start_tok=(), xres_out=None):
    em = cx.em
    cs = cx.cst
    inner = (mode == "inner")
    HCT = bufs["HCT"]
    KTR = bufs["KTR"]
    VAUG = bufs["VAUG"]
    QTR = bufs["QTR"]
    T0 = bufs["T0"]
    TSW = bufs["TSW"]
    PT = bufs["PT"]
    YT = bufs["YT"]
    ZROW = bufs["ZROW"]
    ZC = bufs["ZC"]
    BCOL = bufs["BCOL"]
    WB1 = bufs["WB1"]
    WBO = bufs["WBO"]
    RZX = bufs["RZX"]
    SQB = bufs["SQB"]
    CVO = bufs["CVO"]
    Htmp = bufs["HTMP"]
    XT = bufs["XT"]
    XT2 = bufs["XT2"]
    scr = ln_scratch(cx, "z")
    sdma = cx.em.sem("sw_" + tag)
    sdma2 = cx.em.sem("swb_" + tag)
    sdo = cx.em.sem("swo_" + tag)
    sxa = cx.em.sem("sxa_" + tag)
    sxb = cx.em.sem("sxb_" + tag)
    PS_S = [cx.ps["PS_G"], cx.ps["PS_U"]]
    PS_PV = cx.ps["l0f_psd0"]
    PS_X = cx.ps["l0f_psd1"]

    if inner:
        q_ap = w["attn_w"][:, 0:1024]
        k_ap = w["attn_w"][:, 1024:2048]
        v_ap = w["attn_w"][:, 2048:3072]
        o_ap = w["proj_w"]
    else:
        q_ap = w["Wq"]
        k_ap = w["Wk"]
        v_ap = w["Wv"]
        o_ap = w["Wo"]

    def load_w(ap, wait_st):
        return em.dma("p", WB1[:, :].rearrange("p (a c) -> p a c", a=ND),
                      ap.rearrange("(a p) c -> p a c", p=128),
                      waits=[(cx.st_, wait_st)], inc=sdma)

    tok_small = 0
    if inner:
        tok_small = em.dma("p", bufs["CW"][:, :].rearrange("p (a c) -> p a c", a=ND),
                           w["conv_w"].rearrange("(a p) one c -> p a (one c)", p=128),
                           waits=list(start_tok), inc=sdma2)
        tok_small = em.dma("p", bufs["LN1W"][0:1, :], w["ln1_w"].unsqueeze(0),
                           inc=sdma2)
        tok_small = em.dma("p", bufs["LN1B"][0:1, :], w["ln1_b"].unsqueeze(0),
                           inc=sdma2)
        reps = [("LN1W", "LN1WR"), ("LN1B", "LN1BR")]
    else:
        for a in range(ND):
            tok_small = em.dma("p", bufs["BQ"][:, a:a + 1],
                               w["bq"][a * 128:(a + 1) * 128].unsqueeze(1),
                               waits=list(start_tok) if a == 0 else (),
                               inc=sdma2)
            tok_small = em.dma("p", bufs["BK"][:, a:a + 1],
                               w["bk"][a * 128:(a + 1) * 128].unsqueeze(1),
                               inc=sdma2)
        tok_small = em.dma("p", bufs["BVR"][0:1, :], w["bv"].unsqueeze(0),
                           inc=sdma2)
        tok_small = em.dma("p", bufs["BOR"][0:1, :], w["bo"].unsqueeze(0),
                           inc=sdma2)
        reps = [("BVR", "BVRR"), ("BOR", "BORR")]
    tprev = 0
    for (src, dst) in reps:
        for oc in range(2):
            tt = em.emit("t", lambda t, src=src, oc=oc: t.matmul(
                PS_X[:, 0:512], cs("c_ones_row")[0:1, :],
                bufs[src][0:1, oc * 512:(oc + 1) * 512], start=True, stop=True),
                waits=[(sdma2, tok_small), (cx.sqc, cx.tok_consts),
                       (cx.ss, tprev)] + list(start_tok), inc=cx.st_)
            tprev = em.emit("s", lambda s, dst=dst, oc=oc: s.copy(
                bufs[dst][:, oc * 512:(oc + 1) * 512], PS_X[:, 0:512]),
                waits=[(cx.st_, tt)], inc=cx.ss)
    tok_rep = tprev

    tpad = 0
    for kd in range(ND):
        tpad = em.emit("v", lambda v, kd=kd: v.memset(
            HCT[:, kd * TP:kd * TP + 2], 0.0), waits=list(start_tok),
            inc=cx.sv)
    te_h = tok_rep
    tln_v = tpad
    for st in range(NT):
        tx = em.dma("q", XT[:, 0:1024], xres_hbm.ap()[st * 128:(st + 1) * 128, :],
                    waits=[(cx.sv, tln_v)] + list(start_tok), inc=sxa)
        if inner:
            tln_v, _ = emit_ln_tile(cx, XT[:, 0:1024], XT2[:, 0:1024], scr,
                                    waits=[(sxa, tx)])
            tln_v = emit_ln_affine_tile(cx, XT2[:, 0:1024], Htmp[:, 0:D], scr,
                                        bufs["LN1WR"], bufs["LN1BR"],
                                        waits=[(cx.sv, tln_v),
                                               (cx.ss, tok_rep)])
        else:
            tln_v, _ = emit_ln_tile(cx, XT[:, 0:1024], Htmp[:, 0:D], scr,
                                    waits=[(sxa, tx)])
        for kd in range(ND):
            te_h = th.transpose(Htmp[:, kd * 128:(kd + 1) * 128],
                                HCT[:, kd * TP + 2 + st * 128:
                                    kd * TP + 2 + (st + 1) * 128],
                                waits=[(cx.sv, tln_v), (cx.sv, tpad)])
    if inner:
        tcb = 0
        for kd in range(ND):
            w2 = bufs["CW"][:, kd * 3 + 2:kd * 3 + 3]
            w1 = bufs["CW"][:, kd * 3 + 1:kd * 3 + 2]
            w0 = bufs["CW"][:, kd * 3 + 0:kd * 3 + 1]
            tv = em.emit("v", lambda v, kd=kd, w2=w2: v.tensor_scalar(
                CVO[:, 0:T], HCT[:, kd * TP + 2:kd * TP + 2 + T], w2, None,
                op0=ALU.mult),
                waits=[(cx.ss, te_h), (sdma2, tok_small), (cx.sv, tcb)],
                inc=cx.sv)
            tv = em.emit("v", lambda v, kd=kd, w1=w1: v.scalar_tensor_tensor(
                CVO[:, 0:T], HCT[:, kd * TP + 1:kd * TP + 1 + T], w1,
                CVO[:, 0:T], op0=ALU.mult, op1=ALU.add),
                waits=[(cx.sv, tv)], inc=cx.sv)
            tv = em.emit("v", lambda v, kd=kd, w0=w0: v.scalar_tensor_tensor(
                CVO[:, 0:T], HCT[:, kd * TP + 0:kd * TP + 0 + T], w0,
                CVO[:, 0:T], op0=ALU.mult, op1=ALU.add),
                waits=[(cx.sv, tv)], inc=cx.sv)
            tcb = em.emit("v", lambda v, kd=kd: v.tensor_copy(
                HCT[:, kd * TP + 2:kd * TP + 2 + T], CVO[:, 0:T]),
                waits=[(cx.sv, tv)], inc=cx.sv)
        te_h = em.emit("s", lambda s: s.copy(ZROW[0:1, 0:1], CVO[0:1, 0:1]),
                       waits=[(cx.sv, tcb)], inc=cx.ss)

    rbA = [PS_X, PS_PV]
    rbB = [PS_S[0], PS_S[1]]
    prevA = [0, 0]
    prevB = [0, 0]

    def rope_pass(dst, dst_off, bias_col, ft, ch, prev_v, wtok, slot):
        i = slot % 2
        psA, psB = rbA[i], rbB[i]
        T0i = T0[:, 0:512] if i == 0 else PT[:, 0:512]
        TSWi = TSW[:, 0:512] if i == 0 else PT[:, 512:1024]
        pe = 0
        for kd in range(ND):
            pe = em.emit("t", lambda t, kd=kd, ft=ft, ch=ch, psA=psA: t.matmul(
                psA[:, 0:512],
                WB1[:, kd * 1024 + ft * 128:kd * 1024 + (ft + 1) * 128],
                HCT[:, kd * TP + 2 + ch * 512:kd * TP + 2 + (ch + 1) * 512],
                start=(kd == 0), stop=(kd == ND - 1)),
                waits=([(cx.ss, prevA[i]), (cx.ss, te_h), (cx.sv, prev_v),
                        (sdma, wtok)] if kd == 0 else ()),
                inc=cx.st_ if kd == ND - 1 else None)
        if bias_col is not None:
            t0 = em.emit("s", lambda s, ft=ft, bias_col=bias_col, psA=psA, T0i=T0i: s.activation(
                T0i, psA[:, 0:512], AF.Identity,
                bias=bias_col[:, ft:ft + 1]),
                waits=[(cx.st_, pe)], inc=cx.ss)
        else:
            t0 = em.emit("s", lambda s, psA=psA, T0i=T0i: s.copy(
                T0i, psA[:, 0:512]), waits=[(cx.st_, pe)], inc=cx.ss)
        prevA[i] = t0
        tt = em.emit("t", lambda t, psB=psB, T0i=T0i: t.matmul(
            psB[:, 0:512], cs("c_rot128")[:, :], T0i,
            start=True, stop=True),
            waits=[(cx.ss, t0), (cx.ss, prevB[i])], inc=cx.st_)
        tsw = em.emit("s", lambda s, psB=psB, TSWi=TSWi: s.copy(
            TSWi, psB[:, 0:512]), waits=[(cx.st_, tt)], inc=cx.ss)
        prevB[i] = tsw
        tv = em.emit("v", lambda v, ch=ch, T0i=T0i: v.tensor_tensor(
            T0i, T0i, cs("c_cos2")[:, ch * 512:(ch + 1) * 512],
            op=ALU.mult), waits=[(cx.ss, tsw)], inc=cx.sv)
        tv2 = em.emit("v", lambda v, ch=ch, TSWi=TSWi: v.tensor_tensor(
            TSWi, TSWi,
            cs("c_sinrot")[:, ch * 512:(ch + 1) * 512],
            op=ALU.mult), inc=cx.sv)
        tv3 = em.emit("v", lambda v, dst=dst, dst_off=dst_off, T0i=T0i, TSWi=TSWi: v.tensor_tensor(
            dst[:, dst_off:dst_off + 512], T0i, TSWi,
            op=ALU.add), waits=[(cx.sv, tv2), (cx.sv, tv)], inc=cx.sv)
        return tv3

    def norms_max(src, src_stride, out_base, first_waits, nch=4):
        tred = 0
        tmm = 0
        tsq = 0
        for ch in range(nch):
            for kd in range(ND):
                wq_ = [(cx.st_, tmm), (cx.ss, tsq)]
                if ch == 0 and kd == 0:
                    wq_ += list(first_waits)
                tsq = em.emit("s", lambda s, kd=kd, ch=ch, src=src, ss_=src_stride: s.activation(
                    SQB[:, 0:512],
                    src[:, kd * ss_ + ch * 512:kd * ss_ + ch * 512 + 512],
                    AF.Square), waits=wq_, inc=cx.ss)
                for rb in (0, 64):
                    h = 2 * kd + rb // 64
                    tmm = em.emit("t", lambda t, rb=rb: t.matmul(
                        PS_PV[0:1, 0:512], cs("c_ones_col")[rb:rb + 64, 0:1],
                        SQB[rb:rb + 64, 0:512], start=True, stop=True),
                        waits=[(cx.ss, tsq), (cx.sv, tred)], inc=cx.st_)
                    tred = em.emit("v", lambda v, h=h, ch=ch, nch=nch: v.reduce_max(
                        ZC[0:1, h * nch + ch:h * nch + ch + 1],
                        PS_PV[0:1, 0:512],
                        axis=AX.X), waits=[(cx.st_, tmm)], inc=cx.sv)
        for h in range(NH):
            tred = em.emit("v", lambda v, h=h, ob=out_base, nch=nch: v.reduce_max(
                ZC[0:1, ob + h:ob + h + 1], ZC[0:1, h * nch:h * nch + nch],
                axis=AX.X), waits=[(cx.sv, tred)], inc=cx.sv)
        return tred

    tw = load_w(k_ap, 0)
    tok_rope_v = 0
    prevV = [0, 0]
    slot = 0
    for ft in range(ND):
        for ch in range(4):
            i = slot % 2
            tok_rope_v = rope_pass(
                KTR, ft * T + ch * 512,
                None if inner else bufs["BK"], ft, ch, prevV[i], tw, slot)
            prevV[i] = tok_rope_v
            slot += 1
    tok_ps = em.token(cx.ss)
    tok_kn = norms_max(KTR, T, 64, [(cx.sv, tok_rope_v), (cx.ss, tok_ps)])

    tvm = em.emit("v", lambda v: v.memset(VAUG[:, :], 1.0), inc=cx.sv)
    tw = load_w(v_ap, em.token(cx.st_))
    tvb = [em.token(cx.ss), 0]
    pv_first = True
    for st in range(NT):
        for oc in range(2):
            i = (st * 2 + oc) % 2
            bank = [PS_X, PS_PV][i]
            wv0 = [(sdma, tw), (cx.ss, tvb[i])]
            if i == 1 and pv_first:
                wv0.append((cx.sv, tok_kn))
                pv_first = False
            for kd in range(ND):
                em.emit("t", lambda t, kd=kd, st=st, oc=oc, bank=bank: t.matmul(
                    bank[:, 0:512],
                    HCT[:, kd * TP + 2 + st * 128:kd * TP + 2 + (st + 1) * 128],
                    WB1[:, kd * 1024 + oc * 512:kd * 1024 + (oc + 1) * 512],
                    start=(kd == 0), stop=(inner and kd == ND - 1)),
                    waits=(wv0 if kd == 0 else ()),
                    inc=cx.st_ if (inner and kd == ND - 1) else None)
            if not inner:
                em.emit("t", lambda t, oc=oc, bank=bank: t.matmul(
                    bank[:, 0:512], cs("c_ones_row")[0:1, :],
                    bufs["BVRR"][0:1, oc * 512:(oc + 1) * 512],
                    start=False, stop=True), inc=cx.st_)
            pe = em.token(cx.st_)
            tvb[i] = em.emit("s", lambda s, st=st, oc=oc, bank=bank: s.copy(
                VAUG[:, st * 1040:(st + 1) * 1040].rearrange(
                    "p (h c) -> p h c", c=65)[:, oc * 8:(oc + 1) * 8, 0:64],
                bank[:, 0:512].rearrange("p (h c) -> p h c", c=64)),
                waits=[(cx.st_, pe), (cx.sv, tvm)], inc=cx.ss)
    t_vevac = max(tvb)
    prevA[0] = prevA[1] = em.token(cx.ss)
    prevB[0] = prevB[1] = em.token(cx.ss)
    prevV[0] = prevV[1] = em.token(cx.sv)

    mmask = cs("c_mmask")
    tok_q_v = 0
    tok_ps = t_vevac
    t_exp = t_vevac
    t_ptg = [0, 0]
    t_pvevac = t_vevac
    t_yn = 0
    tok_xe = tln_v if inner else 0
    t_store = 0
    tw_q = load_w(q_ap, em.token(cx.st_))
    for qc in range(4):
        for ft in range(ND):
            i = slot % 2
            tok_q_v = rope_pass(QTR, ft * 512,
                                None if inner else bufs["BQ"],
                                ft, qc, prevV[i], tw_q, slot)
            prevV[i] = tok_q_v
            slot += 1
        tok_ps = em.token(cx.ss)
        t_qn = norms_max(QTR, 512, 80, [(cx.sv, tok_q_v), (cx.ss, tok_ps)],
                         nch=1)
        tb1 = em.emit("v", lambda v: v.tensor_tensor(
            ZC[0:1, 96:112], ZC[0:1, 64:80], ZC[0:1, 80:96], op=ALU.mult),
            waits=[(cx.sv, t_qn), (cx.sv, tok_kn)], inc=cx.sv)
        tb2 = em.emit("s", lambda s: s.activation(ZC[0:1, 112:128],
                                                  ZC[0:1, 96:112], AF.Sqrt),
                      waits=[(cx.sv, tb1)], inc=cx.ss)
        tb2 = em.emit("v", lambda v: v.tensor_scalar(
            ZC[0:1, 112:128], ZC[0:1, 112:128], -0.125, 12.0,
            op0=ALU.mult, op1=ALU.add), waits=[(cx.ss, tb2)], inc=cx.sv)
        tb3 = em.emit("t", lambda t: t.matmul(
            PS_X[0:128, 0:16], cs("c_ones_row32")[0:1, :], ZC[0:1, 112:128],
            start=True, stop=True),
            waits=[(cx.sv, tb2), (cx.ss, tok_ps)], inc=cx.st_)
        t_bc = em.emit("v", lambda v: v.tensor_copy(BCOL[:, 0:16],
                                                    PS_X[0:128, 0:16]),
                       waits=[(cx.st_, tb3)], inc=cx.sv)
        tok_ps_v = t_bc
        nkt = 4 * qc + 4
        for h in range(NH):
            ftile = h // 2
            rb = 64 * (h % 2)
            for kt in range(nkt):
                par = kt % 2
                psS = PS_S[par]
                wssc = [(cx.ss, t_exp)]
                if kt < 2:
                    wssc += [(cx.sv, tok_q_v), (cx.sv, t_bc), (cx.sv, tok_kn)]
                tsc = em.emit("t", lambda t, kt=kt, ftile=ftile, rb=rb, psS=psS: t.matmul(
                    psS[:, 0:512],
                    KTR[rb:rb + 64, ftile * T + kt * 128:ftile * T + (kt + 1) * 128],
                    QTR[rb:rb + 64, ftile * 512:(ftile + 1) * 512],
                    start=True, stop=True),
                    waits=wssc, inc=cx.st_)
                wex = [(cx.st_, tsc)]
                if t_ptg[par]:
                    wex.append((cx.st_, t_ptg[par]))
                t_exp = em.emit("s", lambda s, h=h, par=par, psS=psS: s.activation(
                    PT[:, par * 512:(par + 1) * 512], psS[:, 0:512],
                    AF.Exp, bias=BCOL[:, h:h + 1], scale=0.125),
                    waits=wex, inc=cx.ss)
                tlast, teng = t_exp, "s"
                if kt >= 4 * qc:
                    j = kt - 4 * qc
                    tlast = em.emit("v", lambda v, par=par, j=j: v.tensor_tensor(
                        PT[:, par * 512:(par + 1) * 512],
                        PT[:, par * 512:(par + 1) * 512],
                        mmask[:, j * 512:(j + 1) * 512], op=ALU.mult),
                        waits=[(cx.ss, t_exp)], inc=cx.sv)
                    teng = "v"
                wpv = [((cx.sv if teng == "v" else cx.ss), tlast)]
                if kt == 0:
                    wpv.append((cx.ss, t_pvevac))
                tpv = em.emit("t", lambda t, h=h, kt=kt, par=par, nkt=nkt: t.matmul(
                    PS_PV[0:65, 0:512],
                    VAUG[:, kt * 1040 + h * 65:kt * 1040 + (h + 1) * 65],
                    PT[:, par * 512:(par + 1) * 512],
                    start=(kt == 0), stop=(kt == nkt - 1)),
                    waits=wpv, inc=cx.st_)
                t_ptg[par] = tpv
            t_y = em.emit("s", lambda s, h=h: s.copy(
                YT[0:64, h * 512:(h + 1) * 512], PS_PV[0:64, 0:512]),
                waits=[(cx.st_, t_ptg[(nkt - 1) % 2]), (cx.sv, t_yn)],
                inc=cx.ss)
            t_z = em.emit("s", lambda s: s.copy(ZROW[64:65, 0:512],
                                                PS_PV[64:65, 0:512]),
                          inc=cx.ss)
            t_pvevac = t_z
            t_z = em.emit("s", lambda s: s.activation(
                ZROW[64:65, 0:512], ZROW[64:65, 0:512], AF.Ln),
                waits=[(cx.ss, t_z)], inc=cx.ss)
            t_z = em.emit("s", lambda s: s.activation(
                ZROW[64:65, 0:512], ZROW[64:65, 0:512], AF.Exp, scale=-1.0),
                waits=[(cx.ss, t_z)], inc=cx.ss)
            t_ze = em.emit("t", lambda t: t.matmul(
                PS_X[0:64, 0:512], cs("c_ones65")[64:65, 0:64],
                ZROW[64:65, 0:512], start=True, stop=True),
                waits=[(cx.ss, t_z), (cx.ss, tok_ps), (cx.sv, tok_ps_v)],
                inc=cx.st_)
            t_rzx = em.emit("s", lambda s: s.copy(RZX[0:64, 0:512],
                                                  PS_X[0:64, 0:512]),
                            waits=[(cx.st_, t_ze)], inc=cx.ss)
            tok_ps = t_rzx
            t_yn = em.emit("v", lambda v, h=h: v.tensor_tensor(
                YT[0:64, h * 512:(h + 1) * 512],
                YT[0:64, h * 512:(h + 1) * 512], RZX[0:64, 0:512],
                op=ALU.mult), waits=[(cx.ss, t_rzx)], inc=cx.sv)
        for tt2 in range(2):
            for half in range(2):
                tw_o = em.dma("p", WBO[0:64, :].rearrange("p (a c) -> p a c", a=8),
                              o_ap.rearrange("(a p) c -> p a c", p=64)[:, 8 * half:8 * (half + 1), :]
                              if False else
                              o_ap.rearrange("(hh p) c -> p hh c", p=64)[:, 8 * half:8 * (half + 1), :],
                              waits=[(cx.st_, em.token(cx.st_))], inc=sdo)
                for i in range(2):
                    for oc in range(2):
                        bank = [cx.ps["PS_G"], cx.ps["PS_U"],
                                cx.ps["l0f_psd0"], cx.ps["l0f_psd1"]][i * 2 + oc]
                        for hh in range(8):
                            h = half * 8 + hh
                            em.emit("t", lambda t, h=h, hh=hh, tt2=tt2, i=i, oc=oc, bank=bank, half=half: t.matmul(
                                bank[:, 0:512],
                                YT[0:64, h * 512 + (tt2 * 2 + i) * 128:h * 512 + (tt2 * 2 + i + 1) * 128],
                                WBO[0:64, hh * 1024 + oc * 512:hh * 1024 + (oc + 1) * 512],
                                start=(h == 0), stop=(h == NH - 1)),
                                waits=([(cx.sv, t_yn), (sdo, tw_o),
                                        (cx.ss, em.token(cx.ss)),
                                        (cx.sv, tok_xe)] if hh == 0 else ()),
                                inc=cx.st_ if hh == 7 else None)
            pw = em.token(cx.st_)
            for i in range(2):
                stt = qc * 4 + tt2 * 2 + i
                tx2 = em.dma("q", XT2[:, 0:1024],
                             xres_hbm.ap()[stt * 128:(stt + 1) * 128, :],
                             waits=[(cx.sxb_guard, t_store)] if False else
                                   [(sxb, t_store), (cx.sv, tok_xe)],
                             inc=sxb)
                if inner:
                    tvl, _ = emit_ln_tile(cx, XT2[:, 0:1024], XT[:, 0:1024],
                                          scr, waits=[(sxb, tx2)])
                    for oc in range(2):
                        bank = [cx.ps["PS_G"], cx.ps["PS_U"],
                                cx.ps["l0f_psd0"], cx.ps["l0f_psd1"]][i * 2 + oc]
                        tok_xe = em.emit("v", lambda v, oc=oc, bank=bank: v.tensor_tensor(
                            bufs["XIN"][:, oc * 512:(oc + 1) * 512],
                            XT[:, oc * 512:(oc + 1) * 512], bank[:, 0:512],
                            op=ALU.add),
                            waits=[(cx.st_, pw), (cx.sv, tvl)], inc=cx.sv)
                    t_store = em.dma("q",
                                     hbm_xinner.ap()[stt * 128:(stt + 1) * 128, :],
                                     bufs["XIN"][:, 0:1024],
                                     waits=[(cx.sv, tok_xe)], inc=sxb)
                else:
                    for oc in range(2):
                        bank = [cx.ps["PS_G"], cx.ps["PS_U"],
                                cx.ps["l0f_psd0"], cx.ps["l0f_psd1"]][i * 2 + oc]
                        tok_xe = em.emit("v", lambda v, oc=oc, bank=bank: v.scalar_tensor_tensor(
                            XT2[:, oc * 512:(oc + 1) * 512], bank[:, 0:512],
                            1.0, XT2[:, oc * 512:(oc + 1) * 512],
                            op0=ALU.mult, op1=ALU.add),
                            waits=[(sxb, tx2), (cx.st_, pw),
                                   (cx.sv, tok_xe)], inc=cx.sv)
                    tok_xe = em.emit("v", lambda v: v.tensor_tensor(
                        XT2[:, 0:1024], XT2[:, 0:1024], bufs["BORR"][:, 0:1024],
                        op=ALU.add), waits=[(cx.sv, tok_xe)], inc=cx.sv)
                    t_store = em.dma("q",
                                     (xres_out if xres_out is not None
                                      else xres_hbm).ap()[
                                         stt * 128:(stt + 1) * 128, :],
                                     XT2[:, 0:1024],
                                     waits=[(cx.sv, tok_xe)], inc=sxb)
        prevA[0] = prevA[1] = em.token(cx.ss)
        prevB[0] = prevB[1] = em.token(cx.ss)
        prevV[0] = prevV[1] = em.token(cx.sv)
        tok_q_v = tok_xe
    em.emit("q", lambda q: None, waits=[(sxb, em.token(sxb))])
    return tok_xe



def stage_moe(cx, th, xres_hbm, xinner_hbm, bufs, w, tag, start_tok=(),
              xres_out=None):
    em = cx.em
    cs = cx.cst
    HCT = bufs["HCT"]
    Htmp = bufs["HTMP"]
    XT = bufs["XT"]
    XT2 = bufs["XT2"]
    ZBUF = bufs["ZBUF"]
    WVG = bufs["WVG"]
    WVV = bufs["WVV"]
    PROJB = bufs["PROJB"]
    YAC = bufs["YAC"]
    FE = bufs["FE"]
    RWB = bufs["RWB"]
    SGT = bufs["SGT"]
    scr = ln_scratch(cx, "z")
    smx = em.sem("smx_" + tag)
    smo = em.sem("smo_" + tag)
    smr = em.sem("smr_" + tag)
    swv = em.sem("swv_" + tag)
    spj = em.sem("spj_" + tag)
    PS_G = cx.ps["PS_G"]
    PS_U = cx.ps["PS_U"]
    YPS = [cx.ps["l0f_psd0"], cx.ps["l0f_psd1"]]

    def wv_ap(e, lo):
        return w["expert_wv"][e:e + 1, :, lo:lo + 1024].rearrange(
            "one (a p) n -> p (one a) n", p=128)

    def pj_ap(e, g):
        return w["expert_proj"][e:e + 1, g * 1024:(g + 1) * 1024, :].rearrange(
            "one (a p) c -> p (one a) c", p=128)

    tok_small = em.dma("p", bufs["LN2W"][0:1, :], w["ln2_w"].unsqueeze(0),
                       waits=list(start_tok), inc=smr)
    tok_small = em.dma("p", bufs["LN2B"][0:1, :], w["ln2_b"].unsqueeze(0),
                       inc=smr)
    tok_small = em.dma("p", RWB[:, :].rearrange("p (a e) -> p a e", a=ND),
                       w["router_w"].rearrange("(a p) e -> p a e", p=128),
                       inc=smr)
    tprev = 0
    for (src, dst) in (("LN2W", "LN2WR"), ("LN2B", "LN2BR")):
        for oc in range(2):
            tt = em.emit("t", lambda t, src=src, oc=oc: t.matmul(
                PS_G[:, 0:512], cs("c_ones_row")[0:1, :],
                bufs[src][0:1, oc * 512:(oc + 1) * 512], start=True, stop=True),
                waits=[(smr, tok_small), (cx.sqc, cx.tok_consts),
                       (cx.ss, tprev)] + list(start_tok), inc=cx.st_)
            tprev = em.emit("s", lambda s, dst=dst, oc=oc: s.copy(
                bufs[dst][:, oc * 512:(oc + 1) * 512], PS_G[:, 0:512]),
                waits=[(cx.st_, tt)], inc=cx.ss)
    tok_rep = tprev

    tln_v = 0
    te_h = tok_rep
    for st in range(NT):
        tx = em.dma("q", XT[:, 0:1024],
                    xinner_hbm.ap()[st * 128:(st + 1) * 128, :],
                    waits=[(cx.sv, tln_v)] + list(start_tok), inc=smx)
        tln_v = emit_ln_affine_tile(cx, XT[:, 0:1024], Htmp[:, 0:D], scr,
                                    bufs["LN2WR"], bufs["LN2BR"],
                                    waits=[(smx, tx), (cx.ss, tok_rep)])
        for kd in range(ND):
            te_h = th.transpose(Htmp[:, kd * 128:(kd + 1) * 128],
                                HCT[:, kd * TP + 2 + st * 128:
                                    kd * TP + 2 + (st + 1) * 128],
                                waits=[(cx.sv, tln_v)])

    tfe = 0
    tmr = 0
    for st in range(NT):
        for kd in range(ND):
            tmr = em.emit("t", lambda t, st=st, kd=kd: t.matmul(
                PS_G[:, 0:4],
                HCT[:, kd * TP + 2 + st * 128:kd * TP + 2 + (st + 1) * 128],
                RWB[:, kd * 4:(kd + 1) * 4],
                start=(kd == 0), stop=(kd == ND - 1)),
                waits=([(cx.ss, te_h), (smr, tok_small), (cx.sv, tfe)]
                       if kd == 0 else ()),
                inc=cx.st_ if kd == ND - 1 else None)
        tfe = em.emit("v", lambda v: v.reduce_max(
            FE[:, 64:65], PS_G[:, 0:4], axis=AX.X),
            waits=[(cx.st_, tmr)], inc=cx.sv)
        for e in range(4):
            tfe = em.emit("v", lambda v, st=st, e=e: v.tensor_tensor(
                FE[:, st * 4 + e:st * 4 + e + 1], PS_G[:, e:e + 1],
                FE[:, 64:65], op=ALU.is_equal),
                waits=[(cx.sv, tfe)], inc=cx.sv)
        tfe = em.emit("v", lambda v, st=st: v.tensor_scalar(
            FE[:, 65:66], FE[:, st * 4:st * 4 + 1], -1.0, 1.0,
            op0=ALU.mult, op1=ALU.add), waits=[(cx.sv, tfe)], inc=cx.sv)
        for e in (1, 2, 3):
            tfe = em.emit("v", lambda v, st=st, e=e: v.tensor_tensor(
                FE[:, st * 4 + e:st * 4 + e + 1],
                FE[:, st * 4 + e:st * 4 + e + 1], FE[:, 65:66], op=ALU.mult),
                waits=[(cx.sv, tfe)], inc=cx.sv)
            if e < 3:
                tfe = em.emit("v", lambda v, st=st, e=e: v.tensor_tensor(
                    FE[:, 65:66], FE[:, 65:66],
                    FE[:, st * 4 + e:st * 4 + e + 1], op=ALU.subtract),
                    waits=[(cx.sv, tfe)], inc=cx.sv)

    tsil = 0
    tz = 0
    typs = [0, 0]
    tgv_done = 0
    tpm_prev = 0
    for e in range(4):
        for g in range(4):
            w0 = ([(cx.sv, tfe), (cx.ss, em.token(cx.ss))]
                  if (e == 0 and g == 0) else [])
            twv = em.dma("p", WVG[:, :].rearrange("p (a n) -> p a n", a=ND),
                         wv_ap(e, g * 1024),
                         waits=[(cx.st_, tgv_done)] + w0, inc=swv)
            twv = em.dma("p", WVV[:, :].rearrange("p (a n) -> p a n", a=ND),
                         wv_ap(e, 4096 + g * 1024), inc=swv)
            tpj = em.dma("p", PROJB[:, :].rearrange("p (a c) -> p a c", a=ND),
                         pj_ap(e, g),
                         waits=[(cx.st_, tpm_prev)] + w0, inc=spj)
            for jt in range(8):
                for ch in range(4):
                    for kd in range(ND):
                        em.emit("t", lambda t, jt=jt, ch=ch, kd=kd: t.matmul(
                            PS_G[:, 0:512],
                            WVG[:, kd * 1024 + jt * 128:kd * 1024 + (jt + 1) * 128],
                            HCT[:, kd * TP + 2 + ch * 512:kd * TP + 2 + (ch + 1) * 512],
                            start=(kd == 0), stop=(kd == ND - 1)),
                            waits=([(swv, twv), (cx.ss, tsil), (cx.sv, tfe)]
                                   if kd == 0 else ()),
                            inc=cx.st_ if kd == ND - 1 else None)
                    tgm = em.token(cx.st_)
                    for kd in range(ND):
                        em.emit("t", lambda t, jt=jt, ch=ch, kd=kd: t.matmul(
                            PS_U[:, 0:512],
                            WVV[:, kd * 1024 + jt * 128:kd * 1024 + (jt + 1) * 128],
                            HCT[:, kd * TP + 2 + ch * 512:kd * TP + 2 + (ch + 1) * 512],
                            start=(kd == 0), stop=(kd == ND - 1)),
                            waits=([(swv, twv), (cx.sv, tz)] if kd == 0 else ()),
                            inc=cx.st_ if kd == ND - 1 else None)
                    tvm = em.token(cx.st_)
                    tsil = em.emit("s", lambda s: s.activation(
                        SGT[:, 0:512], PS_G[:, 0:512], AF.Silu),
                        waits=[(cx.st_, tgm), (cx.sv, tz)], inc=cx.ss)
                    tz = em.emit("v", lambda v, jt=jt, ch=ch: v.tensor_tensor(
                        ZBUF[:, jt * 2048 + ch * 512:jt * 2048 + (ch + 1) * 512],
                        SGT[:, 0:512], PS_U[:, 0:512], op=ALU.mult),
                        waits=[(cx.ss, tsil), (cx.st_, tvm)], inc=cx.sv)
            tgv_done = em.token(cx.st_)
            for st in range(NT):
                for oc in range(2):
                    b = oc
                    for jt in range(8):
                        em.emit("t", lambda t, jt=jt, st=st, oc=oc, b=b: t.matmul(
                            YPS[b][:, 0:512],
                            ZBUF[:, jt * 2048 + st * 128:jt * 2048 + (st + 1) * 128],
                            PROJB[:, jt * 1024 + oc * 512:jt * 1024 + (oc + 1) * 512],
                            start=(jt == 0), stop=(jt == ND - 1)),
                            waits=([(cx.sv, tz), (spj, tpj), (cx.sv, typs[b])]
                                   if jt == 0 else ()),
                            inc=cx.st_ if jt == ND - 1 else None)
                    tpm = em.token(cx.st_)
                    if e == 0 and g == 0:
                        typs[b] = em.emit("v", lambda v, st=st, oc=oc, e=e, b=b: v.tensor_scalar(
                            YAC[:, st * 1024 + oc * 512:st * 1024 + (oc + 1) * 512],
                            YPS[b][:, 0:512], FE[:, st * 4 + e:st * 4 + e + 1],
                            None, op0=ALU.mult),
                            waits=[(cx.st_, tpm)], inc=cx.sv)
                    else:
                        typs[b] = em.emit("v", lambda v, st=st, oc=oc, e=e, b=b: v.scalar_tensor_tensor(
                            YAC[:, st * 1024 + oc * 512:st * 1024 + (oc + 1) * 512],
                            YPS[b][:, 0:512], FE[:, st * 4 + e:st * 4 + e + 1],
                            YAC[:, st * 1024 + oc * 512:st * 1024 + (oc + 1) * 512],
                            op0=ALU.mult, op1=ALU.add),
                            waits=[(cx.st_, tpm)], inc=cx.sv)
            tpm_prev = em.token(cx.st_)

    tst = 0
    tx2 = 0
    for st in range(NT):
        tX1 = em.dma("q", XT[:, 0:1024],
                     xinner_hbm.ap()[st * 128:(st + 1) * 128, :],
                     waits=([(cx.sv, em.token(cx.sv))] if st == 0 else
                            [(cx.sv, tx2)]), inc=smx)
        tXR = em.dma("q", XT2[:, 0:1024],
                     xres_hbm.ap()[st * 128:(st + 1) * 128, :],
                     waits=[(smo, tst)] + ([(cx.sv, em.token(cx.sv))]
                                           if st == 0 else []), inc=smo)
        tx2 = em.emit("v", lambda v: v.scalar_tensor_tensor(
            XT2[:, 0:1024], XT[:, 0:1024], 0.5, XT2[:, 0:1024],
            op0=ALU.mult, op1=ALU.add),
            waits=[(smx, tX1), (smo, tXR)], inc=cx.sv)
        tx2 = em.emit("v", lambda v, st=st: v.scalar_tensor_tensor(
            XT2[:, 0:1024], YAC[:, st * 1024:(st + 1) * 1024], 0.5,
            XT2[:, 0:1024], op0=ALU.mult, op1=ALU.add),
            waits=[(cx.sv, tx2)], inc=cx.sv)
        tst = em.dma("q", (xres_out if xres_out is not None
                           else xres_hbm).ap()[st * 128:(st + 1) * 128, :],
                     XT2[:, 0:1024], waits=[(cx.sv, tx2)], inc=smo)
    em.emit("q", lambda q: None, waits=[(smo, em.token(smo))])
    return {"smo": (smo, tst), "sv": tx2}



II = 2816


def _attn_bufs(cx, inner):
    bufs = {
        "HCT": cx.sbuf("a_hct", [128, ND * TP], BF16),
        "KTR": cx.sbuf("a_ktr", [128, ND * T], BF16),
        "VAUG": cx.sbuf("a_vaug", [128, 16 * 1040], BF16),
        "QTR": cx.sbuf("a_qtr", [128, ND * 512], BF16),
        "T0": cx.sbuf("a_t0", [128, 512], BF16),
        "TSW": cx.sbuf("a_tsw", [128, 512], BF16),
        "PT": cx.sbuf("a_pt", [128, 2 * 512], BF16),
        "YT": cx.sbuf("a_yt", [64, 16 * 512], BF16),
        "ZROW": cx.sbuf("a_zrow", [65, 512], F32),
        "ZC": cx.sbuf("a_zc", [1, 128], F32),
        "BCOL": cx.sbuf("a_bcol", [128, 16], F32),
        "WB1": cx.sbuf("a_wb1", [128, ND * 1024], BF16),
        "WBO": cx.sbuf("a_wbo", [64, 8 * 1024], BF16),
        "RZX": cx.sbuf("a_rzx", [64, 512], BF16),
        "SQB": cx.sbuf("a_sqb", [128, 512], BF16),
        "HTMP": cx.sbuf("a_htmp", [128, 1024], BF16),
        "XT": cx.sbuf("a_xt", [128, 1024], F32),
        "XT2": cx.sbuf("a_xt2", [128, 1024], F32),
    }
    if inner:
        bufs.update({
            "CVO": cx.sbuf("a_cvo", [128, T], BF16),
            "XIN": cx.sbuf("a_xin", [128, 1024], F32),
            "CW": cx.sbuf("a_cw", [128, 24], F32),
            "LN1W": cx.sbuf("a_ln1w", [1, 1024], BF16),
            "LN1B": cx.sbuf("a_ln1b", [1, 1024], BF16),
            "LN1WR": cx.sbuf("a_ln1wr", [128, 1024], BF16),
            "LN1BR": cx.sbuf("a_ln1br", [128, 1024], BF16),
        })
    else:
        bufs.update({
            "CVO": cx.sbuf("a_cvo", [128, T], BF16),
            "BQ": cx.sbuf("a_bq", [128, 8], F32),
            "BK": cx.sbuf("a_bk", [128, 8], F32),
            "BVR": cx.sbuf("a_bvr", [1, 1024], BF16),
            "BOR": cx.sbuf("a_bor", [1, 1024], BF16),
            "BVRR": cx.sbuf("a_bvrr", [128, 1024], BF16),
            "BORR": cx.sbuf("a_borr", [128, 1024], BF16),
        })
    return bufs


def build_launch_l0ffn(first):
    nc = bass.Bass("TRN2", target_bir_lowering=False)
    cx = Ctx(nc)
    em = cx.em
    x = nc.dram_tensor("x", [T, D], F32, kind="ExternalInput").ap()
    y = nc.dram_tensor("y", [T, D], F32, kind="ExternalOutput").ap()
    if first:
        wg = nc.dram_tensor("wg", [II, D], F32, kind="ExternalInput").ap()
        wu = nc.dram_tensor("wu", [II, D], F32, kind="ExternalInput").ap()
        wd = nc.dram_tensor("wd", [D, II], F32, kind="ExternalInput").ap()
        wgT = nc.dram_tensor("wgT", [D, II], BF16, kind="ExternalOutput")
        wuT = nc.dram_tensor("wuT", [D, II], BF16, kind="ExternalOutput")
        wdT = nc.dram_tensor("wdT", [II, D], BF16, kind="ExternalOutput")
    else:
        wgT = nc.dram_tensor("wgT", [D, II], BF16, kind="ExternalInput")
        wuT = nc.dram_tensor("wuT", [D, II], BF16, kind="ExternalInput")
        wdT = nc.dram_tensor("wdT", [II, D], BF16, kind="ExternalInput")

    consts = host_consts()
    skip = ()
    cx.load_consts(consts)
    prew = cx.prelude_fconsts()

    XRES = cx.sbuf("XRES", [128, NT * D], F32)
    XQT = cx.sbuf("XQT", [128, ND * T], BF16)
    PS_G = cx.psum("PS_G", [128, 512], F32)
    PS_U = cx.psum("PS_U", [128, 512], F32)
    th = TransposeHelper(cx)
    fp4s = [Fp4Helper(cx), Fp4Helper(cx, tag="fp4b")]

    xt = x.rearrange("(n p) d -> p n d", p=128)
    em.dma("q", XRES[:, :].rearrange("p (n d) -> p n d", d=D), xt, inc=cx.sqx)
    xres_tok = em.token(cx.sqx)

    bufs = alloc_l0f_bufs(cx)
    qs = QSched(fp4s)
    if first:
        units, chunk_tok = prep_units(cx, qs, th, [wg, wu, wd],
                                      [wgT, wuT, wdT])
    else:
        units, chunk_tok = None, None
    stage_l0ffn(cx, qs, th, XRES, XQT, wgT, wuT, wdT, bufs, "c1",
                xres_waits=[(cx.sqx, xres_tok)],
                units=units, chunk_tok=chunk_tok)

    yt = y.rearrange("(n p) d -> p n d", p=128)
    em.dma("q", yt, XRES[:, :].rearrange("p (n d) -> p n d", d=D),
           waits=[(cx.sv, em.token(cx.sv))], inc=cx.sout)
    em.emit("q", lambda q: None, waits=[(cx.sout, em.token(cx.sout)),
                                        (cx.spout, em.token(cx.spout))])
    cx.em.replay(prewaits=prew)
    return nc, consts, skip


def build_launch_gla():
    nc = bass.Bass("TRN2", target_bir_lowering=False)
    cx = Ctx(nc)
    em = cx.em
    x = nc.dram_tensor("x", [T, D], F32, kind="ExternalInput").ap()
    w = {}
    for name, shape in [("Wq", [D, 512]), ("Wk", [D, 512]), ("Wv", [D, D]),
                        ("Wg", [D, D]), ("Wgk1", [D, 16]), ("Wgk2", [16, 512]),
                        ("bgk2", [512]), ("norm_w", [128]), ("Wo", [D, D])]:
        w[name] = nc.dram_tensor(name, shape, F32, kind="ExternalInput").ap()
    y = nc.dram_tensor("y", [T, D], F32, kind="ExternalOutput").ap()

    consts = host_consts()
    skip = ("c_cmask", "c_cos", "c_sin", "c_mmask", "c_rot128", "c_ones65",
            "c_cos2", "c_sinrot")
    cx.load_consts(consts, skip=skip)
    prew = cx.prelude_fconsts()

    XRES = cx.sbuf("XRES", [128, NT * D], F32)
    HT = cx.sbuf("HT", [128, ND * T], BF16)
    cx.psum("PS_G", [128, 512], F32)
    cx.psum("PS_U", [128, 512], F32)
    cx.psum("l0f_psd0", [128, 512], F32)
    cx.psum("l0f_psd1", [128, 512], F32)
    th = TransposeHelper(cx)

    bufs = {
        "WQ": cx.sbuf("g_wq", [128, 8 * 512], BF16),
        "WK": cx.sbuf("g_wk", [128, 8 * 512], BF16),
        "WVG": cx.sbuf("g_wvg", [128, 8 * 1024], BF16),
        "WGK1": cx.sbuf("g_wgk1", [128, 8 * 16], BF16),
        "WGK2": cx.sbuf("g_wgk2", [16, 512], BF16),
        "BGK2": cx.sbuf("g_bgk2", [1, 512], BF16),
        "NW": cx.sbuf("g_nw", [128, 1], F32),
    }

    xt = x.rearrange("(n p) d -> p n d", p=128)
    em.dma("q", XRES[:, :].rearrange("p (n d) -> p n d", d=D), xt, inc=cx.sqx)
    xres_tok = em.token(cx.sqx)
    wtok = stage_gla_weights(cx, w, bufs)
    ht_tok = stage_ln_transpose(cx, th, XRES, HT, "g",
                                xres_waits=[(cx.sqx, xres_tok)])
    stage_gla(cx, th, XRES, HT, bufs, wtok, w, "g", ht_tok=ht_tok, phase=3)

    yt = y.rearrange("(n p) d -> p n d", p=128)
    em.dma("q", yt, XRES[:, :].rearrange("p (n d) -> p n d", d=D),
           waits=[(cx.sv, em.token(cx.sv))], inc=cx.sout)
    em.emit("q", lambda q: None, waits=[(cx.sout, em.token(cx.sout)),
                                        (cx.spout, em.token(cx.spout))])
    cx.em.replay(prewaits=prew)
    return nc, consts, skip


def build_launch_inner():
    nc = bass.Bass("TRN2", target_bir_lowering=False)
    cx = Ctx(nc)
    em = cx.em
    xres = nc.dram_tensor("xres", [T, D], F32, kind="ExternalInput")
    w = {}
    for name, shape in [("ln1_w", [D]), ("ln1_b", [D]), ("conv_w", [D, 1, 3]),
                        ("attn_w", [D, 3 * D]), ("proj_w", [D, D])]:
        w[name] = nc.dram_tensor(name, shape, F32, kind="ExternalInput").ap()
    xinner = nc.dram_tensor("xinner", [T, D], F32, kind="ExternalOutput")

    consts = host_consts()
    skip = ("c_u16", "c_triu", "c_triu2", "c_e32", "c_e32h", "c_cmask",
            "c_cos", "c_sin")
    cx.load_consts(consts, skip=skip)
    prew = cx.prelude_fconsts()

    cx.psum("PS_G", [128, 512], F32)
    cx.psum("PS_U", [128, 512], F32)
    cx.psum("l0f_psd0", [128, 512], F32)
    cx.psum("l0f_psd1", [128, 512], F32)
    th = TransposeHelper(cx)
    bufs = _attn_bufs(cx, inner=True)

    stage_attn(cx, th, xres, bufs, w, "inner", "l1i", hbm_xinner=xinner)
    em.emit("q", lambda q: None,
            waits=[(cx.em.sem("sxb_l1i"), em.token(cx.em.sem("sxb_l1i")))])
    cx.em.replay(prewaits=prew)
    return nc, consts, skip


def build_launch_mha():
    nc = bass.Bass("TRN2", target_bir_lowering=False)
    cx = Ctx(nc)
    em = cx.em
    xres = nc.dram_tensor("xres", [T, D], F32, kind="ExternalInput")
    w = {}
    for name, shape in [("Wq", [D, D]), ("bq", [D]), ("Wk", [D, D]),
                        ("bk", [D]), ("Wv", [D, D]), ("bv", [D]),
                        ("Wo", [D, D]), ("bo", [D])]:
        w[name] = nc.dram_tensor(name, shape, F32, kind="ExternalInput").ap()
    y = nc.dram_tensor("y", [T, D], F32, kind="ExternalOutput")

    consts = host_consts()
    skip = ("c_u16", "c_triu", "c_triu2", "c_e32", "c_e32h", "c_cmask",
            "c_cos", "c_sin")
    cx.load_consts(consts, skip=skip)
    prew = cx.prelude_fconsts()

    cx.psum("PS_G", [128, 512], F32)
    cx.psum("PS_U", [128, 512], F32)
    cx.psum("l0f_psd0", [128, 512], F32)
    cx.psum("l0f_psd1", [128, 512], F32)
    th = TransposeHelper(cx)
    bufs = _attn_bufs(cx, inner=False)

    stage_attn(cx, th, xres, bufs, w, "mha", "l1a", xres_out=y)
    em.emit("q", lambda q: None,
            waits=[(cx.em.sem("sxb_l1a"), em.token(cx.em.sem("sxb_l1a")))])
    cx.em.replay(prewaits=prew)
    return nc, consts, skip


def build_launch_moe():
    nc = bass.Bass("TRN2", target_bir_lowering=False)
    cx = Ctx(nc)
    em = cx.em
    xres = nc.dram_tensor("xres", [T, D], F32, kind="ExternalInput")
    x1 = nc.dram_tensor("x1", [T, D], F32, kind="ExternalInput")
    w = {}
    for name, shape in [("ln2_w", [D]), ("ln2_b", [D]), ("router_w", [D, 4]),
                        ("expert_wv", [4, D, 8192]),
                        ("expert_proj", [4, 4096, D])]:
        w[name] = nc.dram_tensor(name, shape, F32, kind="ExternalInput").ap()
    y = nc.dram_tensor("y", [T, D], F32, kind="ExternalOutput")

    consts = host_consts()
    skip = tuple(k for k in host_consts() if k not in ("c_ident",
                                                       "c_ones_row"))
    cx.load_consts(consts, skip=skip)
    prew = cx.prelude_fconsts()

    cx.psum("PS_G", [128, 512], F32)
    cx.psum("PS_U", [128, 512], F32)
    cx.psum("l0f_psd0", [128, 512], F32)
    cx.psum("l0f_psd1", [128, 512], F32)
    th = TransposeHelper(cx)

    bufs = {
        "HCT": cx.sbuf("a_hct", [128, ND * TP], BF16),
        "HTMP": cx.sbuf("a_htmp", [128, 1024], BF16),
        "XT": cx.sbuf("a_xt", [128, 1024], F32),
        "XT2": cx.sbuf("a_xt2", [128, 1024], F32),
        "ZBUF": cx.sbuf("m_zbuf", [128, 8 * 2048], BF16),
        "WVG": cx.sbuf("m_wvg", [128, 8 * 1024], BF16),
        "WVV": cx.sbuf("m_wvv", [128, 8 * 1024], BF16),
        "PROJB": cx.sbuf("m_projb", [128, 8 * 1024], BF16),
        "YAC": cx.sbuf("m_yac", [128, 16 * 1024], F32),
        "FE": cx.sbuf("m_fe", [128, 80], F32),
        "RWB": cx.sbuf("m_rwb", [128, 32], BF16),
        "SGT": cx.sbuf("m_sgt", [128, 512], BF16),
        "LN2W": cx.sbuf("m_ln2w", [1, 1024], BF16),
        "LN2B": cx.sbuf("m_ln2b", [1, 1024], BF16),
        "LN2WR": cx.sbuf("m_ln2wr", [128, 1024], BF16),
        "LN2BR": cx.sbuf("m_ln2br", [128, 1024], BF16),
    }

    r = stage_moe(cx, th, xres, x1, bufs, w, "l1m", xres_out=y)
    em.emit("q", lambda q: None, waits=[r["smo"]])
    cx.em.replay(prewaits=prew)
    return nc, consts, skip


BOUNDS = []


def emit_barrier(cx, extra=()):
    em = cx.em
    waits = [(cx.sv, em.token(cx.sv)), (cx.ss, em.token(cx.ss)),
             (cx.st_, em.token(cx.st_))] + list(extra)
    BOUNDS.append(em.token(cx.st_))
    for e in ENGS:
        em.emit(e, lambda eng: None, waits=waits)


def build_fused():
    nc = bass.Bass("TRN2", target_bir_lowering=False)
    cx = Ctx(nc)
    em = cx.em

    x = nc.dram_tensor("x", [T, D], F32, kind="ExternalInput").ap()
    y = nc.dram_tensor("y", [T, D], F32, kind="ExternalOutput")
    wnames = [
        ("L0_attn_Wq", [D, 512]), ("L0_attn_Wk", [D, 512]),
        ("L0_attn_Wv", [D, D]), ("L0_attn_Wg", [D, D]),
        ("L0_attn_Wgk1", [D, 16]), ("L0_attn_Wgk2", [16, 512]),
        ("L0_attn_bgk2", [512]), ("L0_attn_norm_w", [128]),
        ("L0_attn_Wo", [D, D]),
        ("L0_ffn_w_gate", [II, D]), ("L0_ffn_w_up", [II, D]),
        ("L0_ffn_w_down", [D, II]),
        ("L1_attn_Wq", [D, D]), ("L1_attn_bq", [D]),
        ("L1_attn_Wk", [D, D]), ("L1_attn_bk", [D]),
        ("L1_attn_Wv", [D, D]), ("L1_attn_bv", [D]),
        ("L1_attn_Wo", [D, D]), ("L1_attn_bo", [D]),
        ("L1_ffn_ln1_w", [D]), ("L1_ffn_ln1_b", [D]),
        ("L1_ffn_ln2_w", [D]), ("L1_ffn_ln2_b", [D]),
        ("L1_ffn_conv_w", [D, 1, 3]), ("L1_ffn_attn_w", [D, 3 * D]),
        ("L1_ffn_proj_w", [D, D]), ("L1_ffn_router_w", [D, 4]),
        ("L1_ffn_expert_wv", [4, D, 8192]),
        ("L1_ffn_expert_proj", [4, 4096, D]),
    ]
    win = {n: nc.dram_tensor(n, sh, F32, kind="ExternalInput").ap()
           for (n, sh) in wnames}
    wgT = nc.dram_tensor("wgT", [D, II], BF16, kind="Internal")
    wuT = nc.dram_tensor("wuT", [D, II], BF16, kind="Internal")
    wdT = nc.dram_tensor("wdT", [II, D], BF16, kind="Internal")
    xres_hbm = nc.dram_tensor("xres_hbm", [T, D], F32, kind="Internal")
    xinner = nc.dram_tensor("xinner", [T, D], F32, kind="Internal")

    consts = host_consts()
    MOVED = ("c_mmask", "c_cos2", "c_sinrot")
    skip = ("c_cmask", "c_cos", "c_sin", "c_e32", "c_e32h")
    cx.load_consts(consts, skip=skip + MOVED)
    cdram = {}
    for k in MOVED:
        arr = consts[k]
        dt = BF16 if arr.dtype == ml_dtypes.bfloat16 else F32
        cdram[k] = nc.dram_tensor(k, list(arr.shape), dt,
                                  kind="ExternalInput")

    def reload_consts(names):
        for k in names:
            t = cx.sb[k + "_sb"]
            em.dma("q", t[:, :], cdram[k].ap(), inc=cx.sqc)
        cx.tok_consts = em.token(cx.sqc)
        return (cx.sqc, cx.tok_consts)

    prew = cx.prelude_fconsts()
    ln_scratch(cx, "z")
    cx.sbuf("a_htmp", [128, 1024], BF16)

    AB = (int(nc.sbuf_base) + 127) & ~127
    R1, R2 = AB, AB + 32768
    R3 = R2 + 65536
    R3LIM = 104448
    assert R3 + R3LIM <= 229376, (AB, R3 + R3LIM)

    def _sz(sh, dt):
        return sh[-1] * (4 if dt == F32 else 2)

    def layout(base, items, limit):
        off = 0
        for (n, sh, dt) in items:
            off = (off + 63) & ~63
            cx.sbuf(n, sh, dt, at=base + off)
            off += _sz(sh, dt)
        assert off <= limit, (base, off, limit)

    for n in ("XQT", "HT", "a_ktr", "m_zbuf"):
        cx.sbuf(n, [128, 16384], BF16, at=R1)
    cx.sbuf("XRES", [128, NT * D], F32, at=R2)
    cx.sbuf("m_yac", [128, 16 * 1024], F32, at=R2)
    layout(R2, [
        ("a_yt", [64, 16 * 512], BF16), ("a_wb1", [128, ND * 1024], BF16),
        ("a_wbo", [64, 8 * 1024], BF16), ("a_cvo", [128, T], BF16),
        ("a_xin", [128, 1024], F32), ("a_t0", [128, 512], BF16),
        ("a_tsw", [128, 512], BF16), ("a_pt", [128, 2 * 512], BF16),
        ("a_zrow", [65, 512], F32), ("a_rzx", [64, 512], BF16),
        ("a_sqb", [128, 512], BF16),
    ], 65536)
    layout(R3, [
        ("l0f_g0", [128, 8 * 512], BF16), ("l0f_g1", [128, 8 * 512], BF16),
        ("l0f_u0", [128, 8 * 512], BF16), ("l0f_u1", [128, 8 * 512], BF16),
        ("l0f_d0", [128, 4 * 1024], BF16), ("l0f_d1", [128, 4 * 1024], BF16),
        ("l0f_hqt", [128, 4 * 2048], BF16),
        ("fp4_xn", [128, 1024], BF16), ("fp4_u", [128, 1024], BF16),
        ("fp4_sg", [128, 1024], BF16), ("fp4_a", [128, 1024], BF16),
        ("fp4_c", [128, 1024], BF16),
        ("l0f_h", [128, D], BF16), ("l0f_xq", [128, D], BF16),
        ("l0f_xq2", [128, D], BF16), ("l0f_xq3", [128, D], BF16),
        ("l0f_tg", [128, 512], BF16), ("l0f_tu", [128, 512], BF16),
        ("fp4_amax", [128, 64], F32), ("fp4_scf", [128, 64], F32),
        ("fp4_rs", [128, 64], F32),
                ("fp4b_xn", [128, 1024], BF16), ("fp4b_u", [128, 1024], BF16),
        ("fp4b_sg", [128, 1024], BF16), ("fp4b_a", [128, 1024], BF16),
        ("fp4b_c", [128, 1024], BF16),
        ("fp4b_amax", [128, 64], F32), ("fp4b_scf", [128, 64], F32),
        ("fp4b_rs", [128, 64], F32),
                ("prp_w0", [128, 1024], BF16), ("prp_w1", [128, 1024], BF16),
            ], R3LIM)
    layout(R3, [
        ("g_wq", [128, 8 * 512], BF16), ("g_wk", [128, 8 * 512], BF16),
        ("g_wvg", [128, 8 * 1024], BF16),
        ("gla_qt", [128, 4 * SC_TOK], BF16), ("gla_kt", [128, 4 * SC_TOK], BF16),
        ("gla_ktok", [64, 4 * 512], BF16), ("gla_vtok", [64, 4 * 1024], BF16),
        ("gla_gk", [64, 4 * 512], BF16),
        ("gla_eb", [128, 4 * 256], BF16), ("gla_emb", [128, 4 * 256], BF16),
        ("gla_qet", [128, 4 * 256], BF16), ("gla_ket", [128, 4 * 256], BF16),
        ("gla_kemb", [64, 4 * 512], BF16),
        ("gla_osc", [64, 4 * 1024], BF16), ("gla_gtok", [64, 4 * 1024], BF16),
        ("gla_ogt", [128, ND * SC_TOK], BF16),
        ("lnt_hg", [128, D], BF16),
        ("gla_t1t", [16, 256], BF16), ("gla_elast", [128, 16], F32),
        ("gla_ats", [64, 128], BF16), ("gla_otmp", [64, 256], BF16),
        ("gla_rstd", [64, 32], F32), ("gla_ssq", [64, 32], F32),
        ("gla_sqs", [64, 128], F32), ("gla_lnt", [64, 32], F32),
        ("gla_s0", [128, 128], F32), ("gla_s1", [128, 128], F32),
        ("gla_s2", [128, 128], F32), ("gla_s3", [128, 128], F32),
        ("gla_sb0", [128, 128], BF16), ("gla_sb1", [128, 128], BF16),
        ("gla_sb2", [128, 128], BF16), ("gla_sb3", [128, 128], BF16),
        ("gla_sg", [64, 512], BF16), ("gla_embt", [64, 512], BF16),
        ("g_wgk1", [128, 128], BF16), ("g_wgk2", [16, 512], BF16),
        ("g_bgk2", [1, 512], BF16), ("g_nw", [128, 1], F32),
    ], R3LIM)
    layout(R3, [
        ("a_hct", [128, ND * TP], BF16), ("a_vaug", [128, 16 * 1040], BF16),
        ("a_xt", [128, 1024], F32), ("a_xt2", [128, 1024], F32),
        ("a_qtr", [128, ND * 512], BF16),
        ("a_ln1w", [1, 1024], BF16), ("a_ln1b", [1, 1024], BF16),
        ("a_ln1wr", [128, 1024], BF16), ("a_ln1br", [128, 1024], BF16),
        ("c_mmask_sb", [128, 2048], BF16), ("c_cos2_sb", [128, 2048], BF16),
        ("c_sinrot_sb", [128, 2048], BF16),
        ("a_zc", [1, 128], F32), ("a_bcol", [128, 16], F32),
        ("a_cw", [128, 24], F32), ("a_bq", [128, 8], F32),
        ("a_bk", [128, 8], F32),
    ], R3LIM)
    cx.sbuf("a_bvr", [1, 1024], BF16, at=R3 + 82496)
    cx.sbuf("a_bor", [1, 1024], BF16, at=R3 + 84544)
    cx.sbuf("a_bvrr", [128, 1024], BF16, at=R3 + 86592)
    cx.sbuf("a_borr", [128, 1024], BF16, at=R3 + 88640)
    cx.sbuf("m_wvg", [128, 8 * 1024], BF16, at=R3 + 32832)
    cx.sbuf("m_wvv", [128, 8 * 1024], BF16, at=R3 + 49216)
    cx.sbuf("m_projb", [128, 8 * 1024], BF16, at=R3 + 66112)
    cx.sbuf("m_ln2w", [1, 1024], BF16, at=R3 + 82496)
    cx.sbuf("m_ln2b", [1, 1024], BF16, at=R3 + 84544)
    cx.sbuf("m_ln2wr", [128, 1024], BF16, at=R3 + 86592)
    cx.sbuf("m_ln2br", [128, 1024], BF16, at=R3 + 88640)
    layout(R3 + 90688, [
        ("m_sgt", [128, 512], BF16), ("m_fe", [128, 80], F32),
        ("m_rwb", [128, 32], BF16),
    ], R3LIM - 90688)
    cx.frozen = True

    PS_G = cx.psum("PS_G", [128, 512], F32)
    PS_U = cx.psum("PS_U", [128, 512], F32)
    cx.psum("l0f_psd0", [128, 512], F32)
    cx.psum("l0f_psd1", [128, 512], F32)
    th = TransposeHelper(cx)
    fp4s = [Fp4Helper(cx), Fp4Helper(cx, tag="fp4b")]

    XRES = cx.sb["XRES"]
    XQT = cx.sb["XQT"]
    HT = cx.sb["HT"]

    xt = x.rearrange("(n p) d -> p n d", p=128)
    em.dma("q", XRES[:, :].rearrange("p (n d) -> p n d", d=D), xt, inc=cx.sqx)
    xres_tok = em.token(cx.sqx)

    l0bufs = alloc_l0f_bufs(cx)
    qs = QSched(fp4s)
    units, chunk_tok = prep_units(cx, qs, th,
                                  [win["L0_ffn_w_gate"], win["L0_ffn_w_up"],
                                   win["L0_ffn_w_down"]],
                                  [wgT, wuT, wdT])
    stage_l0ffn(cx, qs, th, XRES, XQT, wgT, wuT, wdT, l0bufs, "c1",
                xres_waits=[(cx.sqx, xres_tok)],
                units=units, chunk_tok=chunk_tok)
    emit_barrier(cx)

    gw = {"Wq": win["L0_attn_Wq"], "Wk": win["L0_attn_Wk"],
          "Wv": win["L0_attn_Wv"], "Wg": win["L0_attn_Wg"],
          "Wgk1": win["L0_attn_Wgk1"], "Wgk2": win["L0_attn_Wgk2"],
          "bgk2": win["L0_attn_bgk2"], "norm_w": win["L0_attn_norm_w"],
          "Wo": win["L0_attn_Wo"]}
    gbufs = {"WQ": cx.sb["g_wq"], "WK": cx.sb["g_wk"],
             "WVG": cx.sb["g_wvg"], "WGK1": cx.sb["g_wgk1"],
             "WGK2": cx.sb["g_wgk2"], "BGK2": cx.sb["g_bgk2"],
             "NW": cx.sb["g_nw"]}
    wtok = stage_gla_weights(cx, gw, gbufs)
    ht_tok = stage_ln_transpose(cx, th, XRES, HT, "g", xres_waits=())
    stage_gla(cx, th, XRES, HT, gbufs, wtok, gw, "g", ht_tok=ht_tok, phase=3)
    emit_barrier(cx)

    stage_l0ffn(cx, qs, th, XRES, XQT, wgT, wuT, wdT, l0bufs, "c2",
                start_waits=(), xres_waits=())
    emit_barrier(cx)

    tsp = em.dma("q", xres_hbm.ap().rearrange("(n p) d -> p n d", p=128),
                 XRES[:, :].rearrange("p (n d) -> p n d", d=D), inc=cx.sout)
    emit_barrier(cx, extra=[(cx.sout, tsp)])

    iw = {"ln1_w": win["L1_ffn_ln1_w"], "ln1_b": win["L1_ffn_ln1_b"],
          "conv_w": win["L1_ffn_conv_w"], "attn_w": win["L1_ffn_attn_w"],
          "proj_w": win["L1_ffn_proj_w"]}
    mw = {"ln2_w": win["L1_ffn_ln2_w"], "ln2_b": win["L1_ffn_ln2_b"],
          "router_w": win["L1_ffn_router_w"],
          "expert_wv": win["L1_ffn_expert_wv"],
          "expert_proj": win["L1_ffn_expert_proj"]}
    aw = {"Wq": win["L1_attn_Wq"], "bq": win["L1_attn_bq"],
          "Wk": win["L1_attn_Wk"], "bk": win["L1_attn_bk"],
          "Wv": win["L1_attn_Wv"], "bv": win["L1_attn_bv"],
          "Wo": win["L1_attn_Wo"], "bo": win["L1_attn_bo"]}

    abase = {"HCT": cx.sb["a_hct"], "KTR": cx.sb["a_ktr"],
             "VAUG": cx.sb["a_vaug"], "QTR": cx.sb["a_qtr"],
             "T0": cx.sb["a_t0"], "TSW": cx.sb["a_tsw"],
             "PT": cx.sb["a_pt"], "YT": cx.sb["a_yt"],
             "ZROW": cx.sb["a_zrow"], "ZC": cx.sb["a_zc"],
             "BCOL": cx.sb["a_bcol"], "WB1": cx.sb["a_wb1"],
             "WBO": cx.sb["a_wbo"], "RZX": cx.sb["a_rzx"],
             "SQB": cx.sb["a_sqb"], "CVO": cx.sb["a_cvo"],
             "HTMP": cx.sb["a_htmp"], "XT": cx.sb["a_xt"],
             "XT2": cx.sb["a_xt2"]}
    ibufs = dict(abase)
    ibufs.update({"XIN": cx.sb["a_xin"], "CW": cx.sb["a_cw"],
                  "LN1W": cx.sb["a_ln1w"], "LN1B": cx.sb["a_ln1b"],
                  "LN1WR": cx.sb["a_ln1wr"], "LN1BR": cx.sb["a_ln1br"]})
    mhabufs = dict(abase)
    mhabufs.update({"BQ": cx.sb["a_bq"], "BK": cx.sb["a_bk"],
                    "BVR": cx.sb["a_bvr"], "BOR": cx.sb["a_bor"],
                    "BVRR": cx.sb["a_bvrr"], "BORR": cx.sb["a_borr"]})
    mbufs = {"HCT": cx.sb["a_hct"], "HTMP": cx.sb["a_htmp"],
             "XT": cx.sb["a_xt"], "XT2": cx.sb["a_xt2"],
             "ZBUF": cx.sb["m_zbuf"], "WVG": cx.sb["m_wvg"],
             "WVV": cx.sb["m_wvv"], "PROJB": cx.sb["m_projb"],
             "YAC": cx.sb["m_yac"], "FE": cx.sb["m_fe"],
             "RWB": cx.sb["m_rwb"], "SGT": cx.sb["m_sgt"],
             "LN2W": cx.sb["m_ln2w"], "LN2B": cx.sb["m_ln2b"],
             "LN2WR": cx.sb["m_ln2wr"], "LN2BR": cx.sb["m_ln2br"]}

    def bar_sem(name):
        s = em.sem(name)
        return (s, em.token(s))

    ACONSTS = ("c_mmask", "c_cos2", "c_sinrot")
    emit_barrier(cx, extra=[reload_consts(ACONSTS)])
    stage_attn(cx, th, xres_hbm, ibufs, iw, "inner", "i1", hbm_xinner=xinner)
    emit_barrier(cx, extra=[bar_sem("sxb_i1")])
    stage_moe(cx, th, xres_hbm, xinner, mbufs, mw, "m1")
    emit_barrier(cx, extra=[bar_sem("smo_m1"), reload_consts(ACONSTS)])
    stage_attn(cx, th, xres_hbm, mhabufs, aw, "mha", "a1")
    emit_barrier(cx, extra=[bar_sem("sxb_a1"), reload_consts(ACONSTS)])
    stage_attn(cx, th, xres_hbm, ibufs, iw, "inner", "i2", hbm_xinner=xinner)
    emit_barrier(cx, extra=[bar_sem("sxb_i2")])
    stage_moe(cx, th, xres_hbm, xinner, mbufs, mw, "m2", xres_out=y)

    cx.em.replay(prewaits=prew)
    return nc, consts, skip


_BUILD_CACHE = {}


def _get_build(name):
    if name not in _BUILD_CACHE:
        _BUILD_CACHE[name] = {
            "l0ffn1": lambda: build_launch_l0ffn(True),
            "l0ffn2": lambda: build_launch_l0ffn(False),
            "gla": build_launch_gla,
            "inner": build_launch_inner,
            "mha": build_launch_mha,
            "moe": build_launch_moe,
            "fused": build_fused,
        }[name]()
    return _BUILD_CACHE[name]


def kernel(**inputs):
    nc, consts, skip = _get_build("fused")
    cores = list(range(NCORES))
    cmap = {k: np.asarray(v) for k, v in consts.items() if k not in skip}
    xin = np.ascontiguousarray(np.asarray(inputs["L0_attn_x"], np.float32))
    shared = {k: np.ascontiguousarray(np.asarray(inputs[k], np.float32))
              for k in inputs if k != "L0_attn_x"}
    inmaps = []
    for c in cores:
        m = dict(cmap)
        m.update(shared)
        m["x"] = xin[c]
        inmaps.append(m)
    res = run_bass_kernel_spmd(nc, inmaps, core_ids=cores)
    return np.stack([np.asarray(res.results[c]["y"])
                     for c in cores]).astype(np.float32)


def kernel_multi(**inputs):
    xin = np.ascontiguousarray(np.asarray(inputs["L0_attn_x"], np.float32))
    cores = list(range(NCORES))

    def launch(name, shared, per_core):
        nc, consts, skip = _get_build(name)
        cmap = {k: np.asarray(v) for k, v in consts.items() if k not in skip}
        inmaps = []
        for c in cores:
            m = dict(cmap)
            m.update(shared)
            m.update(per_core[c])
            inmaps.append(m)
        return run_bass_kernel_spmd(nc, inmaps, core_ids=cores)

    def g(inp):
        return np.ascontiguousarray(np.asarray(inputs[inp], np.float32))

    res = launch("l0ffn1",
                 {"wg": g("L0_ffn_w_gate"), "wu": g("L0_ffn_w_up"),
                  "wd": g("L0_ffn_w_down")},
                 [{"x": xin[c]} for c in cores])
    xcur = [np.asarray(res.results[c]["y"]) for c in cores]
    wgT = np.asarray(res.results[0]["wgT"])
    wuT = np.asarray(res.results[0]["wuT"])
    wdT = np.asarray(res.results[0]["wdT"])

    gla_w = {"Wq": g("L0_attn_Wq"), "Wk": g("L0_attn_Wk"),
             "Wv": g("L0_attn_Wv"), "Wg": g("L0_attn_Wg"),
             "Wgk1": g("L0_attn_Wgk1"), "Wgk2": g("L0_attn_Wgk2"),
             "bgk2": g("L0_attn_bgk2"), "norm_w": g("L0_attn_norm_w"),
             "Wo": g("L0_attn_Wo")}
    res = launch("gla", gla_w, [{"x": xcur[c]} for c in cores])
    xcur = [np.asarray(res.results[c]["y"]) for c in cores]

    res = launch("l0ffn2", {"wgT": wgT, "wuT": wuT, "wdT": wdT},
                 [{"x": xcur[c]} for c in cores])
    xcur = [np.asarray(res.results[c]["y"]) for c in cores]

    inner_w = {"ln1_w": g("L1_ffn_ln1_w"), "ln1_b": g("L1_ffn_ln1_b"),
               "conv_w": g("L1_ffn_conv_w"), "attn_w": g("L1_ffn_attn_w"),
               "proj_w": g("L1_ffn_proj_w")}
    moe_w = {"ln2_w": g("L1_ffn_ln2_w"), "ln2_b": g("L1_ffn_ln2_b"),
             "router_w": g("L1_ffn_router_w"),
             "expert_wv": g("L1_ffn_expert_wv"),
             "expert_proj": g("L1_ffn_expert_proj")}
    mha_w = {"Wq": g("L1_attn_Wq"), "bq": g("L1_attn_bq"),
             "Wk": g("L1_attn_Wk"), "bk": g("L1_attn_bk"),
             "Wv": g("L1_attn_Wv"), "bv": g("L1_attn_bv"),
             "Wo": g("L1_attn_Wo"), "bo": g("L1_attn_bo")}

    for stage in ("ffn", "mha", "ffn"):
        if stage == "ffn":
            res = launch("inner", inner_w, [{"xres": xcur[c]} for c in cores])
            x1 = [np.asarray(res.results[c]["xinner"]) for c in cores]
            res = launch("moe", moe_w,
                         [{"xres": xcur[c], "x1": x1[c]} for c in cores])
            xcur = [np.asarray(res.results[c]["y"]) for c in cores]
        else:
            res = launch("mha", mha_w, [{"xres": xcur[c]} for c in cores])
            xcur = [np.asarray(res.results[c]["y"]) for c in cores]

    return np.stack(xcur).astype(np.float32)


# version 16
# speedup vs baseline: 1.0283x; 1.0283x over previous
import math
import numpy as np
import ml_dtypes
import concourse.bass as bass
import concourse.mybir as mybir
from concourse.bass_utils import run_bass_kernel_spmd

F32 = mybir.dt.float32
BF16 = mybir.dt.bfloat16
AF = mybir.ActivationFunctionType
ALU = mybir.AluOpType
AX = mybir.AxisListType

NCORES = 8
DEBUG_TAPS = False
T = 2048
D = 1024
NT = T // 128
ND = D // 128
EPS = 1e-5
LN8 = math.log(8.0)

ENGS = ("t", "v", "s", "p", "q")


class Sem:
    __slots__ = ("name", "n", "h")

    def __init__(self, name):
        self.name = name
        self.n = 0
        self.h = None


class Em:

    def __init__(self, nc):
        self.nc = nc
        self.ops = {e: [] for e in ENGS}
        self.sems = {}

    def sem(self, name):
        if name not in self.sems:
            self.sems[name] = Sem(name)
        return self.sems[name]

    def emit(self, eng, fn, waits=(), inc=None, n=1):
        w = [(s, v) for (s, v) in waits if v > 0]
        self.ops[eng].append((w, fn, inc, n))
        if inc is not None:
            inc.n += n
            return inc.n
        return None

    def token(self, sem):
        return sem.n

    def dma(self, eng, out, in_, waits=(), inc=None, **kw):
        return self.emit(eng, lambda e: e.dma_start(out=out, in_=in_, **kw),
                         waits=waits, inc=inc, n=16)

    def replay(self, prewaits=()):
        nc = self.nc
        for s in self.sems.values():
            s.h = nc.alloc_semaphore(name=s.name)
        with nc.Block() as block:
            def body(eng_ops, pre):
                def _(eng):
                    for (s, v) in pre:
                        if v > 0:
                            eng.wait_ge(s.h, v)
                    for (waits, fn, inc, n) in eng_ops:
                        for (s, v) in waits:
                            eng.wait_ge(s.h, v)
                        ins = fn(eng)
                        if inc is not None:
                            assert ins is not None
                            ins.then_inc(inc.h, n)
                return _
            if self.ops["q"]:
                block.sync(body(self.ops["q"], prewaits))
            if self.ops["p"]:
                block.gpsimd(body(self.ops["p"], prewaits))
            if self.ops["t"]:
                block.tensor(body(self.ops["t"], prewaits))
            if self.ops["v"]:
                block.vector(body(self.ops["v"], ()))
            if self.ops["s"]:
                block.scalar(body(self.ops["s"], prewaits))



def host_consts():
    c = {}
    bf = ml_dtypes.bfloat16
    c["c_ident"] = np.eye(128, dtype=np.float32).astype(bf)
    u = np.triu(np.ones((64, 64), np.float32)) / 16.0
    c["c_u16"] = u.astype(bf)
    c["c_triu"] = np.triu(np.ones((64, 64), np.float32)).astype(bf)
    tr2 = np.concatenate([np.triu(np.ones((64, 64), np.float32))] * 2, axis=1)
    c["c_triu2"] = tr2.astype(bf)
    masks = []
    kk = np.arange(128)[:, None]
    qq = np.arange(512)[None, :]
    for j in range(4):
        m = np.where(qq >= 128 * j + kk, 0.0, -1e30).astype(np.float32)
        masks.append(m.astype(bf))
    c["c_cmask"] = np.concatenate(masks, axis=1)
    mmasks = [np.where(qq >= 128 * j + kk, 1.0, 0.0).astype(np.float32).astype(bf)
              for j in range(4)]
    c["c_mmask"] = np.concatenate(mmasks, axis=1)
    pm = np.zeros((128, 128), np.float32)
    for r in range(128):
        blk = (r // 64) * 64
        pm[r, blk + (r % 64 + 32) % 64] = 1.0
    c["c_rot128"] = pm.T.astype(bf)
    o65 = np.zeros((65, 128), np.float32)
    o65[64, :] = 1.0
    c["c_ones65"] = o65
    c["c_ones_row32"] = np.ones((1, 128), np.float32)
    e = (np.arange(512)[None, :] // 16 == np.arange(32)[:, None]).astype(np.float32)
    c["c_e32"] = e.astype(bf)
    c["c_e32h"] = (0.5 * e).astype(bf)
    hd = 64
    inv = 1.0 / (10000.0 ** (np.arange(0, hd, 2, dtype=np.float64) / hd))
    tpos = np.arange(T, dtype=np.float64)
    fr = np.outer(tpos, inv)
    emb = np.concatenate([fr, fr], axis=1)
    cos = np.cos(emb).T.astype(np.float32)
    sin = np.sin(emb).T.astype(np.float32)
    c["c_cos"] = np.concatenate([cos, cos], axis=0).astype(bf)
    c["c_sin"] = np.concatenate([sin, sin], axis=0).astype(bf)
    sr = np.concatenate([-sin[0:32], sin[32:64]], axis=0)
    c["c_cos2"] = c["c_cos"]
    c["c_sinrot"] = np.concatenate([sr, sr], axis=0).astype(bf)
    c["c_ones_row"] = np.ones((1, 128), np.float32).astype(bf)
    c["c_ones_col"] = np.ones((128, 1), np.float32).astype(bf)
    return c


class Ctx:

    def __init__(self, nc):
        self.nc = nc
        self.em = Em(nc)
        self.sb = {}
        self.ps = {}
        self.sv = self.em.sem("sv")
        self.ss = self.em.sem("ss")
        self.st_ = self.em.sem("stn")
        self.sq = self.em.sem("sq")
        self.sp = self.em.sem("sp")
        self.sqc = self.em.sem("sqc")
        self.sqx = self.em.sem("sqx")
        self.sl0w = [self.em.sem("sl0w0"), self.em.sem("sl0w1")]
        self.sprin = self.em.sem("sprin")
        self.sglaw = self.em.sem("sglaw")
        self.swvg = self.em.sem("swvg")
        self.sout = self.em.sem("sout")
        self.spout = self.em.sem("spout")

    def sbuf(self, name, shape, dtype, at=None):
        if name not in self.sb:
            if getattr(self, "frozen", False):
                raise RuntimeError("unplanned sbuf alloc: " + name)
            if at is None:
                self.sb[name] = self.nc.alloc_sbuf_tensor(name, shape, dtype)
            else:
                self.sb[name] = self.nc.alloc_sbuf_tensor_at(
                    name, shape, dtype, offset=at)
        return self.sb[name]

    def psum(self, name, shape, dtype=F32):
        if name not in self.ps:
            self.ps[name] = self.nc.alloc_psum_tensor(name, shape, dtype)
        return self.ps[name]

    def load_consts(self, consts, skip=()):
        for k, arr in consts.items():
            if k in skip:
                continue
            dt = BF16 if arr.dtype == ml_dtypes.bfloat16 else F32
            h = self.nc.dram_tensor(k, list(arr.shape), dt, kind="ExternalInput")
            t = self.sbuf(k + "_sb", list(arr.shape), dt)
            self.em.dma("q", t[:, :], h.ap(), inc=self.sqc)
        self.tok_consts = self.em.token(self.sqc)

    def cst(self, name):
        return self.sb[name + "_sb"]

    FCONST_VALUES = (EPS, 1.0 / D, -0.5, -LN8, -1.0, 1.0 / 128, 1.0, 0.5,
                     0.125, 2.0)

    def prelude_fconsts(self):
        vals = list(self.FCONST_VALUES)
        pool = self.sbuf("fconst_pool", [128, len(vals)], F32)
        tok = None
        for i, val in enumerate(vals):
            ap = pool[:, i:i + 1]
            last = (i == len(vals) - 1)
            tok = self.em.emit("v", lambda v, ap=ap, val=val: v.memset(ap, val),
                               inc=self.sv if last else None)
        for i, val in enumerate(vals):
            self.nc.const_aps.register(F32, float(val), pool[:, i:i + 1])
        self.tok_fc = tok
        return [(self.sv, tok)]



def emit_ln_tile(cx, x_ap, out_ap, scratch, waits=(), width=D, out2_ap=None):
    em = cx.em
    s1, mu, ss2, rstd, xc = (scratch[k] for k in
                             ("s1", "mu", "ss", "rstd", "xc"))
    tv = em.emit("v", lambda v: v.reduce_sum(s1[:, 0:1], x_ap, axis=AX.X),
                 waits=waits, inc=cx.sv)
    ts = em.emit("s", lambda s: s.mul(mu[:, 0:1], s1[:, 0:1], 1.0 / width),
                 waits=[(cx.sv, tv)], inc=cx.ss)
    tv = em.emit("v", lambda v: v.tensor_scalar(xc[:, 0:width], x_ap, mu[:, 0:1],
                                                None, op0=ALU.subtract),
                 waits=[(cx.ss, ts)], inc=cx.sv)
    ts = em.emit("s", lambda s: s.activation(out_ap, xc[:, 0:width],
                                             AF.Square, accum_out=ss2[:, 0:1]),
                 waits=[(cx.sv, tv)], inc=cx.ss)
    ts = em.emit("s", lambda s: s.activation(rstd[:, 0:1], ss2[:, 0:1], AF.Ln,
                                             bias=EPS, scale=1.0 / width),
                 waits=[(cx.ss, ts)], inc=cx.ss)
    ts = em.emit("s", lambda s: s.activation(rstd[:, 0:1], rstd[:, 0:1], AF.Exp,
                                             scale=-0.5),
                 waits=[(cx.ss, ts)], inc=cx.ss)
    tv = em.emit("v", lambda v: v.tensor_scalar(out_ap, xc[:, 0:width],
                                                rstd[:, 0:1], None, op0=ALU.mult),
                 waits=[(cx.ss, ts)], inc=cx.sv)
    if out2_ap is not None:
        tv = em.emit("v", lambda v: v.tensor_scalar(out2_ap, xc[:, 0:width],
                                                    rstd[:, 0:1], None,
                                                    op0=ALU.mult), inc=cx.sv)
    return tv, ts


def ln_scratch(cx, tag=""):
    return {
        "s1": cx.sbuf("lns1" + tag, [128, 1], F32),
        "mu": cx.sbuf("lnmu" + tag, [128, 1], F32),
        "ss": cx.sbuf("lnss" + tag, [128, 1], F32),
        "rstd": cx.sbuf("lnrstd" + tag, [128, 1], F32),
        "xc": cx.sbuf("lnxc" + tag, [128, D], F32),
    }



class TransposeHelper:

    def __init__(self, cx, tag="tr"):
        self.cx = cx
        self.p0 = cx.psum(tag + "_p0", [128, 1024], BF16)
        self.p1 = cx.psum(tag + "_p1", [128, 1024], BF16)
        self.last_evac = [0, 0]
        self.last_eng = ["v", "v"]
        self.slot = 0

    def transpose(self, in_ap, out_ap, waits=(), evac_eng="s"):
        cx, em = self.cx, self.cx.em
        sl = self.slot
        self.slot ^= 1
        ps = (self.p0, self.p1)[sl]
        pin, fin = in_ap.shape
        ident = cx.cst("c_ident")
        w = list(waits) + [(cx.sqc, cx.tok_consts)]
        if self.last_evac[sl]:
            sem = cx.sv if self.last_eng[sl] == "v" else cx.ss
            w.append((sem, self.last_evac[sl]))
        tt = em.emit("t", lambda t: t.transpose(ps[0:fin, 0:pin], in_ap,
                                                ident[0:pin, 0:pin]),
                     waits=w, inc=cx.st_)
        if evac_eng == "s":
            te = em.emit("s", lambda s: s.copy(out_ap, ps[0:fin, 0:pin]),
                         waits=[(cx.st_, tt)], inc=cx.ss)
        else:
            te = em.emit("v", lambda v: v.tensor_copy(out_ap, ps[0:fin, 0:pin]),
                         waits=[(cx.st_, tt)], inc=cx.sv)
        self.last_evac[sl] = te
        self.last_eng[sl] = evac_eng
        return te



class Fp4Helper:
    def __init__(self, cx, tag="fp4", **_):
        self.cx = cx
        self.last_tre = 0
        self.amax = cx.sbuf(tag + "_amax", [128, 64], F32)
        self.scf = cx.sbuf(tag + "_scf", [128, 64], F32)
        self.rs = cx.sbuf(tag + "_rs", [128, 64], F32)
        self.xn = cx.sbuf(tag + "_xn", [128, 1024], BF16)
        self.u = cx.sbuf(tag + "_u", [128, 1024], BF16)
        self.sg = cx.sbuf(tag + "_sg", [128, 1024], BF16)
        self.a = cx.sbuf(tag + "_a", [128, 1024], BF16)
        self.c = cx.sbuf(tag + "_c", [128, 1024], BF16)

    def quant(self, x_ap, out_ap, W, waits=(), pe_gate=0):
        cx, em = self.cx, self.cx.em
        B = W // 16
        x3 = x_ap.rearrange("p (b k) -> p b k", k=16)
        o3 = out_ap.rearrange("p (b k) -> p b k", k=16)
        xn3 = self.xn[:, 0:W].rearrange("p (b k) -> p b k", k=16)
        a3 = self.a[:, 0:W].rearrange("p (b k) -> p b k", k=16)
        sc_b = self.scf[:, 0:B].unsqueeze(2).broadcast_to((128, B, 16))
        tv = em.emit("v", lambda v: v.reduce_max(self.amax[:, 0:B].unsqueeze(2),
                                                 x3, axis=AX.X,
                                                 apply_absolute_value=True),
                     waits=waits, inc=cx.sv)
        tv0 = em.emit("v", lambda v: v.tensor_scalar(self.scf[:, 0:B],
                                                     self.amax[:, 0:B],
                                                     1.0 / 6.0, 1e-12,
                                                     op0=ALU.mult, op1=ALU.max),
                      waits=[(cx.sv, tv)], inc=cx.sv)
        trc = em.emit("v", lambda v: v.reciprocal(self.rs[:, 0:B],
                                                  self.scf[:, 0:B]),
                      waits=[(cx.sv, tv0)], inc=cx.sv)
        rs_b = self.rs[:, 0:B].unsqueeze(2).broadcast_to((128, B, 16))
        tvx = em.emit("v", lambda v: v.tensor_tensor(xn3, x3, rs_b,
                                                     op=ALU.mult),
                      waits=[(cx.sv, trc)], inc=cx.sv)
        tu = em.emit("s", lambda s: s.activation(self.u[:, 0:W],
                                                 self.xn[:, 0:W], AF.Abs),
                     waits=[(cx.sv, tvx)], inc=cx.ss)
        tsg = em.emit("s", lambda s: s.sign(self.sg[:, 0:W], self.xn[:, 0:W]),
                      inc=cx.ss)
        tl = em.emit("v", lambda v: v.tensor_scalar(self.a[:, 0:W],
                                                    self.u[:, 0:W],
                                                    0.25, 1.0, op0=ALU.is_gt,
                                                    op1=ALU.mult),
                     waits=[(cx.ss, tu)], inc=cx.sv)
        for thr in (0.75, 1.25, 1.75):
            tl = em.emit("v", lambda v, th=thr: v.scalar_tensor_tensor(
                self.a[:, 0:W], self.u[:, 0:W], th, self.a[:, 0:W],
                op0=ALU.is_gt, op1=ALU.add), waits=[(cx.sv, tl)], inc=cx.sv)
        tc = em.emit("v", lambda v: v.tensor_scalar(self.c[:, 0:W],
                                                    self.u[:, 0:W],
                                                    5.0, 2.0, op0=ALU.is_gt,
                                                    op1=ALU.mult), inc=cx.sv)
        for thr in (2.5, 3.5):
            tc = em.emit("v", lambda v, th=thr: v.scalar_tensor_tensor(
                self.c[:, 0:W], self.u[:, 0:W], th, self.c[:, 0:W],
                op0=ALU.is_gt, op1=ALU.add), waits=[(cx.sv, tc)], inc=cx.sv)
        tl = em.emit("v", lambda v: v.scalar_tensor_tensor(
            self.a[:, 0:W], self.a[:, 0:W], 0.5, self.c[:, 0:W],
            op0=ALU.mult, op1=ALU.add),
            waits=[(cx.sv, max(tl, tc))], inc=cx.sv)
        tl = em.emit("v", lambda v: v.tensor_tensor(self.a[:, 0:W],
                                                    self.a[:, 0:W],
                                                    self.sg[:, 0:W],
                                                    op=ALU.mult),
                     waits=[(cx.ss, tsg), (cx.sv, tl)], inc=cx.sv)
        tv = em.emit("v", lambda v: v.tensor_tensor(o3, a3, sc_b,
                                                    op=ALU.mult),
                     waits=[(cx.sv, tl)], inc=cx.sv)
        return tv



def chunks_of(total, step=512):
    out = []
    c0 = 0
    while c0 < total:
        cw = min(step, total - c0)
        out.append((c0, cw))
        c0 += cw
    return out


def alloc_l0f_bufs(cx):
    return {
        "GBUF": [cx.sbuf(f"l0f_g{i}", [128, 8 * 512], BF16) for i in range(2)],
        "UBUF": [cx.sbuf(f"l0f_u{i}", [128, 8 * 512], BF16) for i in range(2)],
        "DBUF": [cx.sbuf(f"l0f_d{i}", [128, 4 * 1024], BF16) for i in range(2)],
        "HQT": cx.sbuf("l0f_hqt", [128, 4 * 2048], BF16),
    }


class QSched:

    def __init__(self, fp4s):
        self.fp4s = fp4s
        self.i = 0

    def quant(self, x_ap, out_ap, W, waits=()):
        h = self.fp4s[self.i % len(self.fp4s)]
        self.i += 1
        return h.quant(x_ap, out_ap, W, waits=waits)


def prep_units(cx, qs, th, srcs, dsts):
    em = cx.em
    sld = [cx.sprin, em.sem("sprin_b")]
    sst = [cx.sout, em.sem("sprep_ob")]
    wt = [cx.sbuf("prp_w0", [128, 1024], BF16),
          cx.sbuf("prp_w1", [128, 1024], BF16)]
    state = {"tq": [0, 0], "ttr": [0, 0], "tst": [0, 0], "ui": 0}
    chs = chunks_of(II)
    chunk_tok = [{"g": {}, "u": {}, "d": {}} for _ in chs]

    def part_b(par, cw, tq, dst_slice, nch, kind):
        wtile = stg = wt[par]
        te = 0
        for kt in range(cw // 128):
            te = th.transpose(wtile[:, kt * 128:(kt + 1) * 128],
                              stg[:, kt * 128:(kt + 1) * 128],
                              waits=[(cx.sv, tq),
                                     (sst[par], state["tst"][par])])
        state["ttr"][par] = te
        tso = em.dma("q", dst_slice.rearrange("(a p) c -> p a c", p=128),
                     stg[:, 0:cw].rearrange("p (a c) -> p a c", c=128),
                     waits=[(cx.ss, te)], inc=sst[par])
        state["tst"][par] = tso
        chunk_tok[nch][kind][sst[par].name] = (sst[par], tso)

    pending = []

    def unit(src, rows0, c0, cw, dst_slice, nch, kind):
        def run():
            if pending:
                pending.pop(0)()
            par = state["ui"] % 2
            state["ui"] += 1
            wtile = wt[par]
            w = []
            if state["tq"][par]:
                w.append((cx.sv, state["tq"][par]))
            if state["ttr"][par]:
                w.append((cx.ss, state["ttr"][par]))
            if state["tst"][par]:
                w.append((sst[par], state["tst"][par]))
            tl = em.dma("p", wtile[:, 0:cw],
                        src[rows0:rows0 + 128, c0:c0 + cw],
                        waits=w, inc=sld[par])
            tq = qs.quant(wtile[:, 0:cw], wtile[:, 0:cw], cw,
                          waits=[(sld[par], tl)])
            state["tq"][par] = tq
            pending.append(
                lambda par=par: part_b(par, cw, tq, dst_slice, nch, kind))
        return run

    def drain():
        while pending:
            pending.pop(0)()

    wg, wu, wd = srcs
    wgT, wuT, wdT = dsts
    units = []
    for nch, (c0, cw) in enumerate(chs):
        for r in range(c0 // 128, (c0 + cw) // 128):
            units.append(unit(wg, r * 128, 0, D,
                              wgT.ap()[:, r * 128:(r + 1) * 128], nch, "g"))
            units.append(unit(wu, r * 128, 0, D,
                              wuT.ap()[:, r * 128:(r + 1) * 128], nch, "u"))
        for rr in range(D // 128):
            units.append(unit(wd, rr * 128, c0, cw,
                              wdT.ap()[c0:c0 + cw, rr * 128:(rr + 1) * 128],
                              nch, "d"))
    units.append(lambda: drain())
    return units, chunk_tok



def stage_l0ffn(cx, qs, th, XRES, XQT, wgT, wuT, wdT, bufs, tag,
                start_waits=(), xres_waits=(), units=None, chunk_tok=None):
    em = cx.em
    Htmp = cx.sbuf("l0f_h", [128, D], BF16)
    XQ = cx.sbuf("l0f_xq", [128, D], BF16)
    XQ2 = cx.sbuf("l0f_xq2", [128, D], BF16)
    XQ3 = cx.sbuf("l0f_xq3", [128, D], BF16)
    scr = ln_scratch(cx, "z")
    feed = list(units) if units else []

    def tick():
        if feed:
            feed.pop(0)()

    def qcall(x_ap, out_ap, W, waits):
        return qs.quant(x_ap, out_ap, W, waits=waits)
    def p0_flush(pend):
        st_p, xqb, tq_p = pend
        for kd in range(ND):
            th.transpose(xqb[:, kd * 128:(kd + 1) * 128],
                         XQT[:, kd * T + st_p * 128:kd * T + (st_p + 1) * 128],
                         waits=[(cx.sv, tq_p)])

    p0_pend = None
    for st in range(NT):
        tick()
        tv, _ = emit_ln_tile(cx, XRES[:, st * D:(st + 1) * D], Htmp[:, 0:D], scr,
                             waits=xres_waits)
        xqb = (XQ, XQ2, XQ3)[st % 3]
        tv2 = qcall(Htmp[:, 0:D], xqb[:, 0:D], D,
                    [(cx.sv, tv), (cx.st_, cx.em.token(cx.st_))])
        if p0_pend is not None:
            p0_flush(p0_pend)
        p0_pend = (st, xqb, tv2)
    p0_flush(p0_pend)
    GBUF = bufs["GBUF"]
    UBUF = bufs["UBUF"]
    DBUF = bufs["DBUF"]
    HQT = bufs["HQT"]
    TG = cx.sbuf("l0f_tg", [128, 512], BF16)
    TU = cx.sbuf("l0f_tu", [128, 512], BF16)
    PS_G = cx.psum("PS_G", [128, 512], F32)
    PS_U = cx.psum("PS_U", [128, 512], F32)
    PS_D = [cx.psum("l0f_psd0", [128, 512], F32),
            cx.psum("l0f_psd1", [128, 512], F32)]
    last_d_evac = [0, 0]
    last_use = [0, 0]
    tok_load = [0, 0]
    tvh_prev = 0
    ts2_prev = 0
    chs = chunks_of(2816)
    for nch, (c0, cw) in enumerate(chs):
        par = nch % 2
        nkt = (cw + 127) // 128
        def do_loads(par=par, c0=c0, cw=cw, nkt=nkt, nch=nch):
            w = ([(cx.st_, last_use[par])] if last_use[par]
                 else list(start_waits))
            wu_w, wd_w = [], []
            if chunk_tok is not None:
                for kind, dst in (("g", w), ("u", wu_w), ("d", wd_w)):
                    toks = list(chunk_tok[nch][kind].values())
                    assert toks, (tag, nch, kind)
                    dst.extend(toks)
            em.dma("q", GBUF[par][:, 0:8 * cw].rearrange("p (a c) -> p a c", a=ND),
                   wgT.ap()[:, c0:c0 + cw].rearrange("(a p) c -> p a c", p=128),
                   waits=w, inc=cx.sl0w[par])
            em.dma("q", UBUF[par][:, 0:8 * cw].rearrange("p (a c) -> p a c", a=ND),
                   wuT.ap()[:, c0:c0 + cw].rearrange("(a p) c -> p a c", p=128),
                   waits=wu_w, inc=cx.sl0w[par])
            tl = em.dma("q", DBUF[par][:, 0:nkt * 1024].rearrange("p (a c) -> p a c", a=nkt),
                        wdT.ap()[c0:c0 + cw, :].rearrange("(a p) c -> p a c", p=128),
                        waits=wd_w, inc=cx.sl0w[par])
            tok_load[par] = tl
        tmm = 0
        pendq = []

        def flush(pend, par=par, cw=cw, nkt=nkt):
            st_p, tq_p, xqb = pend
            te = 0
            for kt in range(nkt):
                te = th.transpose(xqb[:, kt * 128:(kt + 1) * 128],
                                  HQT[:, kt * T + st_p * 128:kt * T + (st_p + 1) * 128],
                                  waits=[(cx.sv, tq_p)])
            for oc in range(2):
                dp = oc % 2
                wt2 = [(cx.ss, te), (cx.sv, last_d_evac[dp])]
                tdm = 0
                for kt in range(nkt):
                    tdm = em.emit("t", lambda t, kt=kt, oc=oc, par=par, dp=dp, st=st_p, nkt=nkt: t.matmul(
                        PS_D[dp][:, 0:512],
                        HQT[:, kt * T + st * 128:kt * T + (st + 1) * 128],
                        DBUF[par][:, kt * 1024 + oc * 512:kt * 1024 + (oc + 1) * 512],
                        start=(kt == 0), stop=(kt == nkt - 1)),
                        waits=wt2 if kt == 0 else (),
                        inc=cx.st_ if kt == nkt - 1 else None)
                last_d_evac[dp] = em.emit(
                    "v", lambda v, st=st_p, oc=oc, dp=dp: v.scalar_tensor_tensor(
                        XRES[:, st * D + oc * 512:st * D + (oc + 1) * 512],
                        PS_D[dp][:, 0:512], 0.5,
                        XRES[:, st * D + oc * 512:st * D + (oc + 1) * 512],
                        op0=ALU.mult, op1=ALU.add),
                    waits=[(cx.st_, tdm)], inc=cx.sv)

        for st in range(NT):
            tick()
            if st == 0:
                do_loads()
            wt = [(cx.sl0w[par], tok_load[par]), (cx.ss, ts2_prev)]
            if st == 0 and nch == 0:
                wt.append((cx.ss, cx.em.token(cx.ss)))
            for kd in range(ND):
                tmm = em.emit("t", lambda t, kd=kd, par=par, cw=cw, st=st: t.matmul(
                    PS_G[:, 0:cw],
                    XQT[:, kd * T + st * 128:kd * T + (st + 1) * 128],
                    GBUF[par][:, kd * cw:(kd + 1) * cw],
                    start=(kd == 0), stop=(kd == ND - 1)),
                    waits=wt if kd == 0 else (), inc=cx.st_ if kd == ND - 1 else None)
            for kd in range(ND):
                tmm = em.emit("t", lambda t, kd=kd, par=par, cw=cw, st=st: t.matmul(
                    PS_U[:, 0:cw],
                    XQT[:, kd * T + st * 128:kd * T + (st + 1) * 128],
                    UBUF[par][:, kd * cw:(kd + 1) * cw],
                    start=(kd == 0), stop=(kd == ND - 1)),
                    inc=cx.st_ if kd == ND - 1 else None)
            ts1 = em.emit("s", lambda s, cw=cw: s.activation(TG[:, 0:cw],
                                                             PS_G[:, 0:cw],
                                                             AF.Silu),
                          waits=[(cx.st_, tmm), (cx.sv, tvh_prev)], inc=cx.ss)
            ts2 = em.emit("s", lambda s, cw=cw: s.copy(TU[:, 0:cw], PS_U[:, 0:cw]),
                          inc=cx.ss)
            ts2_prev = ts2
            tvh = em.emit("v", lambda v, cw=cw: v.tensor_tensor(
                Htmp[:, 0:cw], TG[:, 0:cw], TU[:, 0:cw], op=ALU.mult),
                waits=[(cx.ss, ts2)], inc=cx.sv)
            tvh_prev = tvh
            xqb = (XQ, XQ2, XQ3)[st % 3]
            tq = qcall(Htmp[:, 0:cw], xqb[:, 0:cw], cw,
                       [(cx.sv, tvh), (cx.st_, cx.em.token(cx.st_))])
            pendq.append((st, tq, xqb))
            if len(pendq) > 3:
                flush(pendq.pop(0))
        while pendq:
            flush(pendq.pop(0))
        last_use[par] = cx.em.token(cx.st_)
    while feed:
        feed.pop(0)()
    return max(last_d_evac)



def stage_ln_transpose(cx, th, XRES, HT, tag, xres_waits=()):
    Htmp = cx.sbuf("lnt_h" + tag, [128, D], BF16)
    scr = ln_scratch(cx, "z")
    te = 0
    for st in range(NT):
        tv, _ = emit_ln_tile(cx, XRES[:, st * D:(st + 1) * D], Htmp[:, 0:D], scr,
                             waits=xres_waits)
        for kd in range(ND):
            te = th.transpose(Htmp[:, kd * 128:(kd + 1) * 128],
                              HT[:, kd * T + st * 128:kd * T + (st + 1) * 128],
                              waits=[(cx.sv, tv)])
    return te



GLA_H = 8
GLA_DK = 64
GLA_DV = 128
CHUNK = 64
SC_TOK = 256
NSC = T // SC_TOK


def stage_gla_weights(cx, w, bufs):
    em = cx.em
    tl = 0
    for name, ap in (("WQ", w["Wq"]), ("WK", w["Wk"]),
                     ("WGK1", w["Wgk1"])):
        dst = bufs[name]
        tl = em.dma("p", dst[:, :].rearrange("p (a c) -> p a c", a=ND),
                    ap.rearrange("(a p) c -> p a c", p=128), inc=cx.sglaw)
    tl = em.dma("p", bufs["WGK2"][:, :], w["Wgk2"], inc=cx.sglaw)
    tl = em.dma("p", bufs["BGK2"][:, :], w["bgk2"].unsqueeze(0), inc=cx.sglaw)
    tl = em.dma("p", bufs["NW"][:, :], w["norm_w"].unsqueeze(1), inc=cx.sglaw)
    return {"sp": tl, "sv": 0}


def stage_gla(cx, th, XRES, HT, bufs, wtok, w, tag, ht_tok=0, phase=3):
    em = cx.em
    cs = cx.cst
    QT = cx.sbuf("gla_qt", [128, 4 * SC_TOK], BF16)
    KT = cx.sbuf("gla_kt", [128, 4 * SC_TOK], BF16)
    KTOK = cx.sbuf("gla_ktok", [64, 4 * 512], BF16)
    VTOK = cx.sbuf("gla_vtok", [64, 4 * 1024], BF16)
    GKC = cx.sbuf("gla_gk", [64, 4 * 512], BF16)
    T1T = cx.sbuf("gla_t1t", [16, SC_TOK], BF16)
    SGT = cx.sbuf("gla_sg", [64, 512], BF16)
    EB = cx.sbuf("gla_eb", [128, 4 * 256], BF16)
    EMB = cx.sbuf("gla_emb", [128, 4 * 256], BF16)
    QET = cx.sbuf("gla_qet", [128, 4 * 256], BF16)
    KET = cx.sbuf("gla_ket", [128, 4 * 256], BF16)
    EMBT = cx.sbuf("gla_embt", [64, 512], BF16)
    KEMB = cx.sbuf("gla_kemb", [64, 4 * 512], BF16)
    ELAST = cx.sbuf("gla_elast", [128, 4 * 4], F32)
    ATS = cx.sbuf("gla_ats", [64, 128], BF16)
    SHP = [cx.sbuf(f"gla_s{hp}", [128, GLA_DV], F32) for hp in range(4)]
    SHPB = [cx.sbuf(f"gla_sb{hp}", [128, GLA_DV], BF16) for hp in range(4)]
    OSC = cx.sbuf("gla_osc", [64, 4 * 1024], BF16)
    OTMP = cx.sbuf("gla_otmp", [64, 256], BF16)
    GTOK = cx.sbuf("gla_gtok", [64, 4 * 1024], BF16)
    OGT = cx.sbuf("gla_ogt", [128, ND * SC_TOK], BF16)
    RSTD = cx.sbuf("gla_rstd", [64, 4 * 8], F32)
    SSQ = cx.sbuf("gla_ssq", [64, 4 * 8], F32)
    SQS = cx.sbuf("gla_sqs", [64, 128], F32)
    LNT = cx.sbuf("gla_lnt", [64, 4 * 8], F32)
    PSP = cx.ps["PS_G"]
    PSB = cx.ps["PS_U"]
    PSD0 = cx.ps["l0f_psd0"]
    PSD1 = cx.ps["l0f_psd1"]

    tv0 = 0
    for hp in range(4):
        em.emit("v", lambda v, hp=hp: v.memset(SHP[hp][:, :], 0.0))
        tv0 = em.emit("v", lambda v, hp=hp: v.memset(SHPB[hp][:, :], 0.0),
                      inc=cx.sv)

    tok_sc_done = 0
    tok_out_evac = 0
    tok_proj_evac = ht_tok
    tok_wo_mm = 0
    import os as _os
    nsc_lim = int(_os.environ.get("GLA_NSC", str(NSC)))
    for sc in range(nsc_lim):
        st0 = 2 * sc
        pe_tok = 0
        evs = []
        for (dst, wname) in ((QT, "WQ"), (KT, "WK")):
            for hp in range(4):
                wt = [(cx.sglaw, wtok["sp"]), (cx.ss, tok_proj_evac),
                      (cx.st_, tok_wo_mm), (cx.sv, tok_sc_done)]
                for kd in range(ND):
                    pe_tok = em.emit("t", lambda t, kd=kd, hp=hp, sc=sc, wname=wname: t.matmul(
                        PSP[:, 0:SC_TOK],
                        bufs[wname][:, kd * 512 + hp * 128:kd * 512 + (hp + 1) * 128],
                        HT[:, kd * T + sc * SC_TOK:kd * T + (sc + 1) * SC_TOK],
                        start=(kd == 0), stop=(kd == ND - 1)),
                        waits=wt if kd == 0 else (),
                        inc=cx.st_ if kd == ND - 1 else None)
                ev = em.emit("s", lambda s, dst=dst, hp=hp: s.copy(
                    dst[:, hp * SC_TOK:(hp + 1) * SC_TOK], PSP[:, 0:SC_TOK]),
                    waits=[(cx.st_, pe_tok)], inc=cx.ss)
                tok_proj_evac = ev
        for cc in range(4):
            c = 4 * sc + cc
            stc = c // 2
            rb = 64 * (c % 2)
            for kd in range(ND):
                pe_tok = em.emit("t", lambda t, kd=kd, stc=stc, rb=rb, c=c: t.matmul(
                    PSP[0:64, 0:512],
                    HT[:, kd * T + c * 64:kd * T + (c + 1) * 64],
                    bufs["WK"][:, kd * 512:(kd + 1) * 512],
                    start=(kd == 0), stop=(kd == ND - 1)),
                    waits=[(cx.ss, tok_proj_evac)] if kd == 0 else (),
                    inc=cx.st_ if kd == ND - 1 else None)
            tok_proj_evac = em.emit("s", lambda s, cc=cc: s.copy(
                KTOK[0:64, cc * 512:(cc + 1) * 512], PSP[0:64, 0:512]),
                waits=[(cx.st_, pe_tok)], inc=cx.ss)
        WVG = bufs["WVG"]
        for (dst, wap) in ((VTOK, w["Wv"]), (GTOK, w["Wg"])):
            tok_wvg = em.dma("p", WVG[:, :].rearrange("p (a c) -> p a c", a=ND),
                             wap.rearrange("(a p) c -> p a c", p=128),
                             waits=[(cx.st_, em.token(cx.st_))], inc=cx.swvg)
            for cc in range(4):
                c = 4 * sc + cc
                for oc in range(2):
                    for kd in range(ND):
                        pe_tok = em.emit("t", lambda t, kd=kd, oc=oc, c=c: t.matmul(
                            PSP[0:64, 0:512],
                            HT[:, kd * T + c * 64:kd * T + (c + 1) * 64],
                            WVG[:, kd * 1024 + oc * 512:kd * 1024 + (oc + 1) * 512],
                            start=(kd == 0), stop=(kd == ND - 1)),
                            waits=([(cx.ss, tok_proj_evac), (cx.swvg, tok_wvg)]
                                   if kd == 0 else ()),
                            inc=cx.st_ if kd == ND - 1 else None)
                    tok_proj_evac = em.emit("s", lambda s, dst=dst, cc=cc, oc=oc: s.copy(
                        dst[0:64, cc * 1024 + oc * 512:cc * 1024 + (oc + 1) * 512],
                        PSP[0:64, 0:512]),
                        waits=[(cx.st_, pe_tok)], inc=cx.ss)
        for kd in range(ND):
            pe_tok = em.emit("t", lambda t, kd=kd, sc=sc: t.matmul(
                T1T_ps(cx)[0:16, 0:SC_TOK],
                bufs["WGK1"][:, kd * 16:(kd + 1) * 16],
                HT[:, kd * T + sc * SC_TOK:kd * T + (sc + 1) * SC_TOK],
                start=(kd == 0), stop=(kd == ND - 1)),
                waits=[(cx.ss, tok_proj_evac)] if kd == 0 else (),
                inc=cx.st_ if kd == ND - 1 else None)
        tok_proj_evac = em.emit("s", lambda s: s.copy(T1T[0:16, 0:SC_TOK],
                                                      T1T_ps(cx)[0:16, 0:SC_TOK]),
                                waits=[(cx.st_, pe_tok)], inc=cx.ss)
        for cc in range(4):
            pe_tok = em.emit("t", lambda t, cc=cc: t.matmul(
                PSP[0:64, 0:512], T1T[0:16, cc * 64:(cc + 1) * 64],
                bufs["WGK2"][0:16, 0:512], start=True, stop=False),
                waits=[(cx.ss, tok_proj_evac)], inc=cx.st_)
            pe_tok = em.emit("t", lambda t, cc=cc: t.matmul(
                PSP[0:64, 0:512], cs("c_ones_row")[0:1, 0:64],
                bufs["BGK2"][0:1, 0:512], start=False, stop=True),
                inc=cx.st_)
            ts = em.emit("s", lambda s, cc=cc: s.activation(
                SGT[0:64, 0:512], PSP[0:64, 0:512], AF.Sigmoid),
                waits=[(cx.st_, pe_tok)], inc=cx.ss)
            tok_proj_evac = em.emit("s", lambda s, cc=cc: s.activation(
                GKC[0:64, cc * 512:(cc + 1) * 512], SGT[0:64, 0:512], AF.Ln),
                waits=[(cx.ss, ts)], inc=cx.ss)

        if phase < 2:
            continue
        tok_exp = {}
        tok_psb_read = tok_proj_evac
        tkm_prev = 0
        for hp in range(4):
            wb = [(cx.ss, tok_psb_read), (cx.sv, tok_sc_done)]
            pe2 = 0
            for cc in range(4):
                pe2 = em.emit("t", lambda t, cc=cc, hp=hp: t.matmul(
                    PSB[:, cc * 64:(cc + 1) * 64],
                    GKC[0:64, cc * 512 + hp * 128:cc * 512 + (hp + 1) * 128],
                    cs("c_u16")[0:64, 0:64], start=True, stop=True),
                    waits=wb if cc == 0 else (), inc=cx.st_)
                pe2 = em.emit("t", lambda t, cc=cc, hp=hp: t.matmul(
                    PSB[:, 256 + cc:256 + cc + 1],
                    GKC[0:64, cc * 512 + hp * 128:cc * 512 + (hp + 1) * 128],
                    cs("c_u16")[0:64, 63:64], start=True, stop=True),
                    inc=cx.st_)
            t_eb = em.emit("s", lambda s, hp=hp: s.activation(
                EB[:, hp * 256:(hp + 1) * 256], PSB[:, 0:256], AF.Exp,
                bias=-LN8), waits=[(cx.st_, pe2)], inc=cx.ss)
            t_emb = em.emit("s", lambda s, hp=hp: s.activation(
                EMB[:, hp * 256:(hp + 1) * 256], PSB[:, 0:256], AF.Exp,
                scale=-1.0), inc=cx.ss)
            t_el = em.emit("s", lambda s, hp=hp: s.activation(
                ELAST[:, hp * 4:(hp + 1) * 4], PSB[:, 256:260], AF.Exp),
                inc=cx.ss)
            tok_psb_read = t_el
            pe3 = 0
            for cc in range(4):
                pe3 = em.emit("t", lambda t, cc=cc, hp=hp: t.matmul(
                    PSD0[0:64, cc * 128:(cc + 1) * 128],
                    cs("c_u16")[0:64, 0:64],
                    GKC[0:64, cc * 512 + hp * 128:cc * 512 + (hp + 1) * 128],
                    start=True, stop=True),
                    waits=[(cx.ss, t_el)] if cc == 0 else (), inc=cx.st_)
            t_embt = em.emit("s", lambda s, hp=hp: s.activation(
                EMBT[0:64, 0:512], PSD0[0:64, 0:512],
                AF.Exp, scale=-1.0),
                waits=[(cx.st_, pe3), (cx.sv, tkm_prev)], inc=cx.ss)
            tqe = em.emit("v", lambda v, hp=hp: v.tensor_tensor(
                QET[:, hp * 256:(hp + 1) * 256], QT[:, hp * 256:(hp + 1) * 256],
                EB[:, hp * 256:(hp + 1) * 256], op=ALU.mult),
                waits=[(cx.ss, t_embt)], inc=cx.sv)
            tke = em.emit("v", lambda v, hp=hp: v.tensor_tensor(
                KET[:, hp * 256:(hp + 1) * 256], KT[:, hp * 256:(hp + 1) * 256],
                EMB[:, hp * 256:(hp + 1) * 256], op=ALU.mult), inc=cx.sv)
            tkm = 0
            for cc in range(4):
                tkm = em.emit("v", lambda v, cc=cc, hp=hp: v.tensor_tensor(
                    KEMB[0:64, hp * 512 + cc * 128:hp * 512 + (cc + 1) * 128],
                    KTOK[0:64, cc * 512 + hp * 128:cc * 512 + (hp + 1) * 128],
                    EMBT[0:64, cc * 128:(cc + 1) * 128],
                    op=ALU.mult), waits=[(cx.ss, t_embt)] if cc == 0 else (),
                    inc=cx.sv)
            tok_exp[hp] = (tqe, tke, tkm)
            tkm_prev = tkm

        if phase < 2.5:
            continue
        t_at_evac = 0
        t_oevac = em.token(cx.sv)
        t_o2cp = em.token(cx.ss)
        t_psd0_s = em.token(cx.ss)
        t_supd = [em.token(cx.sv)] * 4
        for cc in range(4):
            c = 4 * sc + cc
            for hp in range(4):
                tqe, tke, tkm = tok_exp[hp]
                tat = 0
                for e in range(2):
                    dstp = PSD1 if e == 0 else PSP
                    tat = em.emit("t", lambda t, cc=cc, hp=hp, e=e, dstp=dstp: t.matmul(
                        dstp[0:64, 0:64],
                        KET[64 * e:64 * (e + 1), hp * 256 + cc * 64:hp * 256 + (cc + 1) * 64],
                        QET[64 * e:64 * (e + 1), hp * 256 + cc * 64:hp * 256 + (cc + 1) * 64],
                        start=True, stop=True),
                        waits=([(cx.sv, em.token(cx.sv)), (cx.ss, t_o2cp)]
                               if e == 0 else ()),
                        inc=cx.st_)
                em.emit("v", lambda v: v.tensor_tensor(
                    ATS[0:64, 0:64], PSD1[0:64, 0:64], cs("c_triu")[0:64, 0:64],
                    op=ALU.mult), waits=[(cx.st_, tat)])
                t_at_evac = em.emit("v", lambda v: v.tensor_tensor(
                    ATS[0:64, 64:128], PSP[0:64, 0:64], cs("c_triu")[0:64, 0:64],
                    op=ALU.mult), inc=cx.sv)
                to = 0
                to2 = 0
                for e in range(2):
                    h = 2 * hp + e
                    to = em.emit("t", lambda t, cc=cc, hp=hp, e=e, h=h: t.matmul(
                        PSD0[0:64, 128 * e:128 * (e + 1)],
                        ATS[0:64, 64 * e:64 * (e + 1)],
                        VTOK[0:64, cc * 1024 + h * 128:cc * 1024 + (h + 1) * 128],
                        start=True, stop=True),
                        waits=([(cx.sv, t_at_evac), (cx.sv, t_oevac),
                                (cx.ss, t_psd0_s)] if e == 0 else ()),
                        inc=cx.st_)
                    dst2 = PSB if e == 0 else PSP
                    c2 = 0 if e == 0 else 128
                    to2 = em.emit("t", lambda t, cc=cc, hp=hp, e=e, dst2=dst2, c2=c2: t.matmul(
                        dst2[0:64, c2:c2 + 128],
                        QET[64 * e:64 * (e + 1), hp * 256 + cc * 64:hp * 256 + (cc + 1) * 64],
                        SHPB[hp][64 * e:64 * (e + 1), :],
                        start=True, stop=True),
                        waits=([(cx.sv, t_supd[hp]), (cx.ss, t_o2cp)]
                               if e == 0 else ()),
                        inc=cx.st_)
                em.emit("s", lambda s: s.copy(OTMP[0:64, 0:128],
                                              PSB[0:64, 0:128]),
                        waits=[(cx.st_, to2)])
                t_o2cp = em.emit("s", lambda s: s.copy(OTMP[0:64, 128:256],
                                                       PSP[0:64, 128:256]),
                                 inc=cx.ss)
                t_oevac = em.emit("v", lambda v, cc=cc, hp=hp: v.tensor_tensor(
                    OSC[0:64, cc * 1024 + hp * 256:cc * 1024 + (hp + 1) * 256],
                    PSD0[0:64, 0:256], OTMP[0:64, 0:256], op=ALU.add),
                    waits=[(cx.st_, to), (cx.ss, t_o2cp)], inc=cx.sv)
                tsm = 0
                for e in range(2):
                    h = 2 * hp + e
                    tsm = em.emit("t", lambda t, cc=cc, hp=hp, e=e, h=h: t.matmul(
                        PSD1[64 * e:64 * (e + 1), 0:GLA_DV],
                        KEMB[0:64, hp * 512 + cc * 128 + 64 * e:hp * 512 + cc * 128 + 64 * (e + 1)],
                        VTOK[0:64, cc * 1024 + h * 128:cc * 1024 + (h + 1) * 128],
                        start=True, stop=True, tile_position=(0, 64 * e)),
                        waits=[(cx.sv, t_at_evac), (cx.sv, t_supd[hp])] if e == 0 else (),
                        inc=cx.st_)
                tu1 = em.emit("v", lambda v, hp=hp: v.tensor_tensor(
                    SHP[hp][:, :], SHP[hp][:, :], PSD1[:, 0:GLA_DV], op=ALU.add),
                    waits=[(cx.st_, tsm)], inc=cx.sv)
                tu2 = em.emit("v", lambda v, hp=hp, cc=cc: v.tensor_scalar(
                    SHP[hp][:, :], SHP[hp][:, :],
                    ELAST[:, hp * 4 + cc:hp * 4 + cc + 1], None, op0=ALU.mult),
                    waits=[(cx.sv, tu1)], inc=cx.sv)
                t_supd[hp] = em.emit("v", lambda v, hp=hp: v.tensor_copy(
                    SHPB[hp][:, :], SHP[hp][:, :]), waits=[(cx.sv, tu2)],
                    inc=cx.sv)

        if phase < 3:
            continue
        t_sq = 0
        for cc in range(4):
            for h in range(GLA_H):
                wsq = [(cx.ss, t_sq)]
                if cc == 0 and h == 0:
                    wsq.append((cx.sv, t_oevac))
                t_sq = em.emit("s", lambda s, cc=cc, h=h: s.activation(
                    SQS[0:64, 0:128], OSC[0:64, cc * 1024 + h * 128:cc * 1024 + (h + 1) * 128],
                    AF.Square, accum_out=SSQ[0:64, cc * 8 + h:cc * 8 + h + 1]),
                    waits=wsq, inc=cx.ss)
        tln = em.emit("s", lambda s: s.activation(LNT[0:64, 0:32],
                                                  SSQ[0:64, 0:32],
                                                  AF.Ln, bias=EPS,
                                                  scale=1.0 / 128),
                      waits=[(cx.ss, t_sq)], inc=cx.ss)
        t_rs = em.emit("s", lambda s: s.activation(RSTD[0:64, 0:32],
                                                   LNT[0:64, 0:32], AF.Exp,
                                                   scale=-0.5),
                       waits=[(cx.ss, tln)], inc=cx.ss)
        t_sil = {}
        for cc in range(4):
            t_sil[cc] = em.emit("s", lambda s, cc=cc: s.activation(
                GTOK[0:64, cc * 1024:(cc + 1) * 1024],
                GTOK[0:64, cc * 1024:(cc + 1) * 1024], AF.Silu), inc=cx.ss)
        tvg = 0
        for cc in range(4):
            for h in range(GLA_H):
                tvg = em.emit("v", lambda v, cc=cc, h=h: v.tensor_scalar(
                    OSC[0:64, cc * 1024 + h * 128:cc * 1024 + (h + 1) * 128],
                    OSC[0:64, cc * 1024 + h * 128:cc * 1024 + (h + 1) * 128],
                    RSTD[0:64, cc * 8 + h:cc * 8 + h + 1], None, op0=ALU.mult),
                    waits=[(cx.ss, t_rs)] if (cc == 0 and h == 0) else (),
                    inc=cx.sv)
            tvg = em.emit("v", lambda v, cc=cc: v.tensor_tensor(
                OSC[0:64, cc * 1024:(cc + 1) * 1024],
                OSC[0:64, cc * 1024:(cc + 1) * 1024],
                GTOK[0:64, cc * 1024:(cc + 1) * 1024], op=ALU.mult),
                waits=[(cx.ss, t_sil[cc]), (cx.sv, tvg)], inc=cx.sv)
        tok_wo = em.dma("p", WVG[:, :].rearrange("p (a c) -> p a c", a=ND),
                        w["Wo"].rearrange("(a p) c -> p a c", p=128),
                        waits=[(cx.st_, em.token(cx.st_))], inc=cx.swvg)
        t_tr = 0
        for cc in range(4):
            for kd in range(ND):
                t_tr = th.transpose(
                    OSC[0:64, cc * 1024 + kd * 128:cc * 1024 + (kd + 1) * 128],
                    OGT[:, kd * SC_TOK + cc * 64:kd * SC_TOK + (cc + 1) * 64],
                    waits=[(cx.sv, tvg)])
        t_nw = 0
        for kd in range(ND):
            t_nw = em.emit("v", lambda v, kd=kd: v.tensor_scalar(
                OGT[:, kd * SC_TOK:(kd + 1) * SC_TOK],
                OGT[:, kd * SC_TOK:(kd + 1) * SC_TOK],
                bufs["NW"][:, 0:1], None, op0=ALU.mult),
                waits=[(cx.ss, t_tr)] if kd == 0 else (), inc=cx.sv)
        for tt in range(2):
            stt = st0 + tt
            for oc in range(2):
                pw = 0
                for kd in range(ND):
                    pw = em.emit("t", lambda t, kd=kd, tt=tt, oc=oc: t.matmul(
                        PSP[:, 0:512],
                        OGT[:, kd * SC_TOK + tt * 128:kd * SC_TOK + (tt + 1) * 128],
                        WVG[:, kd * 1024 + oc * 512:kd * 1024 + (oc + 1) * 512],
                        start=(kd == 0), stop=(kd == ND - 1)),
                        waits=([(cx.sv, t_nw), (cx.swvg, tok_wo),
                                (cx.sv, tok_sc_done)] if kd == 0 else ()),
                        inc=cx.st_ if kd == ND - 1 else None)
                tok_wo_mm = pw
                tok_sc_done = em.emit("v", lambda v, stt=stt, oc=oc: v.tensor_tensor(
                    XRES[:, stt * D + oc * 512:stt * D + (oc + 1) * 512],
                    XRES[:, stt * D + oc * 512:stt * D + (oc + 1) * 512],
                    PSP[:, 0:512], op=ALU.add),
                    waits=[(cx.st_, pw)], inc=cx.sv)
        tok_out_evac = tok_sc_done
    return tok_sc_done


def T1T_ps(cx):
    return cx.ps["l0f_psd1"]



NH = 16
HD = 64
TP = T + 2


def emit_ln_affine_tile(cx, x_ap, out_ap, scr, wrep, brep, waits=()):
    em = cx.em
    tv, ts = emit_ln_tile(cx, x_ap, out_ap, scr, waits=waits)
    tv = em.emit("v", lambda v: v.tensor_tensor(out_ap, out_ap, wrep[:, :],
                                                op=ALU.mult),
                 waits=[(cx.sv, tv)], inc=cx.sv)
    tv = em.emit("v", lambda v: v.tensor_tensor(out_ap, out_ap, brep[:, :],
                                                op=ALU.add),
                 waits=[(cx.sv, tv)], inc=cx.sv)
    return tv


def stage_attn(cx, th, xres_hbm, bufs, w, mode, tag, hbm_xinner=None,
               start_tok=(), xres_out=None):
    em = cx.em
    cs = cx.cst
    inner = (mode == "inner")
    HCT = bufs["HCT"]
    KTR = bufs["KTR"]
    VAUG = bufs["VAUG"]
    QTR = bufs["QTR"]
    T0 = bufs["T0"]
    TSW = bufs["TSW"]
    PT = bufs["PT"]
    YT = bufs["YT"]
    ZROW = bufs["ZROW"]
    ZC = bufs["ZC"]
    BCOL = bufs["BCOL"]
    WB1 = bufs["WB1"]
    WBO = bufs["WBO"]
    RZX = bufs["RZX"]
    SQB = bufs["SQB"]
    CVO = bufs["CVO"]
    Htmp = bufs["HTMP"]
    XT = bufs["XT"]
    XT2 = bufs["XT2"]
    scr = ln_scratch(cx, "z")
    sdma = cx.em.sem("sw_" + tag)
    sdma2 = cx.em.sem("swb_" + tag)
    sdo = cx.em.sem("swo_" + tag)
    sxa = cx.em.sem("sxa_" + tag)
    sxb = cx.em.sem("sxb_" + tag)
    PS_S = [cx.ps["PS_G"], cx.ps["PS_U"]]
    PS_PV = cx.ps["l0f_psd0"]
    PS_X = cx.ps["l0f_psd1"]

    if inner:
        q_ap = w["attn_w"][:, 0:1024]
        k_ap = w["attn_w"][:, 1024:2048]
        v_ap = w["attn_w"][:, 2048:3072]
        o_ap = w["proj_w"]
    else:
        q_ap = w["Wq"]
        k_ap = w["Wk"]
        v_ap = w["Wv"]
        o_ap = w["Wo"]

    def load_w(ap, wait_st):
        return em.dma("p", WB1[:, :].rearrange("p (a c) -> p a c", a=ND),
                      ap.rearrange("(a p) c -> p a c", p=128),
                      waits=[(cx.st_, wait_st)], inc=sdma)

    tok_small = 0
    if inner:
        tok_small = em.dma("p", bufs["CW"][:, :].rearrange("p (a c) -> p a c", a=ND),
                           w["conv_w"].rearrange("(a p) one c -> p a (one c)", p=128),
                           waits=list(start_tok), inc=sdma2)
        tok_small = em.dma("p", bufs["LN1W"][0:1, :], w["ln1_w"].unsqueeze(0),
                           inc=sdma2)
        tok_small = em.dma("p", bufs["LN1B"][0:1, :], w["ln1_b"].unsqueeze(0),
                           inc=sdma2)
        reps = [("LN1W", "LN1WR"), ("LN1B", "LN1BR")]
    else:
        for a in range(ND):
            tok_small = em.dma("p", bufs["BQ"][:, a:a + 1],
                               w["bq"][a * 128:(a + 1) * 128].unsqueeze(1),
                               waits=list(start_tok) if a == 0 else (),
                               inc=sdma2)
            tok_small = em.dma("p", bufs["BK"][:, a:a + 1],
                               w["bk"][a * 128:(a + 1) * 128].unsqueeze(1),
                               inc=sdma2)
        tok_small = em.dma("p", bufs["BVR"][0:1, :], w["bv"].unsqueeze(0),
                           inc=sdma2)
        tok_small = em.dma("p", bufs["BOR"][0:1, :], w["bo"].unsqueeze(0),
                           inc=sdma2)
        reps = [("BVR", "BVRR"), ("BOR", "BORR")]
    tprev = 0
    for (src, dst) in reps:
        for oc in range(2):
            tt = em.emit("t", lambda t, src=src, oc=oc: t.matmul(
                PS_X[:, 0:512], cs("c_ones_row")[0:1, :],
                bufs[src][0:1, oc * 512:(oc + 1) * 512], start=True, stop=True),
                waits=[(sdma2, tok_small), (cx.sqc, cx.tok_consts),
                       (cx.ss, tprev)] + list(start_tok), inc=cx.st_)
            tprev = em.emit("s", lambda s, dst=dst, oc=oc: s.copy(
                bufs[dst][:, oc * 512:(oc + 1) * 512], PS_X[:, 0:512]),
                waits=[(cx.st_, tt)], inc=cx.ss)
    tok_rep = tprev

    tpad = 0
    for kd in range(ND):
        tpad = em.emit("v", lambda v, kd=kd: v.memset(
            HCT[:, kd * TP:kd * TP + 2], 0.0), waits=list(start_tok),
            inc=cx.sv)
    te_h = tok_rep
    tln_v = tpad
    for st in range(NT):
        tx = em.dma("q", XT[:, 0:1024], xres_hbm.ap()[st * 128:(st + 1) * 128, :],
                    waits=[(cx.sv, tln_v)] + list(start_tok), inc=sxa)
        if inner:
            tln_v, _ = emit_ln_tile(cx, XT[:, 0:1024], XT2[:, 0:1024], scr,
                                    waits=[(sxa, tx)])
            tln_v = emit_ln_affine_tile(cx, XT2[:, 0:1024], Htmp[:, 0:D], scr,
                                        bufs["LN1WR"], bufs["LN1BR"],
                                        waits=[(cx.sv, tln_v),
                                               (cx.ss, tok_rep)])
        else:
            tln_v, _ = emit_ln_tile(cx, XT[:, 0:1024], Htmp[:, 0:D], scr,
                                    waits=[(sxa, tx)])
        for kd in range(ND):
            te_h = th.transpose(Htmp[:, kd * 128:(kd + 1) * 128],
                                HCT[:, kd * TP + 2 + st * 128:
                                    kd * TP + 2 + (st + 1) * 128],
                                waits=[(cx.sv, tln_v), (cx.sv, tpad)])
    if inner:
        tcb = 0
        for kd in range(ND):
            w2 = bufs["CW"][:, kd * 3 + 2:kd * 3 + 3]
            w1 = bufs["CW"][:, kd * 3 + 1:kd * 3 + 2]
            w0 = bufs["CW"][:, kd * 3 + 0:kd * 3 + 1]
            tv = em.emit("v", lambda v, kd=kd, w2=w2: v.tensor_scalar(
                CVO[:, 0:T], HCT[:, kd * TP + 2:kd * TP + 2 + T], w2, None,
                op0=ALU.mult),
                waits=[(cx.ss, te_h), (sdma2, tok_small), (cx.sv, tcb)],
                inc=cx.sv)
            tv = em.emit("v", lambda v, kd=kd, w1=w1: v.scalar_tensor_tensor(
                CVO[:, 0:T], HCT[:, kd * TP + 1:kd * TP + 1 + T], w1,
                CVO[:, 0:T], op0=ALU.mult, op1=ALU.add),
                waits=[(cx.sv, tv)], inc=cx.sv)
            tv = em.emit("v", lambda v, kd=kd, w0=w0: v.scalar_tensor_tensor(
                CVO[:, 0:T], HCT[:, kd * TP + 0:kd * TP + 0 + T], w0,
                CVO[:, 0:T], op0=ALU.mult, op1=ALU.add),
                waits=[(cx.sv, tv)], inc=cx.sv)
            tcb = em.emit("v", lambda v, kd=kd: v.tensor_copy(
                HCT[:, kd * TP + 2:kd * TP + 2 + T], CVO[:, 0:T]),
                waits=[(cx.sv, tv)], inc=cx.sv)
        te_h = em.emit("s", lambda s: s.copy(ZROW[0:1, 0:1], CVO[0:1, 0:1]),
                       waits=[(cx.sv, tcb)], inc=cx.ss)

    rbA = [PS_X, PS_PV]
    rbB = [PS_S[0], PS_S[1]]
    prevA = [0, 0]
    prevB = [0, 0]

    def rope_pass(dst, dst_off, bias_col, ft, ch, prev_v, wtok, slot):
        i = slot % 2
        psA, psB = rbA[i], rbB[i]
        T0i = T0[:, 0:512] if i == 0 else PT[:, 0:512]
        TSWi = TSW[:, 0:512] if i == 0 else PT[:, 512:1024]
        pe = 0
        for kd in range(ND):
            pe = em.emit("t", lambda t, kd=kd, ft=ft, ch=ch, psA=psA: t.matmul(
                psA[:, 0:512],
                WB1[:, kd * 1024 + ft * 128:kd * 1024 + (ft + 1) * 128],
                HCT[:, kd * TP + 2 + ch * 512:kd * TP + 2 + (ch + 1) * 512],
                start=(kd == 0), stop=(kd == ND - 1)),
                waits=([(cx.ss, prevA[i]), (cx.ss, te_h), (cx.sv, prev_v),
                        (sdma, wtok)] if kd == 0 else ()),
                inc=cx.st_ if kd == ND - 1 else None)
        if bias_col is not None:
            t0 = em.emit("s", lambda s, ft=ft, bias_col=bias_col, psA=psA, T0i=T0i: s.activation(
                T0i, psA[:, 0:512], AF.Identity,
                bias=bias_col[:, ft:ft + 1]),
                waits=[(cx.st_, pe)], inc=cx.ss)
        else:
            t0 = em.emit("s", lambda s, psA=psA, T0i=T0i: s.copy(
                T0i, psA[:, 0:512]), waits=[(cx.st_, pe)], inc=cx.ss)
        prevA[i] = t0
        tt = em.emit("t", lambda t, psB=psB, T0i=T0i: t.matmul(
            psB[:, 0:512], cs("c_rot128")[:, :], T0i,
            start=True, stop=True),
            waits=[(cx.ss, t0), (cx.ss, prevB[i])], inc=cx.st_)
        tsw = em.emit("s", lambda s, psB=psB, TSWi=TSWi: s.copy(
            TSWi, psB[:, 0:512]), waits=[(cx.st_, tt)], inc=cx.ss)
        prevB[i] = tsw
        tv = em.emit("v", lambda v, ch=ch, T0i=T0i: v.tensor_tensor(
            T0i, T0i, cs("c_cos2")[:, ch * 512:(ch + 1) * 512],
            op=ALU.mult), waits=[(cx.ss, tsw)], inc=cx.sv)
        tv2 = em.emit("v", lambda v, ch=ch, TSWi=TSWi: v.tensor_tensor(
            TSWi, TSWi,
            cs("c_sinrot")[:, ch * 512:(ch + 1) * 512],
            op=ALU.mult), inc=cx.sv)
        tv3 = em.emit("v", lambda v, dst=dst, dst_off=dst_off, T0i=T0i, TSWi=TSWi: v.tensor_tensor(
            dst[:, dst_off:dst_off + 512], T0i, TSWi,
            op=ALU.add), waits=[(cx.sv, tv2), (cx.sv, tv)], inc=cx.sv)
        return tv3

    def norms_max(src, src_stride, out_base, first_waits, nch=4):
        tred = 0
        tmm = 0
        tsq = 0
        for ch in range(nch):
            for kd in range(ND):
                wq_ = [(cx.st_, tmm), (cx.ss, tsq)]
                if ch == 0 and kd == 0:
                    wq_ += list(first_waits)
                tsq = em.emit("s", lambda s, kd=kd, ch=ch, src=src, ss_=src_stride: s.activation(
                    SQB[:, 0:512],
                    src[:, kd * ss_ + ch * 512:kd * ss_ + ch * 512 + 512],
                    AF.Square), waits=wq_, inc=cx.ss)
                for rb in (0, 64):
                    h = 2 * kd + rb // 64
                    tmm = em.emit("t", lambda t, rb=rb: t.matmul(
                        PS_PV[0:1, 0:512], cs("c_ones_col")[rb:rb + 64, 0:1],
                        SQB[rb:rb + 64, 0:512], start=True, stop=True),
                        waits=[(cx.ss, tsq), (cx.sv, tred)], inc=cx.st_)
                    tred = em.emit("v", lambda v, h=h, ch=ch, nch=nch: v.reduce_max(
                        ZC[0:1, h * nch + ch:h * nch + ch + 1],
                        PS_PV[0:1, 0:512],
                        axis=AX.X), waits=[(cx.st_, tmm)], inc=cx.sv)
        for h in range(NH):
            tred = em.emit("v", lambda v, h=h, ob=out_base, nch=nch: v.reduce_max(
                ZC[0:1, ob + h:ob + h + 1], ZC[0:1, h * nch:h * nch + nch],
                axis=AX.X), waits=[(cx.sv, tred)], inc=cx.sv)
        return tred

    tw = load_w(k_ap, 0)
    tok_rope_v = 0
    prevV = [0, 0]
    slot = 0
    for ft in range(ND):
        for ch in range(4):
            i = slot % 2
            tok_rope_v = rope_pass(
                KTR, ft * T + ch * 512,
                None if inner else bufs["BK"], ft, ch, prevV[i], tw, slot)
            prevV[i] = tok_rope_v
            slot += 1
    tok_ps = em.token(cx.ss)
    tok_kn = norms_max(KTR, T, 64, [(cx.sv, tok_rope_v), (cx.ss, tok_ps)])

    tvm = em.emit("v", lambda v: v.memset(VAUG[:, :], 1.0), inc=cx.sv)
    tw = load_w(v_ap, em.token(cx.st_))
    tvb = [em.token(cx.ss), 0]
    pv_first = True
    for st in range(NT):
        for oc in range(2):
            i = (st * 2 + oc) % 2
            bank = [PS_X, PS_PV][i]
            wv0 = [(sdma, tw), (cx.ss, tvb[i])]
            if i == 1 and pv_first:
                wv0.append((cx.sv, tok_kn))
                pv_first = False
            for kd in range(ND):
                em.emit("t", lambda t, kd=kd, st=st, oc=oc, bank=bank: t.matmul(
                    bank[:, 0:512],
                    HCT[:, kd * TP + 2 + st * 128:kd * TP + 2 + (st + 1) * 128],
                    WB1[:, kd * 1024 + oc * 512:kd * 1024 + (oc + 1) * 512],
                    start=(kd == 0), stop=(inner and kd == ND - 1)),
                    waits=(wv0 if kd == 0 else ()),
                    inc=cx.st_ if (inner and kd == ND - 1) else None)
            if not inner:
                em.emit("t", lambda t, oc=oc, bank=bank: t.matmul(
                    bank[:, 0:512], cs("c_ones_row")[0:1, :],
                    bufs["BVRR"][0:1, oc * 512:(oc + 1) * 512],
                    start=False, stop=True), inc=cx.st_)
            pe = em.token(cx.st_)
            tvb[i] = em.emit("s", lambda s, st=st, oc=oc, bank=bank: s.copy(
                VAUG[:, st * 1040:(st + 1) * 1040].rearrange(
                    "p (h c) -> p h c", c=65)[:, oc * 8:(oc + 1) * 8, 0:64],
                bank[:, 0:512].rearrange("p (h c) -> p h c", c=64)),
                waits=[(cx.st_, pe), (cx.sv, tvm)], inc=cx.ss)
    t_vevac = max(tvb)
    prevA[0] = prevA[1] = em.token(cx.ss)
    prevB[0] = prevB[1] = em.token(cx.ss)
    prevV[0] = prevV[1] = em.token(cx.sv)

    mmask = cs("c_mmask")
    tok_q_v = 0
    tok_ps = t_vevac
    t_exp = t_vevac
    t_ptg = [0, 0]
    t_pvevac = t_vevac
    t_yn = 0
    tok_xe = tln_v if inner else 0
    t_store = 0
    tw_q = load_w(q_ap, em.token(cx.st_))
    for qc in range(4):
        for ft in range(ND):
            i = slot % 2
            tok_q_v = rope_pass(QTR, ft * 512,
                                None if inner else bufs["BQ"],
                                ft, qc, prevV[i], tw_q, slot)
            prevV[i] = tok_q_v
            slot += 1
        tok_ps = em.token(cx.ss)
        t_qn = norms_max(QTR, 512, 80, [(cx.sv, tok_q_v), (cx.ss, tok_ps)],
                         nch=1)
        tb1 = em.emit("v", lambda v: v.tensor_tensor(
            ZC[0:1, 96:112], ZC[0:1, 64:80], ZC[0:1, 80:96], op=ALU.mult),
            waits=[(cx.sv, t_qn), (cx.sv, tok_kn)], inc=cx.sv)
        tb2 = em.emit("s", lambda s: s.activation(ZC[0:1, 112:128],
                                                  ZC[0:1, 96:112], AF.Sqrt),
                      waits=[(cx.sv, tb1)], inc=cx.ss)
        tb2 = em.emit("v", lambda v: v.tensor_scalar(
            ZC[0:1, 112:128], ZC[0:1, 112:128], -0.125, 12.0,
            op0=ALU.mult, op1=ALU.add), waits=[(cx.ss, tb2)], inc=cx.sv)
        tb3 = em.emit("t", lambda t: t.matmul(
            PS_X[0:128, 0:16], cs("c_ones_row32")[0:1, :], ZC[0:1, 112:128],
            start=True, stop=True),
            waits=[(cx.sv, tb2), (cx.ss, tok_ps)], inc=cx.st_)
        t_bc = em.emit("v", lambda v: v.tensor_copy(BCOL[:, 0:16],
                                                    PS_X[0:128, 0:16]),
                       waits=[(cx.st_, tb3)], inc=cx.sv)
        tok_ps_v = t_bc
        nkt = 4 * qc + 4
        for h in range(NH):
            ftile = h // 2
            rb = 64 * (h % 2)
            for kt in range(nkt):
                par = kt % 2
                psS = PS_S[par]
                wssc = [(cx.ss, t_exp)]
                if kt < 2:
                    wssc += [(cx.sv, tok_q_v), (cx.sv, t_bc), (cx.sv, tok_kn)]
                tsc = em.emit("t", lambda t, kt=kt, ftile=ftile, rb=rb, psS=psS: t.matmul(
                    psS[:, 0:512],
                    KTR[rb:rb + 64, ftile * T + kt * 128:ftile * T + (kt + 1) * 128],
                    QTR[rb:rb + 64, ftile * 512:(ftile + 1) * 512],
                    start=True, stop=True),
                    waits=wssc, inc=cx.st_)
                wex = [(cx.st_, tsc)]
                if t_ptg[par]:
                    wex.append((cx.st_, t_ptg[par]))
                t_exp = em.emit("s", lambda s, h=h, par=par, psS=psS: s.activation(
                    PT[:, par * 512:(par + 1) * 512], psS[:, 0:512],
                    AF.Exp, bias=BCOL[:, h:h + 1], scale=0.125),
                    waits=wex, inc=cx.ss)
                tlast, teng = t_exp, "s"
                if kt >= 4 * qc:
                    j = kt - 4 * qc
                    tlast = em.emit("v", lambda v, par=par, j=j: v.tensor_tensor(
                        PT[:, par * 512:(par + 1) * 512],
                        PT[:, par * 512:(par + 1) * 512],
                        mmask[:, j * 512:(j + 1) * 512], op=ALU.mult),
                        waits=[(cx.ss, t_exp)], inc=cx.sv)
                    teng = "v"
                wpv = [((cx.sv if teng == "v" else cx.ss), tlast)]
                if kt == 0:
                    wpv.append((cx.ss, t_pvevac))
                tpv = em.emit("t", lambda t, h=h, kt=kt, par=par, nkt=nkt: t.matmul(
                    PS_PV[0:65, 0:512],
                    VAUG[:, kt * 1040 + h * 65:kt * 1040 + (h + 1) * 65],
                    PT[:, par * 512:(par + 1) * 512],
                    start=(kt == 0), stop=(kt == nkt - 1)),
                    waits=wpv, inc=cx.st_)
                t_ptg[par] = tpv
            t_y = em.emit("s", lambda s, h=h: s.copy(
                YT[0:64, h * 512:(h + 1) * 512], PS_PV[0:64, 0:512]),
                waits=[(cx.st_, t_ptg[(nkt - 1) % 2]), (cx.sv, t_yn)],
                inc=cx.ss)
            t_z = em.emit("s", lambda s: s.copy(ZROW[64:65, 0:512],
                                                PS_PV[64:65, 0:512]),
                          inc=cx.ss)
            t_pvevac = t_z
            t_z = em.emit("s", lambda s: s.activation(
                ZROW[64:65, 0:512], ZROW[64:65, 0:512], AF.Ln),
                waits=[(cx.ss, t_z)], inc=cx.ss)
            t_z = em.emit("s", lambda s: s.activation(
                ZROW[64:65, 0:512], ZROW[64:65, 0:512], AF.Exp, scale=-1.0),
                waits=[(cx.ss, t_z)], inc=cx.ss)
            t_ze = em.emit("t", lambda t: t.matmul(
                PS_X[0:64, 0:512], cs("c_ones65")[64:65, 0:64],
                ZROW[64:65, 0:512], start=True, stop=True),
                waits=[(cx.ss, t_z), (cx.ss, tok_ps), (cx.sv, tok_ps_v)],
                inc=cx.st_)
            t_rzx = em.emit("s", lambda s: s.copy(RZX[0:64, 0:512],
                                                  PS_X[0:64, 0:512]),
                            waits=[(cx.st_, t_ze)], inc=cx.ss)
            tok_ps = t_rzx
            t_yn = em.emit("v", lambda v, h=h: v.tensor_tensor(
                YT[0:64, h * 512:(h + 1) * 512],
                YT[0:64, h * 512:(h + 1) * 512], RZX[0:64, 0:512],
                op=ALU.mult), waits=[(cx.ss, t_rzx)], inc=cx.sv)
        for tt2 in range(2):
            for half in range(2):
                tw_o = em.dma("p", WBO[0:64, :].rearrange("p (a c) -> p a c", a=8),
                              o_ap.rearrange("(a p) c -> p a c", p=64)[:, 8 * half:8 * (half + 1), :]
                              if False else
                              o_ap.rearrange("(hh p) c -> p hh c", p=64)[:, 8 * half:8 * (half + 1), :],
                              waits=[(cx.st_, em.token(cx.st_))], inc=sdo)
                for i in range(2):
                    for oc in range(2):
                        bank = [cx.ps["PS_G"], cx.ps["PS_U"],
                                cx.ps["l0f_psd0"], cx.ps["l0f_psd1"]][i * 2 + oc]
                        for hh in range(8):
                            h = half * 8 + hh
                            em.emit("t", lambda t, h=h, hh=hh, tt2=tt2, i=i, oc=oc, bank=bank, half=half: t.matmul(
                                bank[:, 0:512],
                                YT[0:64, h * 512 + (tt2 * 2 + i) * 128:h * 512 + (tt2 * 2 + i + 1) * 128],
                                WBO[0:64, hh * 1024 + oc * 512:hh * 1024 + (oc + 1) * 512],
                                start=(h == 0), stop=(h == NH - 1)),
                                waits=([(cx.sv, t_yn), (sdo, tw_o),
                                        (cx.ss, em.token(cx.ss)),
                                        (cx.sv, tok_xe)] if hh == 0 else ()),
                                inc=cx.st_ if hh == 7 else None)
            pw = em.token(cx.st_)
            for i in range(2):
                stt = qc * 4 + tt2 * 2 + i
                tx2 = em.dma("q", XT2[:, 0:1024],
                             xres_hbm.ap()[stt * 128:(stt + 1) * 128, :],
                             waits=[(cx.sxb_guard, t_store)] if False else
                                   [(sxb, t_store), (cx.sv, tok_xe)],
                             inc=sxb)
                if inner:
                    tvl, _ = emit_ln_tile(cx, XT2[:, 0:1024], XT[:, 0:1024],
                                          scr, waits=[(sxb, tx2)])
                    for oc in range(2):
                        bank = [cx.ps["PS_G"], cx.ps["PS_U"],
                                cx.ps["l0f_psd0"], cx.ps["l0f_psd1"]][i * 2 + oc]
                        tok_xe = em.emit("v", lambda v, oc=oc, bank=bank: v.tensor_tensor(
                            bufs["XIN"][:, oc * 512:(oc + 1) * 512],
                            XT[:, oc * 512:(oc + 1) * 512], bank[:, 0:512],
                            op=ALU.add),
                            waits=[(cx.st_, pw), (cx.sv, tvl)], inc=cx.sv)
                    t_store = em.dma("q",
                                     hbm_xinner.ap()[stt * 128:(stt + 1) * 128, :],
                                     bufs["XIN"][:, 0:1024],
                                     waits=[(cx.sv, tok_xe)], inc=sxb)
                else:
                    for oc in range(2):
                        bank = [cx.ps["PS_G"], cx.ps["PS_U"],
                                cx.ps["l0f_psd0"], cx.ps["l0f_psd1"]][i * 2 + oc]
                        tok_xe = em.emit("v", lambda v, oc=oc, bank=bank: v.scalar_tensor_tensor(
                            XT2[:, oc * 512:(oc + 1) * 512], bank[:, 0:512],
                            1.0, XT2[:, oc * 512:(oc + 1) * 512],
                            op0=ALU.mult, op1=ALU.add),
                            waits=[(sxb, tx2), (cx.st_, pw),
                                   (cx.sv, tok_xe)], inc=cx.sv)
                    tok_xe = em.emit("v", lambda v: v.tensor_tensor(
                        XT2[:, 0:1024], XT2[:, 0:1024], bufs["BORR"][:, 0:1024],
                        op=ALU.add), waits=[(cx.sv, tok_xe)], inc=cx.sv)
                    t_store = em.dma("q",
                                     (xres_out if xres_out is not None
                                      else xres_hbm).ap()[
                                         stt * 128:(stt + 1) * 128, :],
                                     XT2[:, 0:1024],
                                     waits=[(cx.sv, tok_xe)], inc=sxb)
        prevA[0] = prevA[1] = em.token(cx.ss)
        prevB[0] = prevB[1] = em.token(cx.ss)
        prevV[0] = prevV[1] = em.token(cx.sv)
        tok_q_v = tok_xe
    em.emit("q", lambda q: None, waits=[(sxb, em.token(sxb))])
    return tok_xe



def stage_moe(cx, th, xres_hbm, xinner_hbm, bufs, w, tag, start_tok=(),
              xres_out=None):
    em = cx.em
    cs = cx.cst
    HCT = bufs["HCT"]
    Htmp = bufs["HTMP"]
    XT = bufs["XT"]
    XT2 = bufs["XT2"]
    ZBUF = bufs["ZBUF"]
    WVG = bufs["WVG"]
    WVV = bufs["WVV"]
    PROJB = bufs["PROJB"]
    YAC = bufs["YAC"]
    FE = bufs["FE"]
    RWB = bufs["RWB"]
    SGT = bufs["SGT"]
    scr = ln_scratch(cx, "z")
    smx = em.sem("smx_" + tag)
    smo = em.sem("smo_" + tag)
    smr = em.sem("smr_" + tag)
    swv = em.sem("swv_" + tag)
    spj = em.sem("spj_" + tag)
    PS_G = cx.ps["PS_G"]
    PS_U = cx.ps["PS_U"]
    YPS = [cx.ps["l0f_psd0"], cx.ps["l0f_psd1"]]

    def wv_ap(e, lo):
        return w["expert_wv"][e:e + 1, :, lo:lo + 1024].rearrange(
            "one (a p) n -> p (one a) n", p=128)

    def pj_ap(e, g):
        return w["expert_proj"][e:e + 1, g * 1024:(g + 1) * 1024, :].rearrange(
            "one (a p) c -> p (one a) c", p=128)

    tok_small = em.dma("p", bufs["LN2W"][0:1, :], w["ln2_w"].unsqueeze(0),
                       waits=list(start_tok), inc=smr)
    tok_small = em.dma("p", bufs["LN2B"][0:1, :], w["ln2_b"].unsqueeze(0),
                       inc=smr)
    tok_small = em.dma("p", RWB[:, :].rearrange("p (a e) -> p a e", a=ND),
                       w["router_w"].rearrange("(a p) e -> p a e", p=128),
                       inc=smr)
    tprev = 0
    for (src, dst) in (("LN2W", "LN2WR"), ("LN2B", "LN2BR")):
        for oc in range(2):
            tt = em.emit("t", lambda t, src=src, oc=oc: t.matmul(
                PS_G[:, 0:512], cs("c_ones_row")[0:1, :],
                bufs[src][0:1, oc * 512:(oc + 1) * 512], start=True, stop=True),
                waits=[(smr, tok_small), (cx.sqc, cx.tok_consts),
                       (cx.ss, tprev)] + list(start_tok), inc=cx.st_)
            tprev = em.emit("s", lambda s, dst=dst, oc=oc: s.copy(
                bufs[dst][:, oc * 512:(oc + 1) * 512], PS_G[:, 0:512]),
                waits=[(cx.st_, tt)], inc=cx.ss)
    tok_rep = tprev

    tln_v = 0
    te_h = tok_rep
    for st in range(NT):
        tx = em.dma("q", XT[:, 0:1024],
                    xinner_hbm.ap()[st * 128:(st + 1) * 128, :],
                    waits=[(cx.sv, tln_v)] + list(start_tok), inc=smx)
        tln_v = emit_ln_affine_tile(cx, XT[:, 0:1024], Htmp[:, 0:D], scr,
                                    bufs["LN2WR"], bufs["LN2BR"],
                                    waits=[(smx, tx), (cx.ss, tok_rep)])
        for kd in range(ND):
            te_h = th.transpose(Htmp[:, kd * 128:(kd + 1) * 128],
                                HCT[:, kd * TP + 2 + st * 128:
                                    kd * TP + 2 + (st + 1) * 128],
                                waits=[(cx.sv, tln_v)])

    tfe = 0
    tmr = 0
    for st in range(NT):
        for kd in range(ND):
            tmr = em.emit("t", lambda t, st=st, kd=kd: t.matmul(
                PS_G[:, 0:4],
                HCT[:, kd * TP + 2 + st * 128:kd * TP + 2 + (st + 1) * 128],
                RWB[:, kd * 4:(kd + 1) * 4],
                start=(kd == 0), stop=(kd == ND - 1)),
                waits=([(cx.ss, te_h), (smr, tok_small), (cx.sv, tfe)]
                       if kd == 0 else ()),
                inc=cx.st_ if kd == ND - 1 else None)
        tfe = em.emit("v", lambda v: v.reduce_max(
            FE[:, 64:65], PS_G[:, 0:4], axis=AX.X),
            waits=[(cx.st_, tmr)], inc=cx.sv)
        for e in range(4):
            tfe = em.emit("v", lambda v, st=st, e=e: v.tensor_tensor(
                FE[:, st * 4 + e:st * 4 + e + 1], PS_G[:, e:e + 1],
                FE[:, 64:65], op=ALU.is_equal),
                waits=[(cx.sv, tfe)], inc=cx.sv)
        tfe = em.emit("v", lambda v, st=st: v.tensor_scalar(
            FE[:, 65:66], FE[:, st * 4:st * 4 + 1], -1.0, 1.0,
            op0=ALU.mult, op1=ALU.add), waits=[(cx.sv, tfe)], inc=cx.sv)
        for e in (1, 2, 3):
            tfe = em.emit("v", lambda v, st=st, e=e: v.tensor_tensor(
                FE[:, st * 4 + e:st * 4 + e + 1],
                FE[:, st * 4 + e:st * 4 + e + 1], FE[:, 65:66], op=ALU.mult),
                waits=[(cx.sv, tfe)], inc=cx.sv)
            if e < 3:
                tfe = em.emit("v", lambda v, st=st, e=e: v.tensor_tensor(
                    FE[:, 65:66], FE[:, 65:66],
                    FE[:, st * 4 + e:st * 4 + e + 1], op=ALU.subtract),
                    waits=[(cx.sv, tfe)], inc=cx.sv)

    tsil = 0
    tz = 0
    typs = [0, 0]
    tgv_done = 0
    tpm_prev = 0
    for e in range(4):
        for g in range(4):
            w0 = ([(cx.sv, tfe), (cx.ss, em.token(cx.ss))]
                  if (e == 0 and g == 0) else [])
            twv = em.dma("p", WVG[:, :].rearrange("p (a n) -> p a n", a=ND),
                         wv_ap(e, g * 1024),
                         waits=[(cx.st_, tgv_done)] + w0, inc=swv)
            twv = em.dma("p", WVV[:, :].rearrange("p (a n) -> p a n", a=ND),
                         wv_ap(e, 4096 + g * 1024), inc=swv)
            tpj = em.dma("p", PROJB[:, :].rearrange("p (a c) -> p a c", a=ND),
                         pj_ap(e, g),
                         waits=[(cx.st_, tpm_prev)] + w0, inc=spj)
            for jt in range(8):
                for ch in range(4):
                    for kd in range(ND):
                        em.emit("t", lambda t, jt=jt, ch=ch, kd=kd: t.matmul(
                            PS_G[:, 0:512],
                            WVG[:, kd * 1024 + jt * 128:kd * 1024 + (jt + 1) * 128],
                            HCT[:, kd * TP + 2 + ch * 512:kd * TP + 2 + (ch + 1) * 512],
                            start=(kd == 0), stop=(kd == ND - 1)),
                            waits=([(swv, twv), (cx.ss, tsil), (cx.sv, tfe)]
                                   if kd == 0 else ()),
                            inc=cx.st_ if kd == ND - 1 else None)
                    tgm = em.token(cx.st_)
                    for kd in range(ND):
                        em.emit("t", lambda t, jt=jt, ch=ch, kd=kd: t.matmul(
                            PS_U[:, 0:512],
                            WVV[:, kd * 1024 + jt * 128:kd * 1024 + (jt + 1) * 128],
                            HCT[:, kd * TP + 2 + ch * 512:kd * TP + 2 + (ch + 1) * 512],
                            start=(kd == 0), stop=(kd == ND - 1)),
                            waits=([(swv, twv), (cx.sv, tz)] if kd == 0 else ()),
                            inc=cx.st_ if kd == ND - 1 else None)
                    tvm = em.token(cx.st_)
                    tsil = em.emit("s", lambda s: s.activation(
                        SGT[:, 0:512], PS_G[:, 0:512], AF.Silu),
                        waits=[(cx.st_, tgm), (cx.sv, tz)], inc=cx.ss)
                    tz = em.emit("v", lambda v, jt=jt, ch=ch: v.tensor_tensor(
                        ZBUF[:, jt * 2048 + ch * 512:jt * 2048 + (ch + 1) * 512],
                        SGT[:, 0:512], PS_U[:, 0:512], op=ALU.mult),
                        waits=[(cx.ss, tsil), (cx.st_, tvm)], inc=cx.sv)
            tgv_done = em.token(cx.st_)
            for st in range(NT):
                for oc in range(2):
                    b = oc
                    for jt in range(8):
                        em.emit("t", lambda t, jt=jt, st=st, oc=oc, b=b: t.matmul(
                            YPS[b][:, 0:512],
                            ZBUF[:, jt * 2048 + st * 128:jt * 2048 + (st + 1) * 128],
                            PROJB[:, jt * 1024 + oc * 512:jt * 1024 + (oc + 1) * 512],
                            start=(jt == 0), stop=(jt == ND - 1)),
                            waits=([(cx.sv, tz), (spj, tpj), (cx.sv, typs[b])]
                                   if jt == 0 else ()),
                            inc=cx.st_ if jt == ND - 1 else None)
                    tpm = em.token(cx.st_)
                    if e == 0 and g == 0:
                        typs[b] = em.emit("v", lambda v, st=st, oc=oc, e=e, b=b: v.tensor_scalar(
                            YAC[:, st * 1024 + oc * 512:st * 1024 + (oc + 1) * 512],
                            YPS[b][:, 0:512], FE[:, st * 4 + e:st * 4 + e + 1],
                            None, op0=ALU.mult),
                            waits=[(cx.st_, tpm)], inc=cx.sv)
                    else:
                        typs[b] = em.emit("v", lambda v, st=st, oc=oc, e=e, b=b: v.scalar_tensor_tensor(
                            YAC[:, st * 1024 + oc * 512:st * 1024 + (oc + 1) * 512],
                            YPS[b][:, 0:512], FE[:, st * 4 + e:st * 4 + e + 1],
                            YAC[:, st * 1024 + oc * 512:st * 1024 + (oc + 1) * 512],
                            op0=ALU.mult, op1=ALU.add),
                            waits=[(cx.st_, tpm)], inc=cx.sv)
            tpm_prev = em.token(cx.st_)

    tst = 0
    tx2 = 0
    for st in range(NT):
        tX1 = em.dma("q", XT[:, 0:1024],
                     xinner_hbm.ap()[st * 128:(st + 1) * 128, :],
                     waits=([(cx.sv, em.token(cx.sv))] if st == 0 else
                            [(cx.sv, tx2)]), inc=smx)
        tXR = em.dma("q", XT2[:, 0:1024],
                     xres_hbm.ap()[st * 128:(st + 1) * 128, :],
                     waits=[(smo, tst)] + ([(cx.sv, em.token(cx.sv))]
                                           if st == 0 else []), inc=smo)
        tx2 = em.emit("v", lambda v: v.scalar_tensor_tensor(
            XT2[:, 0:1024], XT[:, 0:1024], 0.5, XT2[:, 0:1024],
            op0=ALU.mult, op1=ALU.add),
            waits=[(smx, tX1), (smo, tXR)], inc=cx.sv)
        tx2 = em.emit("v", lambda v, st=st: v.scalar_tensor_tensor(
            XT2[:, 0:1024], YAC[:, st * 1024:(st + 1) * 1024], 0.5,
            XT2[:, 0:1024], op0=ALU.mult, op1=ALU.add),
            waits=[(cx.sv, tx2)], inc=cx.sv)
        tst = em.dma("q", (xres_out if xres_out is not None
                           else xres_hbm).ap()[st * 128:(st + 1) * 128, :],
                     XT2[:, 0:1024], waits=[(cx.sv, tx2)], inc=smo)
    em.emit("q", lambda q: None, waits=[(smo, em.token(smo))])
    return {"smo": (smo, tst), "sv": tx2}



II = 2816


def _attn_bufs(cx, inner):
    bufs = {
        "HCT": cx.sbuf("a_hct", [128, ND * TP], BF16),
        "KTR": cx.sbuf("a_ktr", [128, ND * T], BF16),
        "VAUG": cx.sbuf("a_vaug", [128, 16 * 1040], BF16),
        "QTR": cx.sbuf("a_qtr", [128, ND * 512], BF16),
        "T0": cx.sbuf("a_t0", [128, 512], BF16),
        "TSW": cx.sbuf("a_tsw", [128, 512], BF16),
        "PT": cx.sbuf("a_pt", [128, 2 * 512], BF16),
        "YT": cx.sbuf("a_yt", [64, 16 * 512], BF16),
        "ZROW": cx.sbuf("a_zrow", [65, 512], F32),
        "ZC": cx.sbuf("a_zc", [1, 128], F32),
        "BCOL": cx.sbuf("a_bcol", [128, 16], F32),
        "WB1": cx.sbuf("a_wb1", [128, ND * 1024], BF16),
        "WBO": cx.sbuf("a_wbo", [64, 8 * 1024], BF16),
        "RZX": cx.sbuf("a_rzx", [64, 512], BF16),
        "SQB": cx.sbuf("a_sqb", [128, 512], BF16),
        "HTMP": cx.sbuf("a_htmp", [128, 1024], BF16),
        "XT": cx.sbuf("a_xt", [128, 1024], F32),
        "XT2": cx.sbuf("a_xt2", [128, 1024], F32),
    }
    if inner:
        bufs.update({
            "CVO": cx.sbuf("a_cvo", [128, T], BF16),
            "XIN": cx.sbuf("a_xin", [128, 1024], F32),
            "CW": cx.sbuf("a_cw", [128, 24], F32),
            "LN1W": cx.sbuf("a_ln1w", [1, 1024], BF16),
            "LN1B": cx.sbuf("a_ln1b", [1, 1024], BF16),
            "LN1WR": cx.sbuf("a_ln1wr", [128, 1024], BF16),
            "LN1BR": cx.sbuf("a_ln1br", [128, 1024], BF16),
        })
    else:
        bufs.update({
            "CVO": cx.sbuf("a_cvo", [128, T], BF16),
            "BQ": cx.sbuf("a_bq", [128, 8], F32),
            "BK": cx.sbuf("a_bk", [128, 8], F32),
            "BVR": cx.sbuf("a_bvr", [1, 1024], BF16),
            "BOR": cx.sbuf("a_bor", [1, 1024], BF16),
            "BVRR": cx.sbuf("a_bvrr", [128, 1024], BF16),
            "BORR": cx.sbuf("a_borr", [128, 1024], BF16),
        })
    return bufs


def build_launch_l0ffn(first):
    nc = bass.Bass("TRN2", target_bir_lowering=False)
    cx = Ctx(nc)
    em = cx.em
    x = nc.dram_tensor("x", [T, D], F32, kind="ExternalInput").ap()
    y = nc.dram_tensor("y", [T, D], F32, kind="ExternalOutput").ap()
    if first:
        wg = nc.dram_tensor("wg", [II, D], F32, kind="ExternalInput").ap()
        wu = nc.dram_tensor("wu", [II, D], F32, kind="ExternalInput").ap()
        wd = nc.dram_tensor("wd", [D, II], F32, kind="ExternalInput").ap()
        wgT = nc.dram_tensor("wgT", [D, II], BF16, kind="ExternalOutput")
        wuT = nc.dram_tensor("wuT", [D, II], BF16, kind="ExternalOutput")
        wdT = nc.dram_tensor("wdT", [II, D], BF16, kind="ExternalOutput")
    else:
        wgT = nc.dram_tensor("wgT", [D, II], BF16, kind="ExternalInput")
        wuT = nc.dram_tensor("wuT", [D, II], BF16, kind="ExternalInput")
        wdT = nc.dram_tensor("wdT", [II, D], BF16, kind="ExternalInput")

    consts = host_consts()
    skip = ()
    cx.load_consts(consts)
    prew = cx.prelude_fconsts()

    XRES = cx.sbuf("XRES", [128, NT * D], F32)
    XQT = cx.sbuf("XQT", [128, ND * T], BF16)
    PS_G = cx.psum("PS_G", [128, 512], F32)
    PS_U = cx.psum("PS_U", [128, 512], F32)
    th = TransposeHelper(cx)
    fp4s = [Fp4Helper(cx), Fp4Helper(cx, tag="fp4b")]

    xt = x.rearrange("(n p) d -> p n d", p=128)
    em.dma("q", XRES[:, :].rearrange("p (n d) -> p n d", d=D), xt, inc=cx.sqx)
    xres_tok = em.token(cx.sqx)

    bufs = alloc_l0f_bufs(cx)
    qs = QSched(fp4s)
    if first:
        units, chunk_tok = prep_units(cx, qs, th, [wg, wu, wd],
                                      [wgT, wuT, wdT])
    else:
        units, chunk_tok = None, None
    stage_l0ffn(cx, qs, th, XRES, XQT, wgT, wuT, wdT, bufs, "c1",
                xres_waits=[(cx.sqx, xres_tok)],
                units=units, chunk_tok=chunk_tok)

    yt = y.rearrange("(n p) d -> p n d", p=128)
    em.dma("q", yt, XRES[:, :].rearrange("p (n d) -> p n d", d=D),
           waits=[(cx.sv, em.token(cx.sv))], inc=cx.sout)
    em.emit("q", lambda q: None, waits=[(cx.sout, em.token(cx.sout)),
                                        (cx.spout, em.token(cx.spout))])
    cx.em.replay(prewaits=prew)
    return nc, consts, skip


def build_launch_gla():
    nc = bass.Bass("TRN2", target_bir_lowering=False)
    cx = Ctx(nc)
    em = cx.em
    x = nc.dram_tensor("x", [T, D], F32, kind="ExternalInput").ap()
    w = {}
    for name, shape in [("Wq", [D, 512]), ("Wk", [D, 512]), ("Wv", [D, D]),
                        ("Wg", [D, D]), ("Wgk1", [D, 16]), ("Wgk2", [16, 512]),
                        ("bgk2", [512]), ("norm_w", [128]), ("Wo", [D, D])]:
        w[name] = nc.dram_tensor(name, shape, F32, kind="ExternalInput").ap()
    y = nc.dram_tensor("y", [T, D], F32, kind="ExternalOutput").ap()

    consts = host_consts()
    skip = ("c_cmask", "c_cos", "c_sin", "c_mmask", "c_rot128", "c_ones65",
            "c_cos2", "c_sinrot")
    cx.load_consts(consts, skip=skip)
    prew = cx.prelude_fconsts()

    XRES = cx.sbuf("XRES", [128, NT * D], F32)
    HT = cx.sbuf("HT", [128, ND * T], BF16)
    cx.psum("PS_G", [128, 512], F32)
    cx.psum("PS_U", [128, 512], F32)
    cx.psum("l0f_psd0", [128, 512], F32)
    cx.psum("l0f_psd1", [128, 512], F32)
    th = TransposeHelper(cx)

    bufs = {
        "WQ": cx.sbuf("g_wq", [128, 8 * 512], BF16),
        "WK": cx.sbuf("g_wk", [128, 8 * 512], BF16),
        "WVG": cx.sbuf("g_wvg", [128, 8 * 1024], BF16),
        "WGK1": cx.sbuf("g_wgk1", [128, 8 * 16], BF16),
        "WGK2": cx.sbuf("g_wgk2", [16, 512], BF16),
        "BGK2": cx.sbuf("g_bgk2", [1, 512], BF16),
        "NW": cx.sbuf("g_nw", [128, 1], F32),
    }

    xt = x.rearrange("(n p) d -> p n d", p=128)
    em.dma("q", XRES[:, :].rearrange("p (n d) -> p n d", d=D), xt, inc=cx.sqx)
    xres_tok = em.token(cx.sqx)
    wtok = stage_gla_weights(cx, w, bufs)
    ht_tok = stage_ln_transpose(cx, th, XRES, HT, "g",
                                xres_waits=[(cx.sqx, xres_tok)])
    stage_gla(cx, th, XRES, HT, bufs, wtok, w, "g", ht_tok=ht_tok, phase=3)

    yt = y.rearrange("(n p) d -> p n d", p=128)
    em.dma("q", yt, XRES[:, :].rearrange("p (n d) -> p n d", d=D),
           waits=[(cx.sv, em.token(cx.sv))], inc=cx.sout)
    em.emit("q", lambda q: None, waits=[(cx.sout, em.token(cx.sout)),
                                        (cx.spout, em.token(cx.spout))])
    cx.em.replay(prewaits=prew)
    return nc, consts, skip


def build_launch_inner():
    nc = bass.Bass("TRN2", target_bir_lowering=False)
    cx = Ctx(nc)
    em = cx.em
    xres = nc.dram_tensor("xres", [T, D], F32, kind="ExternalInput")
    w = {}
    for name, shape in [("ln1_w", [D]), ("ln1_b", [D]), ("conv_w", [D, 1, 3]),
                        ("attn_w", [D, 3 * D]), ("proj_w", [D, D])]:
        w[name] = nc.dram_tensor(name, shape, F32, kind="ExternalInput").ap()
    xinner = nc.dram_tensor("xinner", [T, D], F32, kind="ExternalOutput")

    consts = host_consts()
    skip = ("c_u16", "c_triu", "c_triu2", "c_e32", "c_e32h", "c_cmask",
            "c_cos", "c_sin")
    cx.load_consts(consts, skip=skip)
    prew = cx.prelude_fconsts()

    cx.psum("PS_G", [128, 512], F32)
    cx.psum("PS_U", [128, 512], F32)
    cx.psum("l0f_psd0", [128, 512], F32)
    cx.psum("l0f_psd1", [128, 512], F32)
    th = TransposeHelper(cx)
    bufs = _attn_bufs(cx, inner=True)

    stage_attn(cx, th, xres, bufs, w, "inner", "l1i", hbm_xinner=xinner)
    em.emit("q", lambda q: None,
            waits=[(cx.em.sem("sxb_l1i"), em.token(cx.em.sem("sxb_l1i")))])
    cx.em.replay(prewaits=prew)
    return nc, consts, skip


def build_launch_mha():
    nc = bass.Bass("TRN2", target_bir_lowering=False)
    cx = Ctx(nc)
    em = cx.em
    xres = nc.dram_tensor("xres", [T, D], F32, kind="ExternalInput")
    w = {}
    for name, shape in [("Wq", [D, D]), ("bq", [D]), ("Wk", [D, D]),
                        ("bk", [D]), ("Wv", [D, D]), ("bv", [D]),
                        ("Wo", [D, D]), ("bo", [D])]:
        w[name] = nc.dram_tensor(name, shape, F32, kind="ExternalInput").ap()
    y = nc.dram_tensor("y", [T, D], F32, kind="ExternalOutput")

    consts = host_consts()
    skip = ("c_u16", "c_triu", "c_triu2", "c_e32", "c_e32h", "c_cmask",
            "c_cos", "c_sin")
    cx.load_consts(consts, skip=skip)
    prew = cx.prelude_fconsts()

    cx.psum("PS_G", [128, 512], F32)
    cx.psum("PS_U", [128, 512], F32)
    cx.psum("l0f_psd0", [128, 512], F32)
    cx.psum("l0f_psd1", [128, 512], F32)
    th = TransposeHelper(cx)
    bufs = _attn_bufs(cx, inner=False)

    stage_attn(cx, th, xres, bufs, w, "mha", "l1a", xres_out=y)
    em.emit("q", lambda q: None,
            waits=[(cx.em.sem("sxb_l1a"), em.token(cx.em.sem("sxb_l1a")))])
    cx.em.replay(prewaits=prew)
    return nc, consts, skip


def build_launch_moe():
    nc = bass.Bass("TRN2", target_bir_lowering=False)
    cx = Ctx(nc)
    em = cx.em
    xres = nc.dram_tensor("xres", [T, D], F32, kind="ExternalInput")
    x1 = nc.dram_tensor("x1", [T, D], F32, kind="ExternalInput")
    w = {}
    for name, shape in [("ln2_w", [D]), ("ln2_b", [D]), ("router_w", [D, 4]),
                        ("expert_wv", [4, D, 8192]),
                        ("expert_proj", [4, 4096, D])]:
        w[name] = nc.dram_tensor(name, shape, F32, kind="ExternalInput").ap()
    y = nc.dram_tensor("y", [T, D], F32, kind="ExternalOutput")

    consts = host_consts()
    skip = tuple(k for k in host_consts() if k not in ("c_ident",
                                                       "c_ones_row"))
    cx.load_consts(consts, skip=skip)
    prew = cx.prelude_fconsts()

    cx.psum("PS_G", [128, 512], F32)
    cx.psum("PS_U", [128, 512], F32)
    cx.psum("l0f_psd0", [128, 512], F32)
    cx.psum("l0f_psd1", [128, 512], F32)
    th = TransposeHelper(cx)

    bufs = {
        "HCT": cx.sbuf("a_hct", [128, ND * TP], BF16),
        "HTMP": cx.sbuf("a_htmp", [128, 1024], BF16),
        "XT": cx.sbuf("a_xt", [128, 1024], F32),
        "XT2": cx.sbuf("a_xt2", [128, 1024], F32),
        "ZBUF": cx.sbuf("m_zbuf", [128, 8 * 2048], BF16),
        "WVG": cx.sbuf("m_wvg", [128, 8 * 1024], BF16),
        "WVV": cx.sbuf("m_wvv", [128, 8 * 1024], BF16),
        "PROJB": cx.sbuf("m_projb", [128, 8 * 1024], BF16),
        "YAC": cx.sbuf("m_yac", [128, 16 * 1024], F32),
        "FE": cx.sbuf("m_fe", [128, 80], F32),
        "RWB": cx.sbuf("m_rwb", [128, 32], BF16),
        "SGT": cx.sbuf("m_sgt", [128, 512], BF16),
        "LN2W": cx.sbuf("m_ln2w", [1, 1024], BF16),
        "LN2B": cx.sbuf("m_ln2b", [1, 1024], BF16),
        "LN2WR": cx.sbuf("m_ln2wr", [128, 1024], BF16),
        "LN2BR": cx.sbuf("m_ln2br", [128, 1024], BF16),
    }

    r = stage_moe(cx, th, xres, x1, bufs, w, "l1m", xres_out=y)
    em.emit("q", lambda q: None, waits=[r["smo"]])
    cx.em.replay(prewaits=prew)
    return nc, consts, skip


BOUNDS = []


def emit_barrier(cx, extra=()):
    em = cx.em
    waits = [(cx.sv, em.token(cx.sv)), (cx.ss, em.token(cx.ss)),
             (cx.st_, em.token(cx.st_))] + list(extra)
    BOUNDS.append(em.token(cx.st_))
    for e in ENGS:
        em.emit(e, lambda eng: None, waits=waits)


def build_fused():
    nc = bass.Bass("TRN2", target_bir_lowering=False)
    cx = Ctx(nc)
    em = cx.em

    x = nc.dram_tensor("x", [T, D], F32, kind="ExternalInput").ap()
    y = nc.dram_tensor("y", [T, D], F32, kind="ExternalOutput")
    wnames = [
        ("L0_attn_Wq", [D, 512]), ("L0_attn_Wk", [D, 512]),
        ("L0_attn_Wv", [D, D]), ("L0_attn_Wg", [D, D]),
        ("L0_attn_Wgk1", [D, 16]), ("L0_attn_Wgk2", [16, 512]),
        ("L0_attn_bgk2", [512]), ("L0_attn_norm_w", [128]),
        ("L0_attn_Wo", [D, D]),
        ("L0_ffn_w_gate", [II, D]), ("L0_ffn_w_up", [II, D]),
        ("L0_ffn_w_down", [D, II]),
        ("L1_attn_Wq", [D, D]), ("L1_attn_bq", [D]),
        ("L1_attn_Wk", [D, D]), ("L1_attn_bk", [D]),
        ("L1_attn_Wv", [D, D]), ("L1_attn_bv", [D]),
        ("L1_attn_Wo", [D, D]), ("L1_attn_bo", [D]),
        ("L1_ffn_ln1_w", [D]), ("L1_ffn_ln1_b", [D]),
        ("L1_ffn_ln2_w", [D]), ("L1_ffn_ln2_b", [D]),
        ("L1_ffn_conv_w", [D, 1, 3]), ("L1_ffn_attn_w", [D, 3 * D]),
        ("L1_ffn_proj_w", [D, D]), ("L1_ffn_router_w", [D, 4]),
        ("L1_ffn_expert_wv", [4, D, 8192]),
        ("L1_ffn_expert_proj", [4, 4096, D]),
    ]
    win = {n: nc.dram_tensor(n, sh, F32, kind="ExternalInput").ap()
           for (n, sh) in wnames}
    wgT = nc.dram_tensor("wgT", [D, II], BF16, kind="Internal")
    wuT = nc.dram_tensor("wuT", [D, II], BF16, kind="Internal")
    wdT = nc.dram_tensor("wdT", [II, D], BF16, kind="Internal")
    xres_hbm = nc.dram_tensor("xres_hbm", [T, D], F32, kind="Internal")
    xinner = nc.dram_tensor("xinner", [T, D], F32, kind="Internal")

    consts = host_consts()
    MOVED = ("c_mmask", "c_cos2", "c_sinrot")
    skip = ("c_cmask", "c_cos", "c_sin", "c_e32", "c_e32h")
    cx.load_consts(consts, skip=skip + MOVED)
    cdram = {}
    for k in MOVED:
        arr = consts[k]
        dt = BF16 if arr.dtype == ml_dtypes.bfloat16 else F32
        cdram[k] = nc.dram_tensor(k, list(arr.shape), dt,
                                  kind="ExternalInput")

    def reload_consts(names):
        for k in names:
            t = cx.sb[k + "_sb"]
            em.dma("q", t[:, :], cdram[k].ap(), inc=cx.sqc)
        cx.tok_consts = em.token(cx.sqc)
        return (cx.sqc, cx.tok_consts)

    prew = cx.prelude_fconsts()
    ln_scratch(cx, "z")
    cx.sbuf("a_htmp", [128, 1024], BF16)

    AB = (int(nc.sbuf_base) + 127) & ~127
    R1, R2 = AB, AB + 32768
    R3 = R2 + 65536
    R3LIM = 104448
    assert R3 + R3LIM <= 229376, (AB, R3 + R3LIM)

    def _sz(sh, dt):
        return sh[-1] * (4 if dt == F32 else 2)

    def layout(base, items, limit):
        off = 0
        for (n, sh, dt) in items:
            off = (off + 63) & ~63
            cx.sbuf(n, sh, dt, at=base + off)
            off += _sz(sh, dt)
        assert off <= limit, (base, off, limit)

    for n in ("XQT", "HT", "a_ktr", "m_zbuf"):
        cx.sbuf(n, [128, 16384], BF16, at=R1)
    cx.sbuf("XRES", [128, NT * D], F32, at=R2)
    cx.sbuf("m_yac", [128, 16 * 1024], F32, at=R2)
    layout(R2, [
        ("a_yt", [64, 16 * 512], BF16), ("a_wb1", [128, ND * 1024], BF16),
        ("a_wbo", [64, 8 * 1024], BF16), ("a_cvo", [128, T], BF16),
        ("a_xin", [128, 1024], F32), ("a_t0", [128, 512], BF16),
        ("a_tsw", [128, 512], BF16), ("a_pt", [128, 2 * 512], BF16),
        ("a_zrow", [65, 512], F32), ("a_rzx", [64, 512], BF16),
        ("a_sqb", [128, 512], BF16),
    ], 65536)
    layout(R3, [
        ("l0f_g0", [128, 8 * 512], BF16), ("l0f_g1", [128, 8 * 512], BF16),
        ("l0f_u0", [128, 8 * 512], BF16), ("l0f_u1", [128, 8 * 512], BF16),
        ("l0f_d0", [128, 4 * 1024], BF16), ("l0f_d1", [128, 4 * 1024], BF16),
        ("l0f_hqt", [128, 4 * 2048], BF16),
        ("fp4_xn", [128, 1024], BF16), ("fp4_u", [128, 1024], BF16),
        ("fp4_sg", [128, 1024], BF16), ("fp4_a", [128, 1024], BF16),
        ("fp4_c", [128, 1024], BF16),
        ("l0f_h", [128, D], BF16), ("l0f_xq", [128, D], BF16),
        ("l0f_xq2", [128, D], BF16), ("l0f_xq3", [128, D], BF16),
        ("l0f_tg", [128, 512], BF16), ("l0f_tu", [128, 512], BF16),
        ("fp4_amax", [128, 64], F32), ("fp4_scf", [128, 64], F32),
        ("fp4_rs", [128, 64], F32),
                ("fp4b_xn", [128, 1024], BF16), ("fp4b_u", [128, 1024], BF16),
        ("fp4b_sg", [128, 1024], BF16), ("fp4b_a", [128, 1024], BF16),
        ("fp4b_c", [128, 1024], BF16),
        ("fp4b_amax", [128, 64], F32), ("fp4b_scf", [128, 64], F32),
        ("fp4b_rs", [128, 64], F32),
                ("prp_w0", [128, 1024], BF16), ("prp_w1", [128, 1024], BF16),
            ], R3LIM)
    layout(R3, [
        ("g_wq", [128, 8 * 512], BF16), ("g_wk", [128, 8 * 512], BF16),
        ("g_wvg", [128, 8 * 1024], BF16),
        ("gla_qt", [128, 4 * SC_TOK], BF16), ("gla_kt", [128, 4 * SC_TOK], BF16),
        ("gla_ktok", [64, 4 * 512], BF16), ("gla_vtok", [64, 4 * 1024], BF16),
        ("gla_gk", [64, 4 * 512], BF16),
        ("gla_eb", [128, 4 * 256], BF16), ("gla_emb", [128, 4 * 256], BF16),
        ("gla_qet", [128, 4 * 256], BF16), ("gla_ket", [128, 4 * 256], BF16),
        ("gla_kemb", [64, 4 * 512], BF16),
        ("gla_osc", [64, 4 * 1024], BF16), ("gla_gtok", [64, 4 * 1024], BF16),
        ("gla_ogt", [128, ND * SC_TOK], BF16),
        ("lnt_hg", [128, D], BF16),
        ("gla_t1t", [16, 256], BF16), ("gla_elast", [128, 16], F32),
        ("gla_ats", [64, 128], BF16), ("gla_otmp", [64, 256], BF16),
        ("gla_rstd", [64, 32], F32), ("gla_ssq", [64, 32], F32),
        ("gla_sqs", [64, 128], F32), ("gla_lnt", [64, 32], F32),
        ("gla_s0", [128, 128], F32), ("gla_s1", [128, 128], F32),
        ("gla_s2", [128, 128], F32), ("gla_s3", [128, 128], F32),
        ("gla_sb0", [128, 128], BF16), ("gla_sb1", [128, 128], BF16),
        ("gla_sb2", [128, 128], BF16), ("gla_sb3", [128, 128], BF16),
        ("gla_sg", [64, 512], BF16), ("gla_embt", [64, 512], BF16),
        ("g_wgk1", [128, 128], BF16), ("g_wgk2", [16, 512], BF16),
        ("g_bgk2", [1, 512], BF16), ("g_nw", [128, 1], F32),
    ], R3LIM)
    layout(R3, [
        ("a_hct", [128, ND * TP], BF16), ("a_vaug", [128, 16 * 1040], BF16),
        ("a_xt", [128, 1024], F32), ("a_xt2", [128, 1024], F32),
        ("a_qtr", [128, ND * 512], BF16),
        ("a_ln1w", [1, 1024], BF16), ("a_ln1b", [1, 1024], BF16),
        ("a_ln1wr", [128, 1024], BF16), ("a_ln1br", [128, 1024], BF16),
        ("c_mmask_sb", [128, 2048], BF16), ("c_cos2_sb", [128, 2048], BF16),
        ("c_sinrot_sb", [128, 2048], BF16),
        ("a_zc", [1, 128], F32), ("a_bcol", [128, 16], F32),
        ("a_cw", [128, 24], F32), ("a_bq", [128, 8], F32),
        ("a_bk", [128, 8], F32),
    ], R3LIM)
    cx.sbuf("a_bvr", [1, 1024], BF16, at=R3 + 82496)
    cx.sbuf("a_bor", [1, 1024], BF16, at=R3 + 84544)
    cx.sbuf("a_bvrr", [128, 1024], BF16, at=R3 + 86592)
    cx.sbuf("a_borr", [128, 1024], BF16, at=R3 + 88640)
    cx.sbuf("m_wvg", [128, 8 * 1024], BF16, at=R3 + 32832)
    cx.sbuf("m_wvv", [128, 8 * 1024], BF16, at=R3 + 49216)
    cx.sbuf("m_projb", [128, 8 * 1024], BF16, at=R3 + 66112)
    cx.sbuf("m_ln2w", [1, 1024], BF16, at=R3 + 82496)
    cx.sbuf("m_ln2b", [1, 1024], BF16, at=R3 + 84544)
    cx.sbuf("m_ln2wr", [128, 1024], BF16, at=R3 + 86592)
    cx.sbuf("m_ln2br", [128, 1024], BF16, at=R3 + 88640)
    layout(R3 + 90688, [
        ("m_sgt", [128, 512], BF16), ("m_fe", [128, 80], F32),
        ("m_rwb", [128, 32], BF16),
    ], R3LIM - 90688)
    cx.frozen = True

    PS_G = cx.psum("PS_G", [128, 512], F32)
    PS_U = cx.psum("PS_U", [128, 512], F32)
    cx.psum("l0f_psd0", [128, 512], F32)
    cx.psum("l0f_psd1", [128, 512], F32)
    th = TransposeHelper(cx)
    fp4s = [Fp4Helper(cx), Fp4Helper(cx, tag="fp4b")]

    XRES = cx.sb["XRES"]
    XQT = cx.sb["XQT"]
    HT = cx.sb["HT"]

    xt = x.rearrange("(n p) d -> p n d", p=128)
    em.dma("q", XRES[:, :].rearrange("p (n d) -> p n d", d=D), xt, inc=cx.sqx)
    xres_tok = em.token(cx.sqx)

    l0bufs = alloc_l0f_bufs(cx)
    qs = QSched(fp4s)
    units, chunk_tok = prep_units(cx, qs, th,
                                  [win["L0_ffn_w_gate"], win["L0_ffn_w_up"],
                                   win["L0_ffn_w_down"]],
                                  [wgT, wuT, wdT])
    stage_l0ffn(cx, qs, th, XRES, XQT, wgT, wuT, wdT, l0bufs, "c1",
                xres_waits=[(cx.sqx, xres_tok)],
                units=units, chunk_tok=chunk_tok)
    emit_barrier(cx)

    gw = {"Wq": win["L0_attn_Wq"], "Wk": win["L0_attn_Wk"],
          "Wv": win["L0_attn_Wv"], "Wg": win["L0_attn_Wg"],
          "Wgk1": win["L0_attn_Wgk1"], "Wgk2": win["L0_attn_Wgk2"],
          "bgk2": win["L0_attn_bgk2"], "norm_w": win["L0_attn_norm_w"],
          "Wo": win["L0_attn_Wo"]}
    gbufs = {"WQ": cx.sb["g_wq"], "WK": cx.sb["g_wk"],
             "WVG": cx.sb["g_wvg"], "WGK1": cx.sb["g_wgk1"],
             "WGK2": cx.sb["g_wgk2"], "BGK2": cx.sb["g_bgk2"],
             "NW": cx.sb["g_nw"]}
    wtok = stage_gla_weights(cx, gw, gbufs)
    ht_tok = stage_ln_transpose(cx, th, XRES, HT, "g", xres_waits=())
    stage_gla(cx, th, XRES, HT, gbufs, wtok, gw, "g", ht_tok=ht_tok, phase=3)
    emit_barrier(cx)

    stage_l0ffn(cx, qs, th, XRES, XQT, wgT, wuT, wdT, l0bufs, "c2",
                start_waits=(), xres_waits=())
    emit_barrier(cx)

    tsp = em.dma("q", xres_hbm.ap().rearrange("(n p) d -> p n d", p=128),
                 XRES[:, :].rearrange("p (n d) -> p n d", d=D), inc=cx.sout)
    emit_barrier(cx, extra=[(cx.sout, tsp)])

    iw = {"ln1_w": win["L1_ffn_ln1_w"], "ln1_b": win["L1_ffn_ln1_b"],
          "conv_w": win["L1_ffn_conv_w"], "attn_w": win["L1_ffn_attn_w"],
          "proj_w": win["L1_ffn_proj_w"]}
    mw = {"ln2_w": win["L1_ffn_ln2_w"], "ln2_b": win["L1_ffn_ln2_b"],
          "router_w": win["L1_ffn_router_w"],
          "expert_wv": win["L1_ffn_expert_wv"],
          "expert_proj": win["L1_ffn_expert_proj"]}
    aw = {"Wq": win["L1_attn_Wq"], "bq": win["L1_attn_bq"],
          "Wk": win["L1_attn_Wk"], "bk": win["L1_attn_bk"],
          "Wv": win["L1_attn_Wv"], "bv": win["L1_attn_bv"],
          "Wo": win["L1_attn_Wo"], "bo": win["L1_attn_bo"]}

    abase = {"HCT": cx.sb["a_hct"], "KTR": cx.sb["a_ktr"],
             "VAUG": cx.sb["a_vaug"], "QTR": cx.sb["a_qtr"],
             "T0": cx.sb["a_t0"], "TSW": cx.sb["a_tsw"],
             "PT": cx.sb["a_pt"], "YT": cx.sb["a_yt"],
             "ZROW": cx.sb["a_zrow"], "ZC": cx.sb["a_zc"],
             "BCOL": cx.sb["a_bcol"], "WB1": cx.sb["a_wb1"],
             "WBO": cx.sb["a_wbo"], "RZX": cx.sb["a_rzx"],
             "SQB": cx.sb["a_sqb"], "CVO": cx.sb["a_cvo"],
             "HTMP": cx.sb["a_htmp"], "XT": cx.sb["a_xt"],
             "XT2": cx.sb["a_xt2"]}
    ibufs = dict(abase)
    ibufs.update({"XIN": cx.sb["a_xin"], "CW": cx.sb["a_cw"],
                  "LN1W": cx.sb["a_ln1w"], "LN1B": cx.sb["a_ln1b"],
                  "LN1WR": cx.sb["a_ln1wr"], "LN1BR": cx.sb["a_ln1br"]})
    mhabufs = dict(abase)
    mhabufs.update({"BQ": cx.sb["a_bq"], "BK": cx.sb["a_bk"],
                    "BVR": cx.sb["a_bvr"], "BOR": cx.sb["a_bor"],
                    "BVRR": cx.sb["a_bvrr"], "BORR": cx.sb["a_borr"]})
    mbufs = {"HCT": cx.sb["a_hct"], "HTMP": cx.sb["a_htmp"],
             "XT": cx.sb["a_xt"], "XT2": cx.sb["a_xt2"],
             "ZBUF": cx.sb["m_zbuf"], "WVG": cx.sb["m_wvg"],
             "WVV": cx.sb["m_wvv"], "PROJB": cx.sb["m_projb"],
             "YAC": cx.sb["m_yac"], "FE": cx.sb["m_fe"],
             "RWB": cx.sb["m_rwb"], "SGT": cx.sb["m_sgt"],
             "LN2W": cx.sb["m_ln2w"], "LN2B": cx.sb["m_ln2b"],
             "LN2WR": cx.sb["m_ln2wr"], "LN2BR": cx.sb["m_ln2br"]}

    def bar_sem(name):
        s = em.sem(name)
        return (s, em.token(s))

    ACONSTS = ("c_mmask", "c_cos2", "c_sinrot")
    emit_barrier(cx, extra=[reload_consts(ACONSTS)])
    stage_attn(cx, th, xres_hbm, ibufs, iw, "inner", "i1", hbm_xinner=xinner)
    emit_barrier(cx, extra=[bar_sem("sxb_i1")])
    stage_moe(cx, th, xres_hbm, xinner, mbufs, mw, "m1")
    emit_barrier(cx, extra=[bar_sem("smo_m1"), reload_consts(ACONSTS)])
    stage_attn(cx, th, xres_hbm, mhabufs, aw, "mha", "a1")
    emit_barrier(cx, extra=[bar_sem("sxb_a1"), reload_consts(ACONSTS)])
    stage_attn(cx, th, xres_hbm, ibufs, iw, "inner", "i2", hbm_xinner=xinner)
    emit_barrier(cx, extra=[bar_sem("sxb_i2")])
    stage_moe(cx, th, xres_hbm, xinner, mbufs, mw, "m2", xres_out=y)

    cx.em.replay(prewaits=prew)
    return nc, consts, skip


_BUILD_CACHE = {}


def _get_build(name):
    if name not in _BUILD_CACHE:
        _BUILD_CACHE[name] = {
            "l0ffn1": lambda: build_launch_l0ffn(True),
            "l0ffn2": lambda: build_launch_l0ffn(False),
            "gla": build_launch_gla,
            "inner": build_launch_inner,
            "mha": build_launch_mha,
            "moe": build_launch_moe,
            "fused": build_fused,
        }[name]()
    return _BUILD_CACHE[name]


def kernel(**inputs):
    nc, consts, skip = _get_build("fused")
    cores = list(range(NCORES))
    cmap = {k: np.asarray(v) for k, v in consts.items() if k not in skip}
    xin = np.ascontiguousarray(np.asarray(inputs["L0_attn_x"], np.float32))
    shared = {k: np.ascontiguousarray(np.asarray(inputs[k], np.float32))
              for k in inputs if k != "L0_attn_x"}
    inmaps = []
    for c in cores:
        m = dict(cmap)
        m.update(shared)
        m["x"] = xin[c]
        inmaps.append(m)
    res = run_bass_kernel_spmd(nc, inmaps, core_ids=cores)
    return np.stack([np.asarray(res.results[c]["y"])
                     for c in cores]).astype(np.float32)


def kernel_multi(**inputs):
    xin = np.ascontiguousarray(np.asarray(inputs["L0_attn_x"], np.float32))
    cores = list(range(NCORES))

    def launch(name, shared, per_core):
        nc, consts, skip = _get_build(name)
        cmap = {k: np.asarray(v) for k, v in consts.items() if k not in skip}
        inmaps = []
        for c in cores:
            m = dict(cmap)
            m.update(shared)
            m.update(per_core[c])
            inmaps.append(m)
        return run_bass_kernel_spmd(nc, inmaps, core_ids=cores)

    def g(inp):
        return np.ascontiguousarray(np.asarray(inputs[inp], np.float32))

    res = launch("l0ffn1",
                 {"wg": g("L0_ffn_w_gate"), "wu": g("L0_ffn_w_up"),
                  "wd": g("L0_ffn_w_down")},
                 [{"x": xin[c]} for c in cores])
    xcur = [np.asarray(res.results[c]["y"]) for c in cores]
    wgT = np.asarray(res.results[0]["wgT"])
    wuT = np.asarray(res.results[0]["wuT"])
    wdT = np.asarray(res.results[0]["wdT"])

    gla_w = {"Wq": g("L0_attn_Wq"), "Wk": g("L0_attn_Wk"),
             "Wv": g("L0_attn_Wv"), "Wg": g("L0_attn_Wg"),
             "Wgk1": g("L0_attn_Wgk1"), "Wgk2": g("L0_attn_Wgk2"),
             "bgk2": g("L0_attn_bgk2"), "norm_w": g("L0_attn_norm_w"),
             "Wo": g("L0_attn_Wo")}
    res = launch("gla", gla_w, [{"x": xcur[c]} for c in cores])
    xcur = [np.asarray(res.results[c]["y"]) for c in cores]

    res = launch("l0ffn2", {"wgT": wgT, "wuT": wuT, "wdT": wdT},
                 [{"x": xcur[c]} for c in cores])
    xcur = [np.asarray(res.results[c]["y"]) for c in cores]

    inner_w = {"ln1_w": g("L1_ffn_ln1_w"), "ln1_b": g("L1_ffn_ln1_b"),
               "conv_w": g("L1_ffn_conv_w"), "attn_w": g("L1_ffn_attn_w"),
               "proj_w": g("L1_ffn_proj_w")}
    moe_w = {"ln2_w": g("L1_ffn_ln2_w"), "ln2_b": g("L1_ffn_ln2_b"),
             "router_w": g("L1_ffn_router_w"),
             "expert_wv": g("L1_ffn_expert_wv"),
             "expert_proj": g("L1_ffn_expert_proj")}
    mha_w = {"Wq": g("L1_attn_Wq"), "bq": g("L1_attn_bq"),
             "Wk": g("L1_attn_Wk"), "bk": g("L1_attn_bk"),
             "Wv": g("L1_attn_Wv"), "bv": g("L1_attn_bv"),
             "Wo": g("L1_attn_Wo"), "bo": g("L1_attn_bo")}

    for stage in ("ffn", "mha", "ffn"):
        if stage == "ffn":
            res = launch("inner", inner_w, [{"xres": xcur[c]} for c in cores])
            x1 = [np.asarray(res.results[c]["xinner"]) for c in cores]
            res = launch("moe", moe_w,
                         [{"xres": xcur[c], "x1": x1[c]} for c in cores])
            xcur = [np.asarray(res.results[c]["y"]) for c in cores]
        else:
            res = launch("mha", mha_w, [{"xres": xcur[c]} for c in cores])
            xcur = [np.asarray(res.results[c]["y"]) for c in cores]

    return np.stack(xcur).astype(np.float32)
